# version 09 (final)
# speedup vs baseline: 1.6339x; 1.0032x over previous
import contextlib
import numpy as np
import concourse.bass as bass
import concourse.mybir as mybir
from concourse.bass_utils import run_bass_kernel_spmd

F32 = mybir.dt.float32
BF16 = mybir.dt.bfloat16
AF = mybir.ActivationFunctionType
ALU = mybir.AluOpType
AX = mybir.AxisListType

NCORES = 8
B, S, D = 32, 2048, 1024
BPC = B // NCORES
T = BPC * S
NT = T // 128
NG = T // 512
MAGIC = 12582912.0

ENGINES = ("sync", "vector", "scalar", "tensor", "gpsimd")


class P:

    def __init__(self, name="prog"):
        self.nc = bass.Bass("TRN2", target_bir_lowering=False)
        self.streams = {k: [] for k in ENGINES}
        self.sems = {}
        self.cnt = {}
        self.trace = {k: [] for k in ENGINES}
        self.guards = {}
        self.prefix = ""
        self.stack = None

    def barrier(self):
        waits = [(s, v) for (s, v) in self.cnt.items() if v > 0]
        for eng in ENGINES:
            self.wait(eng, waits)
        self.guards.clear()

    def sem(self, name):
        if name not in self.sems:
            self.sems[name] = self.nc.alloc_semaphore(name)
            self.cnt[name] = 0
        return self.sems[name]

    def val(self, name):
        self.sem(name)
        return self.cnt[name]

    def op(self, eng, method, *args, wait=(), sem=None, n=1, **kw):
        target = None
        if sem is not None:
            self.sem(sem)
            self.cnt[sem] += n
            target = self.cnt[sem]
        wait = tuple(wait)
        semh = self.sems[sem] if sem is not None else None

        def fn(e, method=method, args=args, kw=kw, wait=wait, semh=semh, n=n):
            for (s, v) in wait:
                e.wait_ge(self.sems[s], v)
            inst = getattr(e, method)(*args, **kw)
            if semh is not None:
                inst.then_inc(semh, n)
        self.streams[eng].append(fn)
        self.trace[eng].append((wait, ((sem, n),) if sem else (), method))
        return target

    def wait(self, eng, wait):
        wait = tuple(wait)

        def fn(e, wait=wait):
            for (s, v) in wait:
                e.wait_ge(self.sems[s], v)
        self.streams[eng].append(fn)
        self.trace[eng].append((wait, (), "wait"))

    def verify(self):
        pos = {k: 0 for k in ENGINES}
        cur = {k: 0 for k in self.cnt}
        progress = True
        while progress:
            progress = False
            for eng in ENGINES:
                while pos[eng] < len(self.trace[eng]):
                    waits, incs, label = self.trace[eng][pos[eng]]
                    if all(cur.get(s, 0) >= v for (s, v) in waits):
                        for (s, n) in incs:
                            cur[s] = cur.get(s, 0) + n
                        pos[eng] += 1
                        progress = True
                    else:
                        break
        stuck = {k: (pos[k], len(self.trace[k])) for k in ENGINES
                 if pos[k] < len(self.trace[k])}
        if stuck:
            msg = []
            for k, (i, n) in stuck.items():
                waits, incs, label = self.trace[k][i]
                blocked = [(s, v, cur.get(s, 0)) for (s, v) in waits
                           if cur.get(s, 0) < v]
                msg.append(f"{k}@{i}/{n} op={label} blocked on {blocked}")
            raise RuntimeError("DEADLOCK:\n" + "\n".join(msg))

    def sbuf(self, name, shape, dtype, **kw):
        name = self.prefix + name
        if self.stack is not None:
            return self.stack.enter_context(
                self.nc.sbuf_tensor(name, shape, dtype, **kw))
        return self.nc.alloc_sbuf_tensor(name, shape, dtype, **kw)

    def psum(self, name, shape, dtype=F32):
        name = self.prefix + name
        if self.stack is not None:
            return self.stack.enter_context(
                self.nc.psum_tensor(name, shape, dtype))
        return self.nc.alloc_psum_tensor(name, shape, dtype)

    def dram(self, name, shape, dtype, kind="Internal"):
        return self.nc.dram_tensor(self.prefix + name, shape, dtype,
                                   kind=kind)

    def finalize(self):
        nc = self.nc
        with nc.Block() as block:
            @block.sync
            def _(e):
                for fn in self.streams["sync"]:
                    fn(e)

            @block.vector
            def _(e):
                for fn in self.streams["vector"]:
                    fn(e)

            @block.scalar
            def _(e):
                for fn in self.streams["scalar"]:
                    fn(e)

            @block.tensor
            def _(e):
                for fn in self.streams["tensor"]:
                    fn(e)

            @block.gpsimd
            def _(e):
                for fn in self.streams["gpsimd"]:
                    fn(e)
        return nc



def build_consts(p, f32=True, bf=True, eps=()):
    c = {}
    ones = p.sbuf("c_ones", [128, 128], F32)
    mv = p.op("gpsimd", "memset", ones[:], 1.0, sem="g")
    if f32:
        idf = p.sbuf("c_idf32", [128, 128], F32)
        p.op("gpsimd", "affine_select", idf[:], ones[:], [[-1, 128]],
             ALU.is_equal, 0.0, base=0, channel_multiplier=1,
             wait=[("g", mv)], sem="g")
        c["idf32"] = idf
    if bf:
        idb = p.sbuf("c_idbf", [128, 128], BF16)
        p.op("gpsimd", "affine_select", idb[:], ones[:], [[-1, 128]],
             ALU.is_equal, 0.0, base=0, channel_multiplier=1,
             wait=[("g", mv)], sem="g")
        c["idbf"] = idb
    for e in eps:
        t = p.sbuf(f"c_eps_{e:.0e}".replace("-", "m"), [128, 1], F32)
        p.op("gpsimd", "memset", t[:], float(e), sem="g")
        c[f"eps{e:.0e}".replace("-", "m")] = t
    c["ones"] = ones
    c["g_done"] = p.val("g")
    return c


def load_w_fm(p, name, dram_ap, kin, mout, out_dtype=BF16):
    nkc = kin // 128
    sb = p.sbuf(name, [128, nkc * mout], out_dtype)
    for kc in range(nkc):
        p.op("gpsimd", "dma_start", sem="gld", n=16,
             out=sb[:, kc * mout:(kc + 1) * mout],
             in_=dram_ap[kc * 128:(kc + 1) * 128, :])
    return sb, p.val("gld")


def load_bias_cols(p, name, dram_ap, nfeat, consts, ps_f32, wait_extra=()):
    nchunk = max(1, nfeat // 128)
    rows = p.sbuf(name + "_rows", [nchunk, 128], F32)
    sb = p.sbuf(name, [128, nchunk], F32)
    p.wait("sync", [("ld", p.val("ld"))])
    lv = p.op("sync", "dma_start", sem="ld", n=16, out=rows[:],
              in_=dram_ap.rearrange("(c p) -> c p", p=128))
    tv = p.op("tensor", "transpose", ps_f32[:, 0:nchunk], rows[:],
              consts["idf32"][0:nchunk, 0:nchunk],
              wait=[("ld", lv), ("g", consts["g_done"])] + list(wait_extra),
              sem="t")
    vv = p.op("vector", "tensor_copy", sb[:], ps_f32[:, 0:nchunk],
              wait=[("t", tv)], sem="v")
    return sb, vv


class LNState:
    def __init__(self, p, tag):
        self.sum4 = p.sbuf(f"{tag}_sum4", [128, 4], F32)
        self.sq4 = p.sbuf(f"{tag}_sq4", [128, 4], F32)
        self.mean4 = p.sbuf(f"{tag}_mean4", [128, 4], F32)
        self.m2 = p.sbuf(f"{tag}_m2", [128, 4], F32)
        self.var4 = p.sbuf(f"{tag}_var4", [128, 4], F32)
        self.sd4 = p.sbuf(f"{tag}_sd4", [128, 4], F32)
        self.rstd4 = p.sbuf(f"{tag}_rstd4", [128, 4], F32)


def ln_stats(p, st, scr, x_tiles, eps_ap, wait_x, gate=(), nd=1024):
    nj = len(x_tiles)
    for j, xt in enumerate(x_tiles):
        p.op("vector", "reduce_sum", st.sum4[:, j:j + 1], xt, axis=AX.X,
             wait=tuple(wait_x[j]) + tuple(gate), sem="v")
        sg = [p.guards["scr"]] if "scr" in p.guards else []
        av = p.op("scalar", "activation", scr[:, 0:nd], xt, AF.Square,
                  wait=tuple(wait_x[j]) + tuple(gate) + tuple(sg), sem="a",
                  accum_out=st.sq4[:, j:j + 1])
        p.guards["scr"] = ("a", av)
    v_sums = p.val("v")
    a_sq = p.val("a")
    p.op("vector", "tensor_scalar_mul", st.mean4[:, 0:nj], st.sum4[:, 0:nj],
         1.0 / nd, wait=[("v", v_sums), ("a", a_sq)], sem="v")
    vm = p.val("v")
    p.op("vector", "tensor_mul", st.m2[:, 0:nj], st.mean4[:, 0:nj],
         st.mean4[:, 0:nj], wait=[("v", vm)], sem="v")
    vm2 = p.val("v")
    p.op("vector", "scalar_tensor_tensor", st.var4[:, 0:nj], st.sq4[:, 0:nj],
         1.0 / nd, st.m2[:, 0:nj], op0=ALU.mult, op1=ALU.subtract,
         wait=[("v", vm2)], sem="v")
    vv = p.val("v")
    p.op("scalar", "activation", st.sd4[:, 0:nj], st.var4[:, 0:nj], AF.Sqrt,
         bias=eps_ap[:], wait=[("v", vv)], sem="a")
    va = p.val("a")
    p.op("vector", "reciprocal", st.rstd4[:, 0:nj], st.sd4[:, 0:nj],
         wait=[("a", va)], sem="v")
    return p.val("v")


def transpose_to_fm(p, ps_tbf, id_bf, src_bf_tiles, dst_view_of, nj,
                    wait_src):
    ev = None
    for j in range(nj):
        tw = list(wait_src[j])
        if ev is not None:
            tw.append(("v", ev))
        for c in range(8):
            p.op("tensor", "transpose",
                 ps_tbf[:, c * 128:(c + 1) * 128],
                 src_bf_tiles[j][:, c * 128:(c + 1) * 128],
                 id_bf[:],
                 wait=tuple(tw) if c == 0 else (),
                 sem="t" if c == 7 else None)
        tv = p.val("t")
        pin = ps_tbf[:].rearrange("p (c m) -> p c m", c=8)
        ev = p.op("vector", "tensor_copy", dst_view_of(j), pin,
                  wait=[("t", tv)], sem="v")
    return ev


def mm_chain(p, psum_ap, lhsT_list, rhs_list, wait_first):
    n = len(lhsT_list)
    for i in range(n):
        p.op("tensor", "matmul", psum_ap, lhsT_list[i], rhs_list[i],
             start=(i == 0), stop=(i == n - 1),
             wait=wait_first if i == 0 else (),
             sem="t" if i == n - 1 else None)
    return p.val("t")



class HBCommon:

    def __init__(self, p, consts, h_cols=0, yfm_cols=8 * 512,
                 banks=None, ep_ap=None):
        self.p = p
        self.consts = consts
        self.xg = p.sbuf("hb_xg", [128, 4 * 1024], F32)
        self.xnb = p.sbuf("hb_xnb", [128, 4 * 1024], BF16)
        self.xnT = p.sbuf("hb_xnT", [128, 8 * 512], BF16)
        self.yfm = (p.sbuf("hb_yfm", [128, yfm_cols], F32)
                    if yfm_cols else None)
        self.ytm = p.sbuf("hb_ytm", [128, 4 * 1024], F32)
        self.h = p.sbuf("hb_h", [128, h_cols], BF16) if h_cols else None
        self.st_in = LNState(p, "hbi")
        self.st_ep = LNState(p, "hbe")
        self.ps_tbf = p.psum("ps_tbf", [128, 1024], BF16)
        self.ps_scr = p.psum("ps_scr", [128, 1024], F32)
        if banks is None:
            psA = p.psum("psA", [128, 512], F32)
            psB = p.psum("psB", [128, 512], F32)
            ps_ep = p.psum("ps_ep", [128, 1024], F32)
            self.ep_ap = ps_ep[:]
            self._banks = {"A": psA[:], "B": psB[:],
                           "ep0": ps_ep[:, 0:512], "ep1": ps_ep[:, 512:1024]}
        else:
            self._banks = banks
            self.ep_ap = ep_ap
        self.free = {"A": None, "B": None, "ep0": None, "ep1": None,
                     "tbf": None}
        self.g_gate = []

    def bank(self, tag):
        return self._banks[tag]

    def fwait(self, *tags):
        return [self.free[t] for t in tags if self.free[t] is not None]

    def group_load(self, g, xin):
        p = self.p
        src = xin[g * 512:(g + 1) * 512, :].rearrange("(j p) d -> p j d", p=128)
        dst = self.xg[:].rearrange("p (j d) -> p j d", j=4)
        ldv = p.op("sync", "dma_start", out=dst, in_=src, sem="ld", n=16,
                   wait=self.g_gate)
        self.xt = [self.xg[:, j * 1024:(j + 1) * 1024] for j in range(4)]
        return ldv

    def sweep(self, g, ldv, eps_ap):
        p = self.p
        c = self.consts
        sv = ln_stats(p, self.st_in, self.ps_scr[:], self.xt, eps_ap,
                      [[("ld", ldv), ("g", c["g_done"])]] * 4)
        for j in range(4):
            p.op("vector", "tensor_scalar",
                 self.xnb[:, j * 1024:(j + 1) * 1024], self.xt[j],
                 self.st_in.mean4[:, j:j + 1], self.st_in.rstd4[:, j:j + 1],
                 op0=ALU.subtract, op1=ALU.mult,
                 wait=[("v", sv)] if j == 0 else (), sem="v")
        nv = p.val("v")
        src_tiles = [self.xnb[:, j * 1024:(j + 1) * 1024] for j in range(4)]

        def dview(j):
            return bass.AP(self.xnT, j * 128,
                           [[8 * 512, 128], [512, 8], [1, 128]])
        ev = transpose_to_fm(
            p, self.ps_tbf, c["idbf"], src_tiles, dview, 4,
            wait_src=[[("v", nv)] + self.fwait("tbf")] * 4)
        self.free["tbf"] = ("v", ev)
        return ev

    def stage(self, dst_sb, src_sb, w_sb, nkc, nmc, func, bias_sb, ready,
              kstride=512, mtot=None):
        p = self.p
        mtot = mtot if mtot is not None else nmc * 128
        for mc in range(nmc):
            tag = "A" if mc % 2 == 0 else "B"
            ps = self.bank(tag)
            w0 = list(ready) + self.fwait(tag)
            tv = mm_chain(
                p, ps,
                [w_sb[:, kc * mtot + mc * 128: kc * mtot + (mc + 1) * 128]
                 for kc in range(nkc)],
                [src_sb[:, kc * kstride:(kc + 1) * kstride]
                 for kc in range(nkc)],
                w0)
            kw = {}
            if bias_sb is not None:
                kw["bias"] = bias_sb[:, mc:mc + 1]
            av = p.op("scalar", "activation",
                      dst_sb[:, mc * 512:(mc + 1) * 512], ps, func,
                      wait=[("t", tv)], sem="a", **kw)
            self.free[tag] = ("a", av)
        return p.val("a")

    def epilogue(self, g, y_ready, xout, eps_ap, extra_add=None, x_wait=()):
        p = self.p
        c = self.consts
        ep_evs = []
        for j in range(4):
            tw = list(y_ready) + self.fwait("ep0", "ep1")
            for cc in range(8):
                p.op("tensor", "transpose",
                     self.ep_ap[:, cc * 128:(cc + 1) * 128],
                     self.yfm[:, cc * 512 + j * 128: cc * 512 + (j + 1) * 128],
                     c["idf32"][:],
                     wait=tw if cc == 0 else (),
                     sem="t" if cc == 7 else None)
            tv = p.val("t")
            ev = p.op("vector", "tensor_copy",
                      self.ytm[:, j * 1024:(j + 1) * 1024],
                      self.ep_ap[:], wait=[("t", tv)], sem="v")
            self.free["ep0"] = self.free["ep1"] = ("v", ev)
            ep_evs.append(ev)

        yt = [self.ytm[:, j * 1024:(j + 1) * 1024] for j in range(4)]
        if extra_add is not None:
            for j in range(4):
                ap, waits = extra_add(j)
                ev = p.op("vector", "tensor_add", yt[j], yt[j], ap,
                          wait=[("v", ep_evs[j])] + list(waits), sem="v")
                ep_evs[j] = ev
        sv2 = ln_stats(p, self.st_ep, self.ps_scr[:], yt, eps_ap,
                       [[("v", ep_evs[j])] for j in range(4)])
        st_vals = []
        for j in range(4):
            p.op("vector", "tensor_scalar", yt[j], yt[j],
                 self.st_ep.mean4[:, j:j + 1], self.st_ep.rstd4[:, j:j + 1],
                 op0=ALU.subtract, op1=ALU.mult,
                 wait=[("v", sv2)] if j == 0 else (), sem="v")
            uv = p.val("v")
            addv = p.op("vector", "tensor_add", self.xt[j], self.xt[j], yt[j],
                        wait=[("v", uv)] + list(x_wait), sem="v")
            stv = p.op("sync", "dma_start",
                       out=xout[g * 512 + j * 128: g * 512 + (j + 1) * 128, :],
                       in_=self.xt[j], wait=[("v", addv)], sem="st", n=16)
            st_vals.append(stv)
        self.g_gate = [("st", st_vals[-1])]
        return st_vals


def finish(p):
    p.wait("sync", [("st", p.val("st"))])
    p.verify()
    return p.finalize()



def build_l2ffn(fused=None):
    if fused is None:
        p = P("l2ffn")
        nc = p.nc
        xin = nc.dram_tensor("xin", [T, D], F32, kind="ExternalInput").ap()
        xout = nc.dram_tensor("xout", [T, D], F32, kind="ExternalOutput").ap()
        W = {k: nc.dram_tensor(k2, sh, F32, kind="ExternalInput").ap()
             for k, k2, sh in (
                 ("L2_ffn_U1", "U1", [1024, 256]),
                 ("L2_ffn_V1", "V1", [256, 4096]),
                 ("L2_ffn_b1", "b1", [4096]),
                 ("L2_ffn_U2", "U2", [4096, 256]),
                 ("L2_ffn_V2", "V2", [256, 1024]),
                 ("L2_ffn_b2", "b2", [1024]))}
        consts = build_consts(p, f32=True, bf=True, eps=(1e-5, 1e-6))
    else:
        p, consts, xin, xout, W = fused
        nc = p.nc
    u1d = W["L2_ffn_U1"]; v1d = W["L2_ffn_V1"]; b1d = W["L2_ffn_b1"]
    u2d = W["L2_ffn_U2"]; v2d = W["L2_ffn_V2"]; b2d = W["L2_ffn_b2"]
    hb = HBCommon(p, consts, h_cols=32 * 512)
    eps5 = consts["eps1em05"]

    u1, _ = load_w_fm(p, "u1", u1d, 1024, 256)
    v1, _ = load_w_fm(p, "v1", v1d, 256, 4096)
    u2, _ = load_w_fm(p, "u2", u2d, 4096, 256)
    v2, _ = load_w_fm(p, "v2", v2d, 256, 1024)
    wv = p.val("gld")
    b1sb, b1v = load_bias_cols(p, "b1sb", b1d, 4096, consts, hb.ps_scr[:])
    b2sb, b2v = load_bias_cols(p, "b2sb", b2d, 1024, consts, hb.ps_scr[:],
                               wait_extra=[("v", b1v)])

    p1sb = p.sbuf("p1sb", [128, 2 * 512], BF16)
    qsb = p.sbuf("qsb", [128, 2 * 512], BF16)
    hb.g_gate = [("v", b2v), ("gld", wv)]

    for g in range(NG):
        ldv = hb.group_load(g, xin)
        xv = hb.sweep(g, ldv, eps5)
        p1a = hb.stage(p1sb, hb.xnT, u1, 8, 2, AF.Copy, None, [("v", xv)],
                       mtot=256)
        ha = hb.stage(hb.h, p1sb, v1, 2, 32, AF.Gelu_apprx_tanh, b1sb,
                      [("a", p1a)], mtot=4096)
        qa = hb.stage(qsb, hb.h, u2, 32, 2, AF.Copy, None, [("a", ha)],
                      mtot=256)
        ya = hb.stage(hb.yfm, qsb, v2, 2, 8, AF.Identity, b2sb, [("a", qa)],
                      mtot=1024)
        hb.epilogue(g, [("a", ya)], xout, eps5)
    if fused is None:
        return finish(p)
    return None



def prep_binarized_T(p, tag, wdram, nrows, ncols, consts, stage_f32, sgn_bf,
                     alpha, ps_tbf, dst_sb=None, dst_sb_width=0,
                     dst_dram=None, stg_dma=None, init_gate=()):
    nrc = nrows // 128
    ndc = ncols // 128
    prev_done = None
    last = None
    for rc in range(nrc):
        w0 = [("g", consts["g_done"])] + list(init_gate)
        if prev_done is not None:
            w0.append(prev_done)
        ldv = p.op("sync", "dma_start", out=stage_f32[:, 0:ncols],
                   in_=wdram[rc * 128:(rc + 1) * 128, :], sem="ld", n=16,
                   wait=w0)
        rv = p.op("vector", "tensor_reduce", alpha[:], stage_f32[:, 0:ncols],
                  axis=AX.X, op=ALU.add, apply_absolute_value=True,
                  wait=[("ld", ldv)], sem="v")
        av = p.op("scalar", "sign", sgn_bf[:, 0:ncols], stage_f32[:, 0:ncols],
                  wait=[("ld", ldv)], sem="a")
        bv = p.op("vector", "tensor_scalar", sgn_bf[:, 0:ncols],
                  sgn_bf[:, 0:ncols], alpha[:], 1.0 / ncols,
                  op0=ALU.mult, op1=ALU.mult,
                  wait=[("v", rv), ("a", av)], sem="v")
        ev = None
        for base in range(0, ndc, 8):
            nb = min(8, ndc - base)
            tw = [("v", bv)]
            if ev is not None:
                tw.append(ev)
            for i in range(nb):
                p.op("tensor", "transpose",
                     ps_tbf[:, i * 128:(i + 1) * 128],
                     sgn_bf[:, (base + i) * 128:(base + i + 1) * 128],
                     consts["idbf"][:],
                     wait=tw if i == 0 else (),
                     sem="t" if i == nb - 1 else None)
            tv = p.val("t")
            pin = bass.AP(ps_tbf, 0, [[1024, 128], [128, nb], [1, 128]])
            if dst_sb is not None:
                dv = bass.AP(dst_sb, base * nrows + rc * 128,
                             [[dst_sb_width, 128], [nrows, nb], [1, 128]])
                vv = p.op("vector", "tensor_copy", dv, pin,
                          wait=[("t", tv)], sem="v")
                ev = ("v", vv)
            else:
                vv = p.op("vector", "tensor_copy", stg_dma[:, 0:nb * 128], pin,
                          wait=[("t", tv)], sem="v")
                dd = bass.AP(dst_dram.tensor, base * 128 * nrows + rc * 128,
                             [[nrows, 128], [128 * nrows, nb], [1, 128]])
                sv = p.op("sync", "dma_start", out=dd,
                          in_=bass.AP(stg_dma, 0,
                                      [[1024, 128], [128, nb], [1, 128]]),
                          wait=[("v", vv)], sem="st", n=16)
                ev = ("st", sv)
            last = ev
        prev_done = last
    return last


def build_l1ffn(fused=None):
    if fused is None:
        p = P("l1ffn")
        nc = p.nc
        xin = nc.dram_tensor("xin", [T, D], F32, kind="ExternalInput").ap()
        xout = nc.dram_tensor("xout", [T, D], F32, kind="ExternalOutput").ap()
        W = {k: nc.dram_tensor(k2, sh, F32, kind="ExternalInput").ap()
             for k, k2, sh in (
                 ("L1_ffn_w1", "w1", [4096, 1024]),
                 ("L1_ffn_b1", "b1", [4096]),
                 ("L1_ffn_w2", "w2", [1024, 4096]),
                 ("L1_ffn_b2", "b2", [1024]))}
        consts = build_consts(p, f32=True, bf=True, eps=(1e-5, 1e-6))
    else:
        p, consts, xin, xout, W = fused
        nc = p.nc
    w1d = W["L1_ffn_w1"]; b1d = W["L1_ffn_b1"]
    w2d = W["L1_ffn_w2"]; b2d = W["L1_ffn_b2"]
    w2T_dram = p.dram("w2T", [32, 128, 1024], BF16).ap()
    hb = HBCommon(p, consts, h_cols=32 * 512)
    eps5 = consts["eps1em05"]

    w1T = p.sbuf("w1T", [128, 8 * 4096], BF16)
    alpha = p.sbuf("alpha", [128, 1], F32)
    stg_dma = p.sbuf("stg_dma", [128, 1024], BF16)
    r_w1 = prep_binarized_T(p, "w1", w1d, 4096, 1024, consts, hb.xg,
                            hb.xnb, alpha, hb.ps_tbf,
                            dst_sb=w1T, dst_sb_width=8 * 4096)
    r_w2 = prep_binarized_T(p, "w2", w2d, 1024, 4096, consts, hb.xg,
                            hb.xnb, alpha, hb.ps_tbf,
                            dst_dram=w2T_dram, stg_dma=stg_dma,
                            init_gate=[r_w1])
    b1sb, b1v = load_bias_cols(p, "b1sb", b1d, 4096, consts, hb.ps_scr[:])
    b2sb, b2v = load_bias_cols(p, "b2sb", b2d, 1024, consts, hb.ps_scr[:],
                               wait_extra=[("v", b1v)])
    hb.g_gate = [r_w1, r_w2, ("v", b2v)]
    hb.free["tbf"] = r_w1

    w2buf = [p.sbuf(f"w2buf{i}", [128, 1024], BF16) for i in range(2)]
    w2v = [None, None]
    w2cons = [None, None]

    for g in range(NG):
        ldv = hb.group_load(g, xin)
        xv = hb.sweep(g, ldv, eps5)
        ha = hb.stage(hb.h, hb.xnT, w1T, 8, 32, AF.Gelu, b1sb, [("v", xv)],
                      mtot=4096)
        for mset in range(2):
            tags = ["A", "B", "ep0", "ep1"]
            bank_free = hb.fwait(*tags)
            for kc in range(32):
                bi = kc % 2
                w0 = [r_w2]
                if w2cons[bi] is not None:
                    w0.append(("t", w2cons[bi]))
                w2v[bi] = p.op("gpsimd", "dma_start", out=w2buf[bi][:, 0:512],
                               in_=bass.AP(w2T_dram.tensor,
                                           kc * 128 * 1024 + mset * 512,
                                           [[1024, 128], [1, 512]]),
                               wait=w0, sem=f"gw{bi}", n=16)
                for i in range(4):
                    mc = mset * 4 + i
                    w1_ = []
                    if i == 0:
                        w1_ = [(f"gw{bi}", w2v[bi]), ("a", ha)]
                        if kc == 0:
                            w1_ += bank_free
                    p.op("tensor", "matmul", hb.bank(tags[i]),
                         w2buf[bi][:, i * 128:(i + 1) * 128],
                         hb.h[:, kc * 512:(kc + 1) * 512],
                         start=(kc == 0), stop=(kc == 31),
                         wait=w1_, sem="t" if i == 3 else None)
                w2cons[bi] = p.val("t")
            tv = p.val("t")
            for i in range(4):
                mc = mset * 4 + i
                av = p.op("scalar", "activation",
                          hb.yfm[:, mc * 512:(mc + 1) * 512],
                          hb.bank(tags[i]), AF.Identity,
                          bias=b2sb[:, mc:mc + 1],
                          wait=[("t", tv)], sem="a")
                hb.free[tags[i]] = ("a", av)
        ya = p.val("a")
        hb.epilogue(g, [("a", ya)], xout, eps5)
    if fused is None:
        return finish(p)
    return None



_prog_cache = {}


def _get_prog(name, builder):
    if name not in _prog_cache:
        _prog_cache[name] = builder()
    return _prog_cache[name]


def _launch(name, builder, x, weights):
    nc = _get_prog(name, builder)
    xs = np.ascontiguousarray(x.reshape(NCORES, T, D))
    in_maps = [{"xin": xs[c], **weights} for c in range(NCORES)]
    res = run_bass_kernel_spmd(nc, in_maps, core_ids=list(range(NCORES)))
    out = np.stack([r["xout"] for r in res.results])
    return out.reshape(B, S, D)


def run_l2ffn(x, U1, V1, b1, U2, V2, b2):
    return _launch("l2ffn", build_l2ffn, x,
                   {"U1": U1, "V1": V1, "b1": b1, "U2": U2, "V2": V2,
                    "b2": b2})


def run_l1ffn(x, w1, b1, w2, b2):
    return _launch("l1ffn", build_l1ffn, x,
                   {"w1": w1, "b1": b1, "w2": w2, "b2": b2})



def mx_quant(p, dst_bf, x_ap, q, wait0, W=1024):
    A = ALU
    nb = W // 32
    xv = x_ap.rearrange("p (b i) -> p b i", i=32)
    s_ = q["s"][:, 0:nb]
    si_ = q["si"][:, 0:nb]
    t1_ = q["qt1"][:, 0:W]
    t2_ = q["qt2"][:, 0:W]
    t3_ = q["qt3"][:, 0:W]
    t4_ = q["qt4"][:, 0:W]
    rv = p.op("vector", "tensor_reduce", s_, xv, axis=AX.X, op=A.max,
              apply_absolute_value=True, wait=wait0, sem="v")
    sv = p.op("vector", "tensor_scalar", s_, s_, 1.0 / 7.5,
              1e-30, op0=A.mult, op1=A.max, wait=[("v", rv)], sem="v")
    sbc = s_.broadcast_to([128, nb, 32])
    rv2 = p.op("vector", "reciprocal", si_, s_,
               wait=[("v", sv)], sem="v")
    sibc = si_.broadcast_to([128, nb, 32])
    yv = p.op("vector", "tensor_tensor",
              t1_.rearrange("p (b i) -> p b i", i=32), xv, sibc,
              op=A.mult, wait=[("v", rv2)], sem="v")
    av = p.op("scalar", "activation", t2_, t1_, AF.Abs,
              wait=[("v", yv)], sem="a")
    v2 = p.op("vector", "tensor_single_scalar", t3_, t2_, 2.0,
              op=A.is_ge, wait=[("a", av)], sem="v")
    v4 = p.op("vector", "tensor_single_scalar", t4_, t2_, 4.0,
              op=A.is_ge, wait=[("v", v2)], sem="v")
    m1 = p.op("vector", "tensor_scalar", t2_, t3_, -4.0, 8.0,
              op0=A.mult, op1=A.add, wait=[("v", v4)], sem="v")
    mv = p.op("vector", "scalar_tensor_tensor", t2_, t4_,
              -2.0, t2_, op0=A.mult, op1=A.add, wait=[("v", m1)],
              sem="v")
    t1 = p.op("vector", "tensor_mul", t1_, t1_, t2_,
              wait=[("v", mv)], sem="v")
    t2v = p.op("vector", "tensor_scalar", t1_, t1_, MAGIC,
               -MAGIC, op0=A.add, op1=A.add, wait=[("v", t1)], sem="v")
    i1 = p.op("vector", "tensor_scalar", t2_, t3_, 0.125,
              0.125, op0=A.mult, op1=A.add, wait=[("v", t2v)], sem="v")
    i2 = p.op("vector", "scalar_tensor_tensor", t2_, t4_,
              0.25, t2_, op0=A.mult, op1=A.add, wait=[("v", i1)],
              sem="v")
    g1 = p.op("vector", "tensor_mul", t1_, t1_, t2_,
              wait=[("v", i2)], sem="v")
    dv = p.op("vector", "tensor_tensor",
              dst_bf.rearrange("p (b i) -> p b i", i=32),
              t1_.rearrange("p (b i) -> p b i", i=32), sbc,
              op=A.mult, wait=[("v", g1)], sem="v")
    return dv


def build_l0ffn(fused=None):
    if fused is None:
        p = P("l0ffn")
        nc = p.nc
        xin = nc.dram_tensor("xin", [T, D], F32, kind="ExternalInput").ap()
        xout = nc.dram_tensor("xout", [T, D], F32, kind="ExternalOutput").ap()
        W = {k: nc.dram_tensor(k2, sh, F32, kind="ExternalInput").ap()
             for k, k2, sh in (
                 ("L0_ffn_weight", "weight", [1024, 1024]),
                 ("L0_ffn_bias", "bias", [1024]))}
        consts = build_consts(p, f32=True, bf=True, eps=(1e-5, 1e-6))
    else:
        p, consts, xin, xout, W = fused
        nc = p.nc
    wd = W["L0_ffn_weight"]; bd = W["L0_ffn_bias"]
    hb = HBCommon(p, consts, h_cols=0)
    eps5 = consts["eps1em05"]

    q = {
        "qt1": p.sbuf("qt1", [128, 2048], F32),
        "qt2": p.sbuf("qt2", [128, 2048], F32),
        "qt3": p.sbuf("qt3", [128, 2048], F32),
        "qt4": p.sbuf("qt4", [128, 2048], F32),
        "s": p.sbuf("qs", [128, 64], F32),
        "si": p.sbuf("qsi", [128, 64], F32),
    }
    xnf = p.sbuf("xnf", [128, 2048], F32)
    wdT = p.sbuf("wdT", [128, 8 * 1024], BF16)

    r_w = None
    for rc in range(8):
        w0 = [("g", consts["g_done"])]
        if r_w is not None:
            w0.append(r_w)
        ldv = p.op("sync", "dma_start", out=hb.xg[:, 0:1024],
                   in_=wd[rc * 128:(rc + 1) * 128, :], sem="ld", n=16,
                   wait=w0)
        qv = mx_quant(p, hb.xnb[:, 0:1024], hb.xg[:, 0:1024], q,
                      [("ld", ldv)])
        tw = [("v", qv)]
        if r_w is not None:
            tw.append(r_w)
        for i in range(8):
            p.op("tensor", "transpose",
                 hb.ps_tbf[:, i * 128:(i + 1) * 128],
                 hb.xnb[:, i * 128:(i + 1) * 128],
                 consts["idbf"][:],
                 wait=tw if i == 0 else (), sem="t" if i == 7 else None)
        tv = p.val("t")
        pin = bass.AP(hb.ps_tbf, 0, [[1024, 128], [128, 8], [1, 128]])
        dv = bass.AP(wdT, rc * 128, [[8 * 1024, 128], [1024, 8], [1, 128]])
        vv = p.op("vector", "tensor_copy", dv, pin, wait=[("t", tv)], sem="v")
        r_w = ("v", vv)
    hb.free["tbf"] = r_w

    p.wait("sync", [("ld", p.val("ld"))])
    bsb, bv = load_bias_cols(p, "bsb", bd, 1024, consts, hb.ps_scr[:])

    xgB = p.sbuf("xgB", [128, 4 * 1024], F32)
    xgs = [hb.xg, xgB]
    xt_of = {}
    ld_of = {}
    st_hist = {}
    init_gate = [r_w, ("v", bv)]

    def do_load(g):
        hb.xg = xgs[g % 2]
        if g == 0:
            gate = list(init_gate)
        elif (g - 1) in st_hist:
            gate = [("st", st_hist[g - 1])]
        else:
            gate = []
        src = xin[g * 512:(g + 1) * 512, :].rearrange("(j p) d -> p j d",
                                                      p=128)
        dst = hb.xg[:].rearrange("p (j d) -> p j d", j=4)
        ld_of[g] = p.op("sync", "dma_start", out=dst, in_=src,
                        sem=f"ld{g % 2}", n=16, wait=gate)
        xt_of[g] = [hb.xg[:, j * 1024:(j + 1) * 1024] for j in range(4)]

    def do_sweep(g):
        hb.xt = xt_of[g]
        sv = ln_stats(p, hb.st_in, hb.ps_scr[:], hb.xt, eps5,
                      [[(f"ld{g % 2}", ld_of[g]),
                        ("g", consts["g_done"])]] * 4)
        qvs = []
        for jj in range(2):
            for j2 in (0, 1):
                j = jj * 2 + j2
                nv = p.op("vector", "tensor_scalar",
                          xnf[:, j2 * 1024:(j2 + 1) * 1024], hb.xt[j],
                          hb.st_in.mean4[:, j:j + 1],
                          hb.st_in.rstd4[:, j:j + 1],
                          op0=ALU.subtract, op1=ALU.mult,
                          wait=[("v", sv)] if j == 0 else (), sem="v")
            qv = mx_quant(p, hb.xnb[:, jj * 2048:(jj + 1) * 2048], xnf[:], q,
                          [("v", nv)], W=2048)
            qvs.extend([qv, qv])
        src_tiles = [hb.xnb[:, j * 1024:(j + 1) * 1024] for j in range(4)]

        def dview(j):
            return bass.AP(hb.xnT, j * 128,
                           [[8 * 512, 128], [512, 8], [1, 128]])
        xv = transpose_to_fm(p, hb.ps_tbf, consts["idbf"], src_tiles, dview,
                             4, wait_src=[[("v", qvs[j])] + hb.fwait("tbf")
                                          for j in range(4)])
        hb.free["tbf"] = ("v", xv)
        return xv

    do_load(0)
    xv = do_sweep(0)
    for g in range(NG):
        if g + 1 < NG:
            do_load(g + 1)
        hb.xt = xt_of[g]
        ya = hb.stage(hb.yfm, hb.xnT, wdT, 8, 8, AF.Identity, bsb,
                      [("v", xv)], mtot=1024)
        if g + 1 < NG:
            xv = do_sweep(g + 1)
        hb.xt = xt_of[g]
        sts = hb.epilogue(g, [("a", ya)], xout, eps5)
        st_hist[g] = sts[-1]
    if fused is None:
        return finish(p)
    return None


def run_l0ffn(x, weight, bias):
    return _launch("l0ffn", build_l0ffn, x, {"weight": weight, "bias": bias})



def bcast_row(p, name, dram_vec, consts, ps2, n=1024, wait_ps=()):
    row = p.sbuf(name + "_row", [1, n], F32)
    dst = p.sbuf(name, [128, n], F32)
    p.wait("sync", [("ld", p.val("ld"))])
    lv = p.op("sync", "dma_start", sem="ld", n=16, out=row[:],
              in_=dram_vec.rearrange("(a n) -> a n", a=1))
    last_v = None
    for i in range(0, n, 512):
        w = min(512, n - i)
        tv = p.op("tensor", "matmul", ps2[:, i % 1024: i % 1024 + w],
                  consts["ones"][0:1, 0:128], row[0:1, i:i + w],
                  start=True, stop=True,
                  wait=[("ld", lv), ("g", consts["g_done"])] + list(wait_ps),
                  sem="t")
        last_v = p.op("vector", "tensor_copy", dst[:, i:i + w],
                      ps2[:, i % 1024: i % 1024 + w], wait=[("t", tv)],
                      sem="v")
    return dst, ("v", last_v)


def build_l2attn(fused=None):
    if fused is None:
        p = P("l2attn")
        nc = p.nc
        xin = nc.dram_tensor("xin", [T, D], F32, kind="ExternalInput").ap()
        xout = nc.dram_tensor("xout", [T, D], F32, kind="ExternalOutput").ap()
        W = {k: nc.dram_tensor(k2, sh, F32, kind="ExternalInput").ap()
             for k, k2, sh in (
                 ("L2_attn_ln_scale", "ln_scale", [1024]),
                 ("L2_attn_ln_bias", "ln_bias", [1024]),
                 ("L2_attn_W_hidden", "W_hidden", [1024, 4096]),
                 ("L2_attn_b_hidden", "b_hidden", [4096]),
                 ("L2_attn_W_qk", "W_qk", [1024, 128]),
                 ("L2_attn_b_qk", "b_qk", [128]),
                 ("L2_attn_os_weight", "os_weight", [2, 128]),
                 ("L2_attn_os_bias", "os_bias", [2, 128]),
                 ("L2_attn_W_out", "W_out", [2048, 1024]),
                 ("L2_attn_b_out", "b_out", [1024]))}
        consts = build_consts(p, f32=True, bf=True, eps=(1e-5, 1e-6))
    else:
        p, consts, xin, xout, W = fused
        nc = p.nc
    gmd = W["L2_attn_ln_scale"]; btd = W["L2_attn_ln_bias"]
    whd = W["L2_attn_W_hidden"]; bhd = W["L2_attn_b_hidden"]
    wqkd = W["L2_attn_W_qk"]; bqkd = W["L2_attn_b_qk"]
    oswd = W["L2_attn_os_weight"]; osbd = W["L2_attn_os_bias"]
    wod = W["L2_attn_W_out"]; bod = W["L2_attn_b_out"]

    vdram = p.dram("vtm", [T, 2048], BF16).ap()
    gdram = p.dram("gfm", [2048, T], BF16).ap()
    qdram = p.dram("qfm", [128, T], BF16).ap()
    kdram = p.dram("kfm", [128, T], BF16).ap()
    hb = HBCommon(p, consts, h_cols=0)
    eps5 = consts["eps1em05"]

    if p.stack is None:
        p.stack = contextlib.ExitStack()
    bo, bov = load_bias_cols(p, "bo", bod, 1024, consts, hb.ps_scr[:])
    wout, _ = load_w_fm(p, "wout", wod, 2048, 1024)
    vacc = p.sbuf("vacc", [128, 4 * 2048], BF16)

    wstack = contextlib.ExitStack()
    saved_stack = p.stack
    p.stack = wstack
    gbc, gv = bcast_row(p, "gbc", gmd, consts, hb.ps_scr[:],
                        wait_ps=[("v", bov)])
    bbc, bv_ = bcast_row(p, "bbc", btd, consts, hb.ps_scr[:], wait_ps=[gv])
    bh, bhv = load_bias_cols(p, "bh", bhd, 4096, consts, hb.ps_scr[:],
                             wait_extra=[bv_])
    bqk, bqkv = load_bias_cols(p, "bqk", bqkd, 128, consts, hb.ps_scr[:],
                               wait_extra=[("v", bhv)])
    osw0, v1 = load_bias_cols(p, "osw0", oswd[0, :], 128, consts,
                              hb.ps_scr[:], wait_extra=[("v", bqkv)])
    osw1, v2 = load_bias_cols(p, "osw1", oswd[1, :], 128, consts,
                              hb.ps_scr[:], wait_extra=[("v", v1)])
    osb0, v3 = load_bias_cols(p, "osb0", osbd[0, :], 128, consts,
                              hb.ps_scr[:], wait_extra=[("v", v2)])
    osb1, v4 = load_bias_cols(p, "osb1", osbd[1, :], 128, consts,
                              hb.ps_scr[:], wait_extra=[("v", v3)])
    wqk, _ = load_w_fm(p, "wqk", wqkd, 1024, 128)
    wv = p.val("gld")

    xnf = p.sbuf("xnf", [128, 1024], F32)
    xnf2 = p.sbuf("xnf2", [128, 1024], F32)
    st_in2 = LNState(p, "hbi2")
    whbuf = [p.sbuf(f"whbuf{i}", [128, 8 * 128], BF16) for i in range(2)]
    vtmp = p.sbuf("vtmp", [128, 512], BF16)
    gstg = p.sbuf("gstg", [128, 512], BF16)
    qks = p.sbuf("qks", [128, 512], F32)
    qkbf = p.sbuf("qkbf", [128, 512], BF16)
    p.stack = saved_stack

    hb.g_gate = [("v", v4), ("gld", wv)]

    whv = [None, None]
    whcons = [None, None]
    for g in range(NG):
        ldv = hb.group_load(g, xin)
        sv = ln_stats(p, hb.st_in, hb.ps_scr[:], hb.xt, eps5,
                      [[("ld", ldv), ("g", consts["g_done"])]] * 4)
        xn_vals = []
        for j in range(4):
            n1 = p.op("vector", "tensor_scalar", xnf[:], hb.xt[j],
                      hb.st_in.mean4[:, j:j + 1], hb.st_in.rstd4[:, j:j + 1],
                      op0=ALU.subtract, op1=ALU.mult,
                      wait=[("v", sv)] if j == 0 else (), sem="v")
            sv2 = ln_stats(p, st_in2, hb.ps_scr[:], [xnf[:]], eps5,
                           [[("v", n1)]])
            n2 = p.op("vector", "tensor_scalar", xnf2[:], xnf[:],
                      st_in2.mean4[:, 0:1], st_in2.rstd4[:, 0:1],
                      op0=ALU.subtract, op1=ALU.mult,
                      wait=[("v", sv2)], sem="v")
            n3 = p.op("vector", "tensor_mul", xnf2[:], xnf2[:], gbc[:],
                      wait=[("v", n2)], sem="v")
            n4 = p.op("vector", "tensor_add",
                      hb.xnb[:, j * 1024:(j + 1) * 1024], xnf2[:], bbc[:],
                      wait=[("v", n3)], sem="v")
            xn_vals.append(n4)
        src_tiles = [hb.xnb[:, j * 1024:(j + 1) * 1024] for j in range(4)]

        def dview(j):
            return bass.AP(hb.xnT, j * 128,
                           [[8 * 512, 128], [512, 8], [1, 128]])
        xv = transpose_to_fm(p, hb.ps_tbf, consts["idbf"], src_tiles, dview,
                             4, wait_src=[[("v", xn_vals[j])] + hb.fwait("tbf")
                                          for j in range(4)])
        hb.free["tbf"] = ("v", xv)

        vacc_evs = []
        for mc in range(32):
            bi = mc % 2
            w0 = []
            if whcons[bi] is not None:
                w0.append(("t", whcons[bi]))
            whv[bi] = p.op("gpsimd", "dma_start", out=whbuf[bi][:],
                           in_=bass.AP(whd.tensor, mc * 128,
                                       [[4096, 128], [128 * 4096, 8],
                                        [1, 128]]),
                           wait=w0, sem=f"wh{bi}", n=16)
            tag = "A" if mc % 2 == 0 else "B"
            ps = hb.bank(tag)
            w1_ = [(f"wh{bi}", whv[bi]), ("v", xv)] + hb.fwait(tag)
            tv = mm_chain(p, ps,
                          [whbuf[bi][:, kc * 128:(kc + 1) * 128]
                           for kc in range(8)],
                          [hb.xnT[:, kc * 512:(kc + 1) * 512]
                           for kc in range(8)], w1_)
            whcons[bi] = tv
            if mc < 16:
                av = p.op("scalar", "activation", vtmp[:], ps, AF.Silu,
                          bias=bh[:, mc:mc + 1], wait=[("t", tv)], sem="a")
                hb.free[tag] = ("a", av)
                tw = [("a", av)] + hb.fwait("tbf")
                for jj in range(4):
                    p.op("tensor", "transpose",
                         hb.ps_tbf[:, jj * 128:(jj + 1) * 128],
                         vtmp[:, jj * 128:(jj + 1) * 128],
                         consts["idbf"][:],
                         wait=tw if jj == 0 else (),
                         sem="t" if jj == 3 else None)
                ttv = p.val("t")
                dv = bass.AP(vacc, mc * 128,
                             [[4 * 2048, 128], [2048, 4], [1, 128]])
                pin = bass.AP(hb.ps_tbf, 0, [[1024, 128], [128, 4], [1, 128]])
                vw = [p.guards["vacc"]] if "vacc" in p.guards else []
                ev = p.op("vector", "tensor_copy", dv, pin,
                          wait=[("t", ttv)] + vw, sem="v")
                hb.free["tbf"] = ("v", ev)
                vacc_evs.append(ev)
            else:
                gw = [p.guards["gstg"]] if "gstg" in p.guards else []
                av = p.op("scalar", "activation", gstg[:], ps, AF.Silu,
                          bias=bh[:, mc:mc + 1], wait=[("t", tv)] + gw,
                          sem="a")
                hb.free[tag] = ("a", av)
                gsv = p.op("sync", "dma_start",
                     out=bass.AP(gdram.tensor, ((mc - 16) * 128) * T + g * 512,
                                 [[T, 128], [1, 512]]),
                     in_=gstg[:], wait=[("a", av)], sem="sg", n=16)
                p.guards["gstg"] = ("sg", gsv)
        for jj in range(4):
            vsv = p.op("sync", "dma_start",
                 out=vdram[g * 512 + jj * 128: g * 512 + (jj + 1) * 128, :],
                 in_=vacc[:, jj * 2048:(jj + 1) * 2048],
                 wait=[("v", vacc_evs[-1])], sem="sv", n=16)
        p.guards["vacc"] = ("sv", vsv)

        tagq = "A"
        w1_ = [("v", xv)] + hb.fwait(tagq)
        tv = mm_chain(p, hb.bank(tagq),
                      [wqk[:, kc * 128:(kc + 1) * 128] for kc in range(8)],
                      [hb.xnT[:, kc * 512:(kc + 1) * 512] for kc in range(8)],
                      w1_)
        av = p.op("scalar", "activation", qks[:], hb.bank(tagq), AF.Silu,
                  bias=bqk[:, 0:1], wait=[("t", tv)], sem="a")
        hb.free[tagq] = ("a", av)
        qkw = [p.guards["qkbf"]] if "qkbf" in p.guards else []
        qv = p.op("vector", "tensor_scalar", qkbf[:], qks[:], osw0[:, 0:1],
                  osb0[:, 0:1], op0=ALU.mult, op1=ALU.add,
                  wait=[("a", av)] + qkw, sem="v")
        sq = p.op("sync", "dma_start",
                  out=bass.AP(qdram.tensor, g * 512, [[T, 128], [1, 512]]),
                  in_=qkbf[:], wait=[("v", qv)], sem="sq", n=16)
        kv = p.op("vector", "tensor_scalar", qkbf[:], qks[:], osw1[:, 0:1],
                  osb1[:, 0:1], op0=ALU.mult, op1=ALU.add,
                  wait=[("sq", sq)], sem="v")
        sk = p.op("sync", "dma_start",
                  out=bass.AP(kdram.tensor, g * 512, [[T, 128], [1, 512]]),
                  in_=qkbf[:], wait=[("v", kv)], sem="sq", n=16)
        p.guards["qkbf"] = ("sq", sk)
        hb.g_gate = [("v", xn_vals[-1])]

    p.barrier()
    wstack.close()
    vq2 = [p.sbuf("vquart0", [128, 16 * 512], BF16),
           p.sbuf("vquart1", [128, 16 * 512], BF16)]
    attnT = vacc
    simtmp = p.sbuf("simtmp", [128, 512], F32)
    ogsb = p.sbuf("ogsb", [128, 16 * 512], BF16)
    gatesb = p.sbuf("gatesb", [128, 4 * 512], BF16)
    qsb = p.sbuf("qsb", [128, 2048], BF16)
    ksb = p.sbuf("ksb", [128, 2048], BF16)
    phaseA_done = [("sv", p.val("sv")), ("sg", p.val("sg")),
                   ("sq", p.val("sq"))]
    vstate = {0: None, 1: None}
    vld = {0: None, 1: None}
    vcons = {0: None, 1: None}
    vrr = [0]
    qk_v = None
    for g in range(NG):
        s, intile = divmod(g, 4)
        ldv = hb.group_load(g, xin)
        if intile == 0:
            p.wait("sync", [("ld", p.val("ld"))])
            qkg = [p.guards["qksb"]] if "qksb" in p.guards else []
            qk_lv = p.op("sync", "dma_start", out=qsb[:],
                         in_=bass.AP(qdram.tensor, s * 2048,
                                     [[T, 128], [1, 2048]]),
                         wait=phaseA_done + qkg, sem="ld", n=16)
            p.wait("sync", [("ld", p.val("ld"))])
            qk_v = p.op("sync", "dma_start", out=ksb[:],
                        in_=bass.AP(kdram.tensor, s * 2048,
                                    [[T, 128], [1, 2048]]),
                        sem="ld", n=16)
        for jc in range(16):
            tag = "A" if jc % 2 == 0 else "B"
            ps = hb.bank(tag)
            w0 = [("ld", qk_v)] + hb.fwait(tag)
            tv = p.op("tensor", "matmul", ps,
                      ksb[:, jc * 128:(jc + 1) * 128],
                      qsb[:, intile * 512:(intile + 1) * 512],
                      start=True, stop=True, wait=w0, sem="t")
            p.guards["qksb"] = ("t", tv)
            smw = [p.guards["simtmp"]] if "simtmp" in p.guards else []
            rv = p.op("vector", "tensor_scalar", simtmp[:], ps,
                      1.0 / 2048, 0.0, op0=ALU.mult, op1=ALU.max,
                      wait=[("t", tv)] + smw, sem="v")
            hb.free[tag] = ("v", rv)
            av = p.op("scalar", "activation",
                      attnT[:, jc * 512:(jc + 1) * 512], simtmp[:],
                      AF.Square, wait=[("v", rv)], sem="a")
            p.guards["simtmp"] = ("a", av)
            hb.free[tag] = ("a", av) if False else hb.free[tag]
        at_a = p.val("a")

        quarts = [0, 1, 2, 3] if intile % 2 == 0 else [3, 2, 1, 0]
        for qf in quarts:
            bi = None
            for b in (0, 1):
                if vstate[b] == (s, qf):
                    bi = b
                    break
            if bi is None:
                bi = vrr[0] % 2
                vrr[0] += 1
                w0 = list(phaseA_done)
                if vcons[bi] is not None:
                    w0.append(("t", vcons[bi]))
                p.wait("sync", [("ld", p.val("ld"))])
                for jc in range(16):
                    p.op("sync", "dma_start",
                         out=vq2[bi][:, jc * 512:(jc + 1) * 512],
                         in_=vdram[s * 2048 + jc * 128:
                                   s * 2048 + (jc + 1) * 128,
                                   qf * 512:(qf + 1) * 512],
                         wait=w0 if jc == 0 else (), sem="ld", n=16)
                vstate[bi] = (s, qf)
                vld[bi] = p.val("ld")
            vl = vld[bi]
            p.wait("sync", [("ld", p.val("ld"))])
            ggw = [p.guards["gatesb"]] if "gatesb" in p.guards else []
            gt_v = p.op("sync", "dma_start",
                        out=gatesb[:].rearrange("p (c m) -> p c m", c=4),
                        in_=bass.AP(gdram.tensor, qf * 4 * 128 * T + g * 512,
                                    [[T, 128], [128 * T, 4], [1, 512]]),
                        wait=ggw, sem="ld", n=16)
            for dv4 in range(4):
                dvc = qf * 4 + dv4
                tag = "A" if dv4 % 2 == 0 else "B"
                ps = hb.bank(tag)
                w0 = [("ld", vl), ("a", at_a)] + hb.fwait(tag)
                tv = mm_chain(p, ps,
                              [vq2[bi][:, jc * 512 + dv4 * 128:
                                       jc * 512 + (dv4 + 1) * 128]
                               for jc in range(16)],
                              [attnT[:, jc * 512:(jc + 1) * 512]
                               for jc in range(16)], w0)
                vcons[bi] = tv
                ev = p.op("vector", "tensor_tensor",
                          ogsb[:, dvc * 512:(dvc + 1) * 512], ps,
                          gatesb[:, dv4 * 512:(dv4 + 1) * 512],
                          op=ALU.mult, wait=[("t", tv), ("ld", gt_v)],
                          sem="v")
                hb.free[tag] = ("v", ev)
                p.guards["gatesb"] = ("v", ev)
        og_v = p.val("v")

        ya = hb.stage(hb.yfm, ogsb, wout, 16, 8, AF.Identity, bo,
                      [("v", og_v)], mtot=1024)
        hb.epilogue(g, [("a", ya)], xout, eps5)
    if fused is None:
        return finish(p)
    return None


def run_l2attn(x, ln_scale, ln_bias, W_hidden, b_hidden, W_qk, b_qk,
               os_weight, os_bias, W_out, b_out):
    return _launch("l2attn", build_l2attn, x,
                   {"ln_scale": ln_scale, "ln_bias": ln_bias,
                    "W_hidden": W_hidden, "b_hidden": b_hidden,
                    "W_qk": W_qk, "b_qk": b_qk, "os_weight": os_weight,
                    "os_bias": os_bias, "W_out": W_out, "b_out": b_out})




def build_hsel(p, consts):
    hsel = p.sbuf("hsel", [128, 2], BF16)
    mv = p.op("gpsimd", "memset", hsel[:], 0.0, sem="g")
    p.op("gpsimd", "affine_select", hsel[:, 0:1], consts["ones"][:, 0:1],
         [[1, 1]], ALU.is_ge, 0.0, base=63, channel_multiplier=-1,
         wait=[("g", mv)], sem="g")
    p.op("gpsimd", "affine_select", hsel[:, 1:2], consts["ones"][:, 0:1],
         [[1, 1]], ALU.is_ge, 0.0, base=-64, channel_multiplier=1,
         sem="g")
    return hsel, ("g", p.val("g"))


def qk_norm_chunk(p, hsel, ps_qn, q2tmp, qnstg, psrc, wait0, dram_ap, tag):
    sg = [p.guards[tag + "q2"]] if (tag + "q2") in p.guards else []
    av = p.op("scalar", "activation", q2tmp[:], psrc, AF.Square,
              wait=wait0 + sg, sem="a")
    p.guards[tag + "q2"] = ("a", av)
    mmw = [("a", av)] + ([p.guards["psqn"]] if "psqn" in p.guards else [])
    tv = p.op("tensor", "matmul", ps_qn[0:2, 0:512], hsel[:], q2tmp[:],
              start=True, stop=True, wait=mmw, sem="t")
    gv = [p.guards["qnstg"]] if "qnstg" in p.guards else []
    ev = p.op("vector", "tensor_copy", qnstg[0:2, :], ps_qn[0:2, 0:512],
              wait=[("t", tv)] + gv, sem="v")
    p.guards["psqn"] = ("v", ev)
    sv = p.op("sync", "dma_start", out=dram_ap, in_=qnstg[0:2, :],
              wait=[("v", ev)], sem="qn", n=16)
    p.guards["qnstg"] = ("qn", sv)
    return sv


def mhat_rows(p, hb, qn2d, kn2d, mdram, s, qnt, knt, kmax2, mnt, extra=None,
              ex_scalar=None, kmax2_ready=None):
    gq = [p.guards["qnt"]] if "qnt" in p.guards else []
    gq += [p.guards["xng"]] if "xng" in p.guards else []
    lv1 = p.op("sync", "dma_start", out=qnt,
               in_=bass.AP(qn2d.tensor, s * 2048, [[T, 16], [1, 2048]]),
               wait=gq, sem="ld", n=16)
    if kmax2_ready is None:
        lv2 = p.op("sync", "dma_start", out=knt,
                   in_=bass.AP(kn2d.tensor, s * 2048, [[T, 16], [1, 2048]]),
                   sem="ld", n=16)
        rv = p.op("vector", "reduce_max", kmax2[:], knt, axis=AX.X,
                  wait=[("ld", lv2)], sem="v")
        kwait = [("v", rv), ("ld", lv2)]
    else:
        kwait = list(kmax2_ready) + [("ld", lv1)]
    t1 = p.op("vector", "tensor_scalar", mnt, qnt, kmax2[:], None,
              op0=ALU.mult, wait=kwait, sem="v")
    t2 = p.op("scalar", "activation", mnt, mnt, AF.Sqrt,
              wait=[("v", t1)], sem="a")
    if extra is not None:
        t3 = p.op("vector", "tensor_scalar", mnt, mnt, ex_scalar,
                  -1.0, op0=ALU.add, op1=ALU.mult, wait=[("a", t2)], sem="v")
    else:
        t3 = p.op("vector", "tensor_scalar_mul", mnt, mnt, -1.0,
                  wait=[("a", t2)], sem="v")
    sv = p.op("sync", "dma_start",
              out=bass.AP(mdram.tensor, s * 16 * 2048, [[2048, 16], [1, 2048]]),
              in_=mnt, wait=[("v", t3)], sem="sm", n=16)
    p.guards["qnt"] = ("sm", sv)
    return sv



def load_w_scaled(p, name, dram_ap, kin, mout, scale, stg, gate=()):
    nkc = kin // 128
    sb = p.sbuf(name, [128, nkc * mout], BF16)
    last = None
    for kc in range(nkc):
        w0 = list(gate)
        if last is not None:
            w0.append(last)
        p.wait("sync", [("ld", p.val("ld"))])
        lv = p.op("sync", "dma_start", sem="ld", n=16, out=stg[:, 0:mout],
                  in_=dram_ap[kc * 128:(kc + 1) * 128, :], wait=w0)
        vv = p.op("vector", "tensor_scalar_mul",
                  sb[:, kc * mout:(kc + 1) * mout], stg[:, 0:mout],
                  float(scale), wait=[("ld", lv)], sem="v")
        last = ("v", vv)
    return sb, last


def build_l0attn(fused=None):
    if fused is None:
        p = P("l0attn")
        nc = p.nc
        xin = nc.dram_tensor("xin", [T, D], F32, kind="ExternalInput").ap()
        xout = nc.dram_tensor("xout", [T, D], F32, kind="ExternalOutput").ap()
        W = {k: nc.dram_tensor(k2, sh, F32, kind="ExternalInput").ap()
             for k, k2, sh in (
                 ("L0_attn_ln_scale", "ln_scale", [1024]),
                 ("L0_attn_ln_bias", "ln_bias", [1024]),
                 ("L0_attn_wq", "wq", [1024, 1024]),
                 ("L0_attn_wk", "wk", [1024, 1024]),
                 ("L0_attn_wv", "wv", [1024, 1024]))}
        consts = build_consts(p, f32=True, bf=True, eps=(1e-5, 1e-6))
    else:
        p, consts, xin, xout, W = fused
        nc = p.nc
    gmd = W["L0_attn_ln_scale"]; btd = W["L0_attn_ln_bias"]
    wqd = W["L0_attn_wq"]; wkd = W["L0_attn_wk"]; wvd = W["L0_attn_wv"]

    qdram = p.dram("qfm", [1024, T], BF16).ap()
    kdram = p.dram("kfm", [1024, T], BF16).ap()
    vdram = p.dram("vtm", [T, 1024], BF16).ap()
    xndram = p.dram("xn32", [T, 1024], F32).ap()
    mdram = p.dram("mdr", [BPC, 16, 2048], F32).ap()
    qn2d = p.dram("qn2d", [2, 8, T], F32).ap()
    kn2d = p.dram("kn2d", [2, 8, T], F32).ap()
    eps5 = consts["eps1em05"]
    eps6 = consts["eps1em06"]

    ps_s1 = p.psum("ps_s1", [128, 2048], F32)
    banks = {"A": ps_s1[:, 0:512], "B": ps_s1[:, 512:1024],
             "ep0": ps_s1[:, 1024:1536], "ep1": ps_s1[:, 1536:2048]}
    hb = HBCommon(p, consts, h_cols=0, yfm_cols=0, banks=banks,
                  ep_ap=ps_s1[:, 1024:2048])

    kmax2 = p.sbuf("kmax2", [16, 1], F32)

    if p.stack is None:
        p.stack = contextlib.ExitStack()
    wstack = contextlib.ExitStack()
    saved_stack = p.stack
    p.stack = wstack

    gbc, gv = bcast_row(p, "gbc", gmd, consts, hb.ps_scr[:])
    bbc, bv_ = bcast_row(p, "bbc", btd, consts, hb.ps_scr[:], wait_ps=[gv])
    hsel, hselv = build_hsel(p, consts)
    ps_qn = p.psum("ps_qn", [128, 512], F32)
    q2tmp = p.sbuf("q2tmp", [128, 512], BF16)
    qnstg = p.sbuf("qnstg", [2, 512], F32)

    xnf = p.sbuf("xnf", [128, 1024], F32)
    xnf2 = p.sbuf("xnf2", [128, 1024], F32)
    xnf3 = p.sbuf("xnf3", [128, 1024], F32)
    st_in2 = LNState(p, "hbi2")
    wstg = xnf
    wq, wqv = load_w_scaled(p, "wq_sb", wqd, 1024, 1024, 1.0 / 32.0, wstg,
                            gate=[bv_])
    wk, _ = load_w_fm(p, "wk_sb", wkd, 1024, 1024)
    wv_, _ = load_w_fm(p, "wv_sb", wvd, 1024, 1024)
    wldv = p.val("gld")

    pstg2 = [p.sbuf("pstg0", [128, 512], BF16),
             p.sbuf("pstg1", [128, 512], BF16)]
    vtmp = p.sbuf("vtmp", [128, 512], BF16)
    vacc = p.sbuf("vacc_a", [128, 4 * 1024], BF16)

    p.stack = saved_stack

    hb.g_gate = [bv_, wqv, ("gld", wldv)]

    for g in range(NG):
        ldv = hb.group_load(g, xin)
        sv = ln_stats(p, hb.st_in, hb.ps_scr[:], hb.xt, eps5,
                      [[("ld", ldv), ("g", consts["g_done"])]] * 4)
        xn_vals = []
        for j in range(4):
            n1 = p.op("vector", "tensor_scalar", xnf[:], hb.xt[j],
                      hb.st_in.mean4[:, j:j + 1], hb.st_in.rstd4[:, j:j + 1],
                      op0=ALU.subtract, op1=ALU.mult,
                      wait=[("v", sv)] if j == 0 else (), sem="v")
            sv2 = ln_stats(p, st_in2, hb.ps_scr[:], [xnf[:]], eps6,
                           [[("v", n1)]])
            n2 = p.op("vector", "tensor_scalar", xnf2[:], xnf[:],
                      st_in2.mean4[:, 0:1], st_in2.rstd4[:, 0:1],
                      op0=ALU.subtract, op1=ALU.mult,
                      wait=[("v", sv2)], sem="v")
            n3 = p.op("vector", "tensor_mul", xnf2[:], xnf2[:], gbc[:],
                      wait=[("v", n2)], sem="v")
            xw = [p.guards["xnf3"]] if "xnf3" in p.guards else []
            n4 = p.op("vector", "tensor_add", xnf3[:], xnf2[:], bbc[:],
                      wait=[("v", n3)] + xw, sem="v")
            sx = p.op("sync", "dma_start",
                      out=xndram[g * 512 + j * 128: g * 512 + (j + 1) * 128, :],
                      in_=xnf3[:], wait=[("v", n4)], sem="sx", n=16)
            p.guards["xnf3"] = ("sx", sx)
            n5 = p.op("vector", "tensor_copy",
                      hb.xnb[:, j * 1024:(j + 1) * 1024], xnf3[:],
                      wait=[("v", n4)], sem="v")
            xn_vals.append(n5)
        src_tiles = [hb.xnb[:, j * 1024:(j + 1) * 1024] for j in range(4)]

        def dview(j):
            return bass.AP(hb.xnT, j * 128,
                           [[8 * 512, 128], [512, 8], [1, 128]])
        xv = transpose_to_fm(p, hb.ps_tbf, consts["idbf"], src_tiles, dview,
                             4, wait_src=[[("v", xn_vals[j])] + hb.fwait("tbf")
                                          for j in range(4)])
        hb.free["tbf"] = ("v", xv)

        for which, wsb, sem_s, dr, nd in (("q", wq, "qs", qdram, qn2d),
                                          ("k", wk, "ks", kdram, kn2d)):
            for mc in range(8):
                tag = "A" if mc % 2 == 0 else "B"
                ps = hb.bank(tag)
                tv = mm_chain(p, ps,
                              [wsb[:, kc * 1024 + mc * 128:
                                   kc * 1024 + (mc + 1) * 128]
                               for kc in range(8)],
                              [hb.xnT[:, kc * 512:(kc + 1) * 512]
                               for kc in range(8)],
                              [("v", xv)] + hb.fwait(tag))
                pi = mc % 2
                pstg = pstg2[pi]
                pw = ([p.guards[f"pstg{pi}"]]
                      if f"pstg{pi}" in p.guards else [])
                av = p.op("scalar", "activation", pstg[:], ps, AF.Copy,
                          wait=[("t", tv)] + pw, sem="a")
                psv = p.op("sync", "dma_start",
                           out=bass.AP(dr.tensor, (mc * 128) * T + g * 512,
                                       [[T, 128], [1, 512]]),
                           in_=pstg[:], wait=[("a", av)], sem=sem_s, n=16)
                p.guards[f"pstg{pi}"] = (sem_s, psv)
                qk_norm_chunk(p, hsel, ps_qn, q2tmp, qnstg, pstg[:],
                              [("a", av), hselv],
                              bass.AP(nd.tensor, mc * T + g * 512,
                                      [[8 * T, 2], [1, 512]]), which)
                hb.free[tag] = ("a", av)
        vacc_ev = None
        for mc in range(8):
            tag = "A" if mc % 2 == 0 else "B"
            ps = hb.bank(tag)
            tv = mm_chain(p, ps,
                          [wv_[:, kc * 1024 + mc * 128:
                               kc * 1024 + (mc + 1) * 128]
                           for kc in range(8)],
                          [hb.xnT[:, kc * 512:(kc + 1) * 512]
                           for kc in range(8)],
                          [("v", xv)] + hb.fwait(tag))
            vw = [p.guards["vtmp"]] if "vtmp" in p.guards else []
            av = p.op("scalar", "activation", vtmp[:], ps, AF.Copy,
                      wait=[("t", tv)] + vw, sem="a")
            hb.free[tag] = ("a", av)
            tw = [("a", av)] + hb.fwait("tbf")
            for jj in range(4):
                p.op("tensor", "transpose",
                     hb.ps_tbf[:, jj * 128:(jj + 1) * 128],
                     vtmp[:, jj * 128:(jj + 1) * 128],
                     consts["idbf"][:],
                     wait=tw if jj == 0 else (),
                     sem="t" if jj == 3 else None)
            ttv = p.val("t")
            p.guards["vtmp"] = ("t", ttv)
            dv = bass.AP(vacc, mc * 128, [[4 * 1024, 128], [1024, 4], [1, 128]])
            pin = bass.AP(hb.ps_tbf, 0, [[1024, 128], [128, 4], [1, 128]])
            vw2 = [p.guards["vacc"]] if "vacc" in p.guards else []
            vacc_ev = p.op("vector", "tensor_copy", dv, pin,
                           wait=[("t", ttv)] + vw2, sem="v")
            hb.free["tbf"] = ("v", vacc_ev)
        for jj in range(4):
            vsv = p.op("sync", "dma_start",
                       out=vdram[g * 512 + jj * 128: g * 512 + (jj + 1) * 128, :],
                       in_=vacc[:, jj * 1024:(jj + 1) * 1024],
                       wait=[("v", vacc_ev)], sem="sv", n=16)
        p.guards["vacc"] = ("sv", vsv)
        hb.g_gate = [("v", xn_vals[-1])]

    p.barrier()
    wstack.close()

    khat2 = [p.sbuf("khat0", [65, 2048], BF16),
             p.sbuf("khat1", [65, 2048], BF16)]
    qhat2 = [p.sbuf("qhat0", [65, 2048], BF16),
             p.sbuf("qhat1", [65, 2048], BF16)]
    v12 = [p.sbuf("v1_0", [128, 16 * 65], BF16),
           p.sbuf("v1_1", [128, 16 * 65], BF16)]
    PT = p.sbuf("PT", [128, 16 * 512], BF16)
    lr2 = [p.sbuf("lr0", [65, 512], F32), p.sbuf("lr1", [65, 512], F32)]
    yst2 = [p.sbuf("yst0", [64, 512], BF16), p.sbuf("yst1", [64, 512], BF16)]
    invl4 = p.sbuf("invl4", [128, 4], F32)
    ybf = p.sbuf("ybf", [128, 4 * 4096], BF16)
    ps_av = p.psum("ps_av", [128, 512], F32)
    avbanks = [ps_av[0:65, 0:512], ps_s1[0:65, 1536:2048]]
    avkeys = ("avA", "ep1")
    hb.free["avA"] = None
    itc = [0]
    pend_tail = [None]
    xng = p.sbuf("xng", [128, 4 * 1024], F32)
    qnt = bass.AP(xng, 0, [[4096, 16], [1, 2048]])
    knt = bass.AP(xng, 2048, [[4096, 16], [1, 2048]])
    mnt = bass.AP(xng, 32 * 4096, [[4096, 16], [1, 2048]])

    p.op("gpsimd", "memset", khat2[0][64:65, :], 1.0, sem="g")
    p.op("gpsimd", "memset", khat2[1][64:65, :], 1.0, sem="g")
    p.op("gpsimd", "memset", v12[0][:], 1.0, sem="g")
    p.op("gpsimd", "memset", v12[1][:], 1.0, sem="g")
    gB_done = p.val("g")
    p.guards["pb0"] = ("g", gB_done)
    p.guards["pb1"] = ("g", gB_done)

    ldsem = ("gld", "gl2")

    def emit_loads(par, s_, h, m_done):
        sm = ldsem[par]
        key = "pb%d" % par
        kg = [p.guards[key]] if key in p.guards else []
        p.op("gpsimd", "dma_start", out=khat2[par][0:64, :],
             in_=bass.AP(kdram.tensor, (h * 64) * T + s_ * 2048,
                         [[T, 64], [1, 2048]]),
             wait=kg, sem=sm, n=16)
        p.op("gpsimd", "dma_start", out=qhat2[par][0:64, :],
             in_=bass.AP(qdram.tensor, (h * 64) * T + s_ * 2048,
                         [[T, 64], [1, 2048]]),
             sem=sm, n=16)
        r_ = (h % 2) * 8 + h // 2
        p.op("gpsimd", "dma_start", out=qhat2[par][64:65, :],
             in_=bass.AP(mdram.tensor, (s_ * 16 + r_) * 2048,
                         [[2048, 1], [1, 2048]]),
             wait=[("sm", m_done)], sem=sm, n=16)
        vlv = None
        for jc in range(16):
            vlv = p.op("gpsimd", "dma_start",
                       out=v12[par][:, jc * 65: jc * 65 + 64],
                       in_=vdram[s_ * 2048 + jc * 128:
                                 s_ * 2048 + (jc + 1) * 128,
                                 h * 64:(h + 1) * 64],
                       sem=sm, n=16)
        return (sm, vlv)

    for s in range(BPC):
        mhat_rows(p, hb, qn2d, kn2d, mdram, s, qnt, knt, kmax2, mnt)
        m_done = p.val("sm")

        pend = {0: emit_loads(0, s, 0, m_done)}
        for h in range(16):
            par = h % 2
            if h + 1 < 16:
                pend[1 - par] = emit_loads(1 - par, s, h + 1, m_done)
            khat, qhat, v1 = khat2[par], qhat2[par], v12[par]
            smp, vlv = pend[par]
            first = [("g", gB_done)] if (s == 0 and h == 0) else []
            for ib in range(4):
                bk = itc[0] % 2
                avps = avbanks[bk]
                avkey = avkeys[bk]
                ptg = [p.guards["PT"]] if "PT" in p.guards else []
                avs = []
                for jc in range(16):
                    tag = "A" if jc % 2 == 0 else "B"
                    ps = hb.bank(tag)
                    tv = p.op("tensor", "matmul", ps,
                              khat[:, jc * 128:(jc + 1) * 128],
                              qhat[:, ib * 512:(ib + 1) * 512],
                              start=True, stop=True,
                              wait=[(smp, vlv)] + first + hb.fwait(tag) +
                              (ptg if jc == 0 else []),
                              sem="t")
                    av = p.op("scalar", "activation",
                              PT[:, jc * 512:(jc + 1) * 512], ps, AF.Exp,
                              wait=[("t", tv)], sem="a")
                    avs.append(av)
                    hb.free[tag] = ("a", av)
                    if jc >= 2:
                        j2 = jc - 2
                        p.op("tensor", "matmul", avps,
                             v1[:, j2 * 65:(j2 + 1) * 65],
                             PT[:, j2 * 512:(j2 + 1) * 512],
                             start=(j2 == 0), stop=False,
                             wait=[("a", avs[j2])] +
                             (hb.fwait(avkey) if j2 == 0 else []))
                for j2 in (14, 15):
                    p.op("tensor", "matmul", avps,
                         v1[:, j2 * 65:(j2 + 1) * 65],
                         PT[:, j2 * 512:(j2 + 1) * 512],
                         start=False, stop=(j2 == 15),
                         wait=[("a", avs[j2])],
                         sem="t" if j2 == 15 else None)
                tv = p.val("t")
                p.guards["PT"] = ("t", tv)
                tp_ = itc[0] % 2
                lw = ([p.guards["lrp%d" % tp_]]
                      if ("lrp%d" % tp_) in p.guards else [])
                lev = p.op("vector", "tensor_copy", lr2[tp_][64:65, :],
                           avps[64:65, :], wait=[("t", tv)] + lw, sem="v")
                yw = ([p.guards["ystp%d" % tp_]]
                      if ("ystp%d" % tp_) in p.guards else [])
                ysv = p.op("vector", "tensor_copy", yst2[tp_][:],
                           avps[0:64, :], wait=[("t", tv)] + yw, sem="v")
                hb.free[avkey] = ("v", ysv)
                if pend_tail[0] is not None:
                    pend_tail[0]()

                def mk_tail(tp_=tp_, lev=lev, ysv=ysv, ib=ib, h=h):
                    def tail():
                        sg = [p.guards["scr"]] if "scr" in p.guards else []
                        for j in range(4):
                            p.op("tensor", "transpose",
                                 hb.ps_scr[:, 16 + j:16 + j + 1],
                                 lr2[tp_][64:65, j * 128:(j + 1) * 128],
                                 consts["idf32"][64:65, 64:65],
                                 wait=[("v", lev)] + sg if j == 0 else (),
                                 sem="t" if j == 3 else None)
                        ltv = p.val("t")
                        p.guards["lrp%d" % tp_] = ("t", ltv)
                        ig = ([p.guards["invg"]]
                              if "invg" in p.guards else [])
                        inv_v = p.op("vector", "reciprocal", invl4[:],
                                     bass.AP(hb.ps_scr, 16,
                                             [[1024, 128], [1, 4]]),
                                     wait=[("t", ltv)] + ig, sem="v")
                        p.guards["scr"] = ("v", inv_v)
                        tw = [("v", ysv)] + hb.fwait("tbf")
                        for j in range(4):
                            p.op("tensor", "transpose",
                                 hb.ps_tbf[:, j * 64:(j + 1) * 64],
                                 yst2[tp_][:, j * 128:(j + 1) * 128],
                                 consts["idbf"][0:64, 0:64],
                                 wait=tw if j == 0 else (),
                                 sem="t" if j == 3 else None)
                        ytv = p.val("t")
                        p.guards["ystp%d" % tp_] = ("t", ytv)
                        for j in range(4):
                            p.op("vector", "tensor_scalar",
                                 ybf[:, ib * 4096 + j * 1024 + h * 64:
                                     ib * 4096 + j * 1024 + (h + 1) * 64],
                                 hb.ps_tbf[:, j * 64:(j + 1) * 64],
                                 invl4[:, j:j + 1], None, op0=ALU.mult,
                                 wait=([("t", ytv), ("v", inv_v)]
                                       if j == 0 else ()),
                                 sem="v" if j == 3 else None)
                        hb.free["tbf"] = ("v", p.val("v"))
                        p.guards["invg"] = ("v", p.val("v"))
                    return tail
                pend_tail[0] = mk_tail()
                itc[0] += 1
            p.guards["pb%d" % par] = ("t", p.val("t"))
        if pend_tail[0] is not None:
            pend_tail[0]()
            pend_tail[0] = None
        heads_v = p.val("v")

        for ib in range(4):
            g = s * 4 + ib
            ldv = hb.group_load(g, xin)
            xw = [p.guards["xng"]] if "xng" in p.guards else []
            xgl = p.op("sync", "dma_start",
                       out=xng[:].rearrange("p (j d) -> p j d", j=4),
                       in_=xndram[g * 512:(g + 1) * 512, :].rearrange(
                           "(j p) d -> p j d", p=128),
                       wait=xw + [("sm", m_done)], sem="ld", n=16)
            ep_evs = []
            for j in range(4):
                cg = [("v", heads_v)] if j == 0 else []
                cg += [p.guards["ytm"]] if "ytm" in p.guards and j == 0 else []
                cv = p.op("vector", "tensor_copy",
                          hb.ytm[:, j * 1024:(j + 1) * 1024],
                          ybf[:, ib * 4096 + j * 1024:
                              ib * 4096 + (j + 1) * 1024],
                          wait=cg, sem="v")
                av2 = p.op("vector", "tensor_add",
                           hb.ytm[:, j * 1024:(j + 1) * 1024],
                           hb.ytm[:, j * 1024:(j + 1) * 1024],
                           xng[:, j * 1024:(j + 1) * 1024],
                           wait=[("v", cv), ("ld", xgl)], sem="v")
                ep_evs.append(av2)
            p.guards["xng"] = ("v", ep_evs[-1])
            yt = [hb.ytm[:, j * 1024:(j + 1) * 1024] for j in range(4)]
            sv2 = ln_stats(p, hb.st_ep, hb.ps_scr[:], yt, eps5,
                           [[("v", ep_evs[j])] for j in range(4)])
            st_vals = []
            for j in range(4):
                p.op("vector", "tensor_scalar", yt[j], yt[j],
                     hb.st_ep.mean4[:, j:j + 1], hb.st_ep.rstd4[:, j:j + 1],
                     op0=ALU.subtract, op1=ALU.mult,
                     wait=[("v", sv2)] if j == 0 else (), sem="v")
                uv = p.val("v")
                addv = p.op("vector", "tensor_add", hb.xt[j], hb.xt[j], yt[j],
                            wait=[("v", uv), ("ld", xgl)], sem="v")
                stv = p.op("sync", "dma_start",
                           out=xout[g * 512 + j * 128:
                                    g * 512 + (j + 1) * 128, :],
                           in_=hb.xt[j], wait=[("v", addv)], sem="st", n=16)
                st_vals.append(stv)
            hb.g_gate = [("st", st_vals[-1])]
            p.guards["ytm"] = ("a", p.val("a"))
    if fused is None:
        return finish(p)
    return None


def run_l0attn(x, ln_scale, ln_bias, wq, wk, wv):
    return _launch("l0attn", build_l0attn, x,
                   {"ln_scale": ln_scale, "ln_bias": ln_bias,
                    "wq": wq, "wk": wk, "wv": wv})



def prep_transpose_w(p, tag, wdram, nrows, ncols, consts, stage_f32, sgn_bf,
                     ps_tbf, dst_sb, dst_sb_width, scale=1.0, init_gate=()):
    nrc = nrows // 128
    ndc = ncols // 128
    prev_done = None
    last = None
    for rc in range(nrc):
        w0 = [("g", consts["g_done"])] + list(init_gate)
        if prev_done is not None:
            w0.append(prev_done)
        p.wait("sync", [("ld", p.val("ld"))])
        ldv = p.op("sync", "dma_start", out=stage_f32[:, 0:ncols],
                   in_=wdram[rc * 128:(rc + 1) * 128, :], sem="ld", n=16,
                   wait=w0)
        bv = p.op("vector", "tensor_scalar_mul", sgn_bf[:, 0:ncols],
                  stage_f32[:, 0:ncols], float(scale),
                  wait=[("ld", ldv)], sem="v")
        ev = None
        for base in range(0, ndc, 8):
            nb = min(8, ndc - base)
            tw = [("v", bv)]
            if ev is not None:
                tw.append(ev)
            for i in range(nb):
                p.op("tensor", "transpose",
                     ps_tbf[:, i * 128:(i + 1) * 128],
                     sgn_bf[:, (base + i) * 128:(base + i + 1) * 128],
                     consts["idbf"][:],
                     wait=tw if i == 0 else (),
                     sem="t" if i == nb - 1 else None)
            tv = p.val("t")
            pin = bass.AP(ps_tbf, 0, [[1024, 128], [128, nb], [1, 128]])
            dv = bass.AP(dst_sb, base * nrows + rc * 128,
                         [[dst_sb_width, 128], [nrows, nb], [1, 128]])
            vv = p.op("vector", "tensor_copy", dv, pin,
                      wait=[("t", tv)], sem="v")
            ev = ("v", vv)
            last = ev
        prev_done = last
    return last


def build_l3attn(fused=None):
    if fused is None:
        p = P("l3attn")
        nc = p.nc
        xin = nc.dram_tensor("xin", [T, D], F32, kind="ExternalInput").ap()
        xout = nc.dram_tensor("xout", [T, D], F32, kind="ExternalOutput").ap()
        W = {k: nc.dram_tensor(k2, sh, F32, kind="ExternalInput").ap()
             for k, k2, sh in (
                 ("L3_attn_wq", "wq", [1024, 1024]),
                 ("L3_attn_wk", "wk", [256, 1024]),
                 ("L3_attn_wv", "wv", [256, 1024]),
                 ("L3_attn_wo", "wo", [1024, 1024]))}
        consts = build_consts(p, f32=True, bf=True, eps=(1e-5, 1e-6))
    else:
        p, consts, xin, xout, W = fused
        nc = p.nc
    wqd = W["L3_attn_wq"]; wkd = W["L3_attn_wk"]
    wvd = W["L3_attn_wv"]; wod = W["L3_attn_wo"]

    qdram = p.dram("qfm", [1024, T], BF16).ap()
    kdram = p.dram("kfm", [256, T], BF16).ap()
    vdram = p.dram("vtm", [T, 256], BF16).ap()
    mdram = p.dram("mdr", [BPC, 16, 2048], F32).ap()
    qn2d = p.dram("qn2d", [2, 8, T], F32).ap()
    kn2d = p.dram("kn2d", [2, 2, T], F32).ap()
    kx4d = p.dram("kx4d", [4], F32).ap()
    eps5 = consts["eps1em05"]

    ps_s1 = p.psum("ps_s1", [128, 2048], F32)
    banks = {"A": ps_s1[:, 0:512], "B": ps_s1[:, 512:1024],
             "ep0": ps_s1[:, 1024:1536], "ep1": ps_s1[:, 1536:2048]}
    hb = HBCommon(p, consts, h_cols=0, banks=banks,
                  ep_ap=ps_s1[:, 1024:2048])

    cmaskT = p.sbuf("cmaskT", [128, 4 * 512], F32)
    kmax2 = p.sbuf("kmax2", [16, 1], F32)
    kmax4 = p.sbuf("kmax4", [4, 1], F32)
    knt4 = p.sbuf("knt4", [4, 2048], F32)
    PT = p.sbuf("PT", [128, 16 * 512], BF16)
    woT = p.sbuf("woT", [128, 8 * 1024], BF16)
    yfm = p.sbuf("yfm_l3", [128, 8 * 512], F32)
    vacc = PT[:, 0:2048]
    qnt = bass.AP(hb.ytm, 0, [[4096, 16], [1, 2048]])
    knt = None
    mnt = bass.AP(hb.ytm, 32 * 4096, [[4096, 16], [1, 2048]])

    if p.stack is None:
        p.stack = contextlib.ExitStack()
    wstack = contextlib.ExitStack()
    saved_stack = p.stack
    p.stack = wstack

    zt = p.sbuf("zt", [128, 512], F32)
    zv = p.op("gpsimd", "memset", zt[:], 0.0, sem="g")
    for off in range(4):
        p.op("gpsimd", "affine_select", cmaskT[:, off * 512:(off + 1) * 512],
             zt[:], [[1, 512]], ALU.is_ge, -1e30, base=-off * 128,
             channel_multiplier=-1, wait=[("g", zv)], sem="g")
    g_done2 = p.val("g")

    wstg = hb.xg
    wbfs = hb.xnb
    wqT = p.sbuf("wqT", [128, 8 * 1024], BF16)
    wkT = p.sbuf("wkT", [128, 8 * 256], BF16)
    wvT = p.sbuf("wvT", [128, 8 * 256], BF16)
    r1 = prep_transpose_w(p, "wq", wqd, 1024, 1024, consts, wstg, wbfs,
                          hb.ps_tbf, wqT, 8 * 1024, scale=1.0 / 8.0)
    r2 = prep_transpose_w(p, "wk", wkd, 256, 1024, consts, wstg, wbfs,
                          hb.ps_tbf, wkT, 8 * 256, init_gate=[r1])
    r3 = prep_transpose_w(p, "wv", wvd, 256, 1024, consts, wstg, wbfs,
                          hb.ps_tbf, wvT, 8 * 256, init_gate=[r2])
    r4 = prep_transpose_w(p, "wo", wod, 1024, 1024, consts, wstg, wbfs,
                          hb.ps_tbf, woT, 8 * 1024, init_gate=[r3])
    hb.free["tbf"] = r4

    pstg2 = [p.sbuf("pstg0", [128, 512], BF16),
             p.sbuf("pstg1", [128, 512], BF16)]
    vtmp = p.sbuf("vtmp", [128, 512], BF16)
    hsel, hselv = build_hsel(p, consts)
    ps_qn = p.psum("ps_qn", [128, 512], F32)
    q2tmp = p.sbuf("q2tmp", [128, 512], BF16)
    qnstg = p.sbuf("qnstg", [2, 512], F32)
    p.stack = saved_stack

    hb.g_gate = [r4, ("g", g_done2)]

    for g in range(NG):
        ldv = hb.group_load(g, xin)
        xv = hb.sweep(g, ldv, eps5)
        for mc in range(8):
            tag = "A" if mc % 2 == 0 else "B"
            ps = hb.bank(tag)
            tv = mm_chain(p, ps,
                          [wqT[:, kc * 1024 + mc * 128:
                               kc * 1024 + (mc + 1) * 128]
                           for kc in range(8)],
                          [hb.xnT[:, kc * 512:(kc + 1) * 512]
                           for kc in range(8)],
                          [("v", xv)] + hb.fwait(tag))
            pi = mc % 2
            pstg = pstg2[pi]
            pw = ([p.guards[f"pstg{pi}"]]
                  if f"pstg{pi}" in p.guards else [])
            av = p.op("scalar", "activation", pstg[:], ps, AF.Copy,
                      wait=[("t", tv)] + pw, sem="a")
            hb.free[tag] = ("a", av)
            psv = p.op("sync", "dma_start",
                       out=bass.AP(qdram.tensor, (mc * 128) * T + g * 512,
                                   [[T, 128], [1, 512]]),
                       in_=pstg[:], wait=[("a", av)], sem="qs", n=16)
            p.guards[f"pstg{pi}"] = ("qs", psv)
            qk_norm_chunk(p, hsel, ps_qn, q2tmp, qnstg, pstg[:],
                          [("a", av), hselv],
                          bass.AP(qn2d.tensor, mc * T + g * 512,
                                  [[8 * T, 2], [1, 512]]), "q")
        for mc in range(2):
            tag = "A" if mc % 2 == 0 else "B"
            ps = hb.bank(tag)
            tv = mm_chain(p, ps,
                          [wkT[:, kc * 256 + mc * 128:
                               kc * 256 + (mc + 1) * 128]
                           for kc in range(8)],
                          [hb.xnT[:, kc * 512:(kc + 1) * 512]
                           for kc in range(8)],
                          [("v", xv)] + hb.fwait(tag))
            pi = mc % 2
            pstg = pstg2[pi]
            pw = ([p.guards[f"pstg{pi}"]]
                  if f"pstg{pi}" in p.guards else [])
            av = p.op("scalar", "activation", pstg[:], ps, AF.Copy,
                      wait=[("t", tv)] + pw, sem="a")
            hb.free[tag] = ("a", av)
            psv = p.op("sync", "dma_start",
                       out=bass.AP(kdram.tensor, (mc * 128) * T + g * 512,
                                   [[T, 128], [1, 512]]),
                       in_=pstg[:], wait=[("a", av)], sem="ks", n=16)
            p.guards[f"pstg{pi}"] = ("ks", psv)
            qk_norm_chunk(p, hsel, ps_qn, q2tmp, qnstg, pstg[:],
                          [("a", av)],
                          bass.AP(kn2d.tensor, mc * T + g * 512,
                                  [[2 * T, 2], [1, 512]]), "k")
        vacc_ev = None
        for mc in range(2):
            tag = "A" if mc % 2 == 0 else "B"
            ps = hb.bank(tag)
            tv = mm_chain(p, ps,
                          [wvT[:, kc * 256 + mc * 128:
                               kc * 256 + (mc + 1) * 128]
                           for kc in range(8)],
                          [hb.xnT[:, kc * 512:(kc + 1) * 512]
                           for kc in range(8)],
                          [("v", xv)] + hb.fwait(tag))
            vw = [p.guards["vtmp"]] if "vtmp" in p.guards else []
            av = p.op("scalar", "activation", vtmp[:], ps, AF.Copy,
                      wait=[("t", tv)] + vw, sem="a")
            hb.free[tag] = ("a", av)
            tw = [("a", av)] + hb.fwait("tbf")
            for jj in range(4):
                p.op("tensor", "transpose",
                     hb.ps_tbf[:, jj * 128:(jj + 1) * 128],
                     vtmp[:, jj * 128:(jj + 1) * 128],
                     consts["idbf"][:],
                     wait=tw if jj == 0 else (),
                     sem="t" if jj == 3 else None)
            ttv = p.val("t")
            p.guards["vtmp"] = ("t", ttv)
            dv = bass.AP(PT, mc * 128, [[16 * 512, 128], [256, 4], [1, 128]])
            pin = bass.AP(hb.ps_tbf, 0, [[1024, 128], [128, 4], [1, 128]])
            vw2 = [p.guards["vacc"]] if "vacc" in p.guards else []
            vacc_ev = p.op("vector", "tensor_copy", dv, pin,
                           wait=[("t", ttv)] + vw2, sem="v")
            hb.free["tbf"] = ("v", vacc_ev)
        for jj in range(4):
            vsv = p.op("sync", "dma_start",
                       out=vdram[g * 512 + jj * 128: g * 512 + (jj + 1) * 128, :],
                       in_=vacc[:, jj * 256:(jj + 1) * 256],
                       wait=[("v", vacc_ev)], sem="sv", n=16)
        p.guards["vacc"] = ("sv", vsv)
        hb.g_gate = [("v", xv)]

    p.barrier()
    wstack.close()

    khat2 = [p.sbuf("khat0", [65, 2048], BF16),
             p.sbuf("khat1", [65, 2048], BF16)]
    qhat2 = [p.sbuf("qhat0", [65, 2048], BF16),
             p.sbuf("qhat1", [65, 2048], BF16)]
    v12 = [p.sbuf("v1_0", [128, 16 * 65], BF16),
           p.sbuf("v1_1", [128, 16 * 65], BF16)]
    lr2 = [p.sbuf("lr0", [65, 512], F32), p.sbuf("lr1", [65, 512], F32)]
    yst2 = [p.sbuf("yst0", [64, 512], BF16), p.sbuf("yst1", [64, 512], BF16)]
    invl4 = p.sbuf("invl4", [128, 4], F32)
    cbf = p.sbuf("cbf", [128, 4 * 4096], BF16)
    ps_av = p.psum("ps_av", [128, 512], F32)
    avbanks = [ps_av[0:65, 0:512], ps_s1[0:65, 1536:2048]]
    avkeys = ("avA", "ep1")
    hb.free["avA"] = None
    itc = [0]
    pend_tail = [None]

    p.op("gpsimd", "memset", khat2[0][64:65, :], 1.0, sem="g")
    p.op("gpsimd", "memset", khat2[1][64:65, :], 1.0, sem="g")
    p.op("gpsimd", "memset", v12[0][:], 1.0, sem="g")
    p.op("gpsimd", "memset", v12[1][:], 1.0, sem="g")
    gB_done = p.val("g")
    p.guards["pb0"] = ("g", gB_done)
    p.guards["pb1"] = ("g", gB_done)

    ldsem = ("gld", "gl2")

    def emit_loads(par, s_, h, m_done):
        sm = ldsem[par]
        kvh = h // 4
        key = "pb%d" % par
        kg = [p.guards[key]] if key in p.guards else []
        p.op("gpsimd", "dma_start", out=khat2[par][0:64, :],
             in_=bass.AP(kdram.tensor, (kvh * 64) * T + s_ * 2048,
                         [[T, 64], [1, 2048]]),
             wait=kg, sem=sm, n=16)
        p.op("gpsimd", "dma_start", out=qhat2[par][0:64, :],
             in_=bass.AP(qdram.tensor, (h * 64) * T + s_ * 2048,
                         [[T, 64], [1, 2048]]),
             sem=sm, n=16)
        r_ = (h % 2) * 8 + h // 2
        p.op("gpsimd", "dma_start", out=qhat2[par][64:65, :],
             in_=bass.AP(mdram.tensor, (s_ * 16 + r_) * 2048,
                         [[2048, 1], [1, 2048]]),
             wait=[("sm", m_done)], sem=sm, n=16)
        vlv = None
        for jc in range(16):
            vlv = p.op("gpsimd", "dma_start",
                       out=v12[par][:, jc * 65: jc * 65 + 64],
                       in_=vdram[s_ * 2048 + jc * 128:
                                 s_ * 2048 + (jc + 1) * 128,
                                 kvh * 64:(kvh + 1) * 64],
                       sem=sm, n=16)
        return (sm, vlv)

    for s in range(BPC):
        gq2 = [p.guards["qnt"]] if "qnt" in p.guards else []
        gq2 += [p.guards["xng"]] if "xng" in p.guards else []
        klv4 = p.op("sync", "dma_start", out=knt4[:],
                    in_=bass.AP(kn2d.tensor, s * 2048, [[T, 4], [1, 2048]]),
                    wait=gq2, sem="ld", n=16)
        kr4 = p.op("vector", "reduce_max", kmax4[:], knt4[:], axis=AX.X,
                   wait=[("ld", klv4)], sem="v")
        ks4 = p.op("sync", "dma_start",
                   out=kx4d.rearrange("(a b) -> a b", b=1),
                   in_=kmax4[:],
                   wait=[("v", kr4)], sem="sm", n=16)
        p.wait("sync", [("sm", p.val("sm"))])
        for r_ in range(16):
            h_ = 2 * (r_ % 8) + r_ // 8
            kv_ = h_ // 4
            r2_ = (kv_ % 2) * 2 + kv_ // 2
            p.wait("sync", [("ld", p.val("ld"))])
            p.op("sync", "dma_start",
                 out=kmax2[r_:r_ + 1, 0:1],
                 in_=bass.AP(kx4d.tensor, r2_, [[1, 1], [1, 1]]),
                 sem="ld", n=16)
        k16v = p.val("ld")
        p.wait("sync", [("ld", k16v)])
        mhat_rows(p, hb, qn2d, kn2d, mdram, s, qnt, knt, kmax2, mnt,
                  kmax2_ready=[("ld", k16v)])
        m_done = p.val("sm")

        pend = {0: emit_loads(0, s, 0, m_done)}
        for h in range(16):
            par = h % 2
            if h + 1 < 16:
                pend[1 - par] = emit_loads(1 - par, s, h + 1, m_done)
            khat, qhat, v1 = khat2[par], qhat2[par], v12[par]
            smp, vlv = pend[par]
            first = [("g", gB_done)] if (s == 0 and h == 0) else []
            for ib in range(4):
                njc = 4 * ib + 4
                bk = itc[0] % 2
                avps = avbanks[bk]
                avkey = avkeys[bk]
                ptg = [p.guards["PT"]] if "PT" in p.guards else []
                avs = []
                for jc in range(njc):
                    tag = "A" if jc % 2 == 0 else "B"
                    ps = hb.bank(tag)
                    tv = p.op("tensor", "matmul", ps,
                              khat[:, jc * 128:(jc + 1) * 128],
                              qhat[:, ib * 512:(ib + 1) * 512],
                              start=True, stop=True,
                              wait=[(smp, vlv)] + first + hb.fwait(tag) +
                              (ptg if jc == 0 else []),
                              sem="t")
                    if jc >= 4 * ib:
                        off = jc - 4 * ib
                        mv2 = p.op("vector", "tensor_tensor", ps, ps,
                                   cmaskT[:, off * 512:(off + 1) * 512],
                                   op=ALU.add,
                                   wait=[("t", tv), ("g", g_done2)], sem="v")
                        av = p.op("scalar", "activation",
                                  PT[:, jc * 512:(jc + 1) * 512], ps, AF.Exp,
                                  wait=[("v", mv2)], sem="a")
                    else:
                        av = p.op("scalar", "activation",
                                  PT[:, jc * 512:(jc + 1) * 512], ps, AF.Exp,
                                  wait=[("t", tv)], sem="a")
                    avs.append(av)
                    hb.free[tag] = ("a", av)
                    if jc >= 2:
                        j2 = jc - 2
                        p.op("tensor", "matmul", avps,
                             v1[:, j2 * 65:(j2 + 1) * 65],
                             PT[:, j2 * 512:(j2 + 1) * 512],
                             start=(j2 == 0), stop=False,
                             wait=[("a", avs[j2])] +
                             (hb.fwait(avkey) if j2 == 0 else []))
                for j2 in (njc - 2, njc - 1):
                    p.op("tensor", "matmul", avps,
                         v1[:, j2 * 65:(j2 + 1) * 65],
                         PT[:, j2 * 512:(j2 + 1) * 512],
                         start=False, stop=(j2 == njc - 1),
                         wait=[("a", avs[j2])],
                         sem="t" if j2 == njc - 1 else None)
                tv = p.val("t")
                p.guards["PT"] = ("t", tv)
                tp_ = itc[0] % 2
                lw = ([p.guards["lrp%d" % tp_]]
                      if ("lrp%d" % tp_) in p.guards else [])
                lev = p.op("vector", "tensor_copy", lr2[tp_][64:65, :],
                           avps[64:65, :], wait=[("t", tv)] + lw, sem="v")
                yw = ([p.guards["ystp%d" % tp_]]
                      if ("ystp%d" % tp_) in p.guards else [])
                ysv = p.op("vector", "tensor_copy", yst2[tp_][:],
                           avps[0:64, :], wait=[("t", tv)] + yw, sem="v")
                hb.free[avkey] = ("v", ysv)
                if pend_tail[0] is not None:
                    pend_tail[0]()

                def mk_tail(tp_=tp_, lev=lev, ysv=ysv, ib=ib, h=h):
                    def tail():
                        sg = [p.guards["scr"]] if "scr" in p.guards else []
                        for j in range(4):
                            p.op("tensor", "transpose",
                                 hb.ps_scr[:, 16 + j:16 + j + 1],
                                 lr2[tp_][64:65, j * 128:(j + 1) * 128],
                                 consts["idf32"][64:65, 64:65],
                                 wait=[("v", lev)] + sg if j == 0 else (),
                                 sem="t" if j == 3 else None)
                        ltv = p.val("t")
                        p.guards["lrp%d" % tp_] = ("t", ltv)
                        ig = ([p.guards["invg"]]
                              if "invg" in p.guards else [])
                        inv_v = p.op("vector", "reciprocal", invl4[:],
                                     bass.AP(hb.ps_scr, 16,
                                             [[1024, 128], [1, 4]]),
                                     wait=[("t", ltv)] + ig, sem="v")
                        p.guards["scr"] = ("v", inv_v)
                        tw = [("v", ysv)] + hb.fwait("tbf")
                        for j in range(4):
                            p.op("tensor", "transpose",
                                 hb.ps_tbf[:, j * 64:(j + 1) * 64],
                                 yst2[tp_][:, j * 128:(j + 1) * 128],
                                 consts["idbf"][0:64, 0:64],
                                 wait=tw if j == 0 else (),
                                 sem="t" if j == 3 else None)
                        ytv = p.val("t")
                        p.guards["ystp%d" % tp_] = ("t", ytv)
                        for j in range(4):
                            p.op("vector", "tensor_scalar",
                                 cbf[:, ib * 4096 + j * 1024 + h * 64:
                                     ib * 4096 + j * 1024 + (h + 1) * 64],
                                 hb.ps_tbf[:, j * 64:(j + 1) * 64],
                                 invl4[:, j:j + 1], None, op0=ALU.mult,
                                 wait=([("t", ytv), ("v", inv_v)]
                                       if j == 0 else ()),
                                 sem="v" if j == 3 else None)
                        hb.free["tbf"] = ("v", p.val("v"))
                        p.guards["invg"] = ("v", p.val("v"))
                    return tail
                pend_tail[0] = mk_tail()
                itc[0] += 1
            p.guards["pb%d" % par] = ("t", p.val("t"))
        if pend_tail[0] is not None:
            pend_tail[0]()
            pend_tail[0] = None
        heads_v = p.val("v")

        for ib in range(4):
            g = s * 4 + ib
            ldv = hb.group_load(g, xin)
            src_tiles = [cbf[:, ib * 4096 + j * 1024:
                             ib * 4096 + (j + 1) * 1024] for j in range(4)]

            def dview(j):
                return bass.AP(hb.xnT, j * 128,
                               [[8 * 512, 128], [512, 8], [1, 128]])
            cxv = transpose_to_fm(p, hb.ps_tbf, consts["idbf"], src_tiles,
                                  dview, 4,
                                  wait_src=[[("v", heads_v)] +
                                            hb.fwait("tbf")
                                            for j in range(4)])
            hb.free["tbf"] = ("v", cxv)
            for mc in range(8):
                tag = "A" if mc % 2 == 0 else "B"
                ps = hb.bank(tag)
                tv = mm_chain(p, ps,
                              [woT[:, kc * 1024 + mc * 128:
                                   kc * 1024 + (mc + 1) * 128]
                               for kc in range(8)],
                              [hb.xnT[:, kc * 512:(kc + 1) * 512]
                               for kc in range(8)],
                              [("v", cxv)] + hb.fwait(tag))
                av = p.op("scalar", "activation",
                          yfm[:, mc * 512:(mc + 1) * 512], ps, AF.Copy,
                          wait=[("t", tv)], sem="a")
                hb.free[tag] = ("a", av)
            ya = p.val("a")
            hb.yfm = yfm
            stv = hb.epilogue(g, [("a", ya)], xout, eps5,
                              x_wait=[("ld", ldv)])
            p.guards["xng"] = ("st", stv[-1])
    if fused is None:
        return finish(p)
    return None


def run_l3attn(x, wq, wk, wv, wo):
    return _launch("l3attn", build_l3attn, x,
                   {"wq": wq, "wk": wk, "wv": wv, "wo": wo})



def build_l3ffn(fused=None):
    if fused is None:
        p = P("l3ffn")
        nc = p.nc
        xin = nc.dram_tensor("xin", [T, D], F32, kind="ExternalInput").ap()
        xout = nc.dram_tensor("xout", [T, D], F32, kind="ExternalOutput").ap()
        W = {k: nc.dram_tensor(k2, sh, F32, kind="ExternalInput").ap()
             for k, k2, sh in (
                 ("L3_ffn_Wg", "Wg", [1024, 8]),
                 ("L3_ffn_W1", "W1", [8, 1024, 1408]),
                 ("L3_ffn_W2", "W2", [8, 1024, 1408]),
                 ("L3_ffn_W3", "W3", [8, 1408, 1024]),
                 ("L3_ffn_Ws1", "Ws1", [1024, 2816]),
                 ("L3_ffn_Ws2", "Ws2", [1024, 2816]),
                 ("L3_ffn_Ws3", "Ws3", [2816, 1024]))}
        consts = build_consts(p, f32=True, bf=True, eps=(1e-5, 1e-6))
    else:
        p, consts, xin, xout, W = fused
        nc = p.nc
    wgd = W["L3_ffn_Wg"]; w1d = W["L3_ffn_W1"]; w2d = W["L3_ffn_W2"]
    w3d = W["L3_ffn_W3"]; ws1d = W["L3_ffn_Ws1"]; ws2d = W["L3_ffn_Ws2"]
    ws3d = W["L3_ffn_Ws3"]

    xnTd = p.dram("xnTd", [1024, T], BF16).ap()
    wfmd = p.dram("wfmd", [8, T], F32).ap()
    yaccd = p.dram("yaccd", [1024, T], F32).ap()
    eps5 = consts["eps1em05"]
    hb = HBCommon(p, consts, h_cols=0)

    xnf3 = p.sbuf("xnf3", [128, 1024], F32)
    xnT32 = p.sbuf("xnT32", [128, 8 * 512], F32)
    wg = p.sbuf("wg_sb", [128, 8 * 8], F32)
    rsb = p.sbuf("rsb", [128, 8], F32)
    m1 = p.sbuf("m1", [128, 1], F32)
    m2 = p.sbuf("m2", [128, 1], F32)
    eq1 = p.sbuf("eq1", [128, 8], F32)
    eq2 = p.sbuf("eq2", [128, 8], F32)
    rmk = p.sbuf("rmk", [128, 8], F32)
    dd = p.sbuf("dd", [128, 2], F32)
    pp = p.sbuf("pp", [128, 2], F32)
    w8 = p.sbuf("w8", [128, 8], F32)
    wfstg = p.sbuf("wfstg", [8, 512], F32)
    w1e = p.sbuf("w1e", [128, 8 * 1408], BF16)
    w2e = p.sbuf("w2e", [128, 8 * 1408], BF16)
    w3e = p.sbuf("w3e", [128, 11 * 1024], BF16)
    hsb = p.sbuf("hsb", [128, 11 * 512], BF16)
    stmp = p.sbuf("stmp", [128, 512], F32)
    tmp_sb = p.sbuf("tmp_sb", [128, 512], F32)
    webc = p.sbuf("webc", [128, 512], F32)
    wrow = p.sbuf("wrow", [1, 512], F32)

    for kc in range(8):
        p.op("sync", "dma_start", out=wg[:, kc * 8:(kc + 1) * 8],
             in_=wgd[kc * 128:(kc + 1) * 128, :], sem="ld", n=16)
    wgv = p.val("ld")
    hb.g_gate = [("ld", wgv), ("g", consts["g_done"])]

    for g in range(NG):
        ldv = hb.group_load(g, xin)
        sv = ln_stats(p, hb.st_in, hb.ps_scr[:], hb.xt, eps5,
                      [[("ld", ldv), ("g", consts["g_done"])]] * 4)
        xnb_vals = []
        xn32_vals = []
        for j in range(4):
            xw = [p.guards["xnf3"]] if "xnf3" in p.guards else []
            n1 = p.op("vector", "tensor_scalar", xnf3[:], hb.xt[j],
                      hb.st_in.mean4[:, j:j + 1], hb.st_in.rstd4[:, j:j + 1],
                      op0=ALU.subtract, op1=ALU.mult,
                      wait=([("v", sv)] if j == 0 else []) + xw,
                      sem="v")
            n2 = p.op("vector", "tensor_copy",
                      hb.xnb[:, j * 1024:(j + 1) * 1024], xnf3[:],
                      wait=[("v", n1)], sem="v")
            xnb_vals.append(n2)
            tw = [("v", n1)] + hb.fwait("ep0", "ep1")
            for cc in range(8):
                p.op("tensor", "transpose",
                     hb.ep_ap[:, cc * 128:(cc + 1) * 128],
                     xnf3[:, cc * 128:(cc + 1) * 128],
                     consts["idf32"][:],
                     wait=tw if cc == 0 else (),
                     sem="t" if cc == 7 else None)
            ttv = p.val("t")
            dv32 = bass.AP(xnT32, j * 128, [[8 * 512, 128], [512, 8], [1, 128]])
            pin = hb.ep_ap.rearrange("p (c m) -> p c m", c=8)
            ev = p.op("vector", "tensor_copy", dv32, pin,
                      wait=[("t", ttv)], sem="v")
            hb.free["ep0"] = hb.free["ep1"] = ("v", ev)
            p.guards["xnf3"] = ("v", ev)
            xn32_vals.append(ev)
        src_tiles = [hb.xnb[:, j * 1024:(j + 1) * 1024] for j in range(4)]

        def dview(j):
            return bass.AP(hb.xnT, j * 128,
                           [[8 * 512, 128], [512, 8], [1, 128]])
        xtg = [p.guards["xnTs"]] if "xnTs" in p.guards else []
        xv = transpose_to_fm(p, hb.ps_tbf, consts["idbf"], src_tiles, dview,
                             4, wait_src=[[("v", xnb_vals[j])] +
                                          hb.fwait("tbf") + xtg
                                          for j in range(4)])
        hb.free["tbf"] = ("v", xv)
        for kc in range(8):
            sxv = p.op("sync", "dma_start",
                       out=bass.AP(xnTd.tensor, (kc * 128) * T + g * 512,
                                   [[T, 128], [1, 512]]),
                       in_=hb.xnT[:, kc * 512:(kc + 1) * 512],
                       wait=[("v", xv)] if kc == 0 else (), sem="sx", n=16)
        p.guards["xnTs"] = ("sx", sxv)
        for j in range(4):
            for kc in range(8):
                p.op("tensor", "matmul", hb.bank("A")[:, 0:8],
                     xnT32[:, kc * 512 + j * 128: kc * 512 + (j + 1) * 128],
                     wg[:, kc * 8:(kc + 1) * 8],
                     start=(kc == 0), stop=(kc == 7),
                     wait=([("v", xn32_vals[3]), ("ld", wgv)] +
                           hb.fwait("A")) if kc == 0 else (),
                     sem="t" if kc == 7 else None)
            rtv = p.val("t")
            rg = [p.guards["rsb"]] if "rsb" in p.guards else []
            r0 = p.op("vector", "tensor_copy", rsb[:], hb.bank("A")[:, 0:8],
                      wait=[("t", rtv)] + rg, sem="v")
            hb.free["A"] = ("v", r0)
            v1_ = p.op("vector", "reduce_max", m1[:], rsb[:], axis=AX.X,
                       wait=[("v", r0)], sem="v")
            v2_ = p.op("vector", "tensor_single_scalar", eq1[:], rsb[:],
                       m1[:], op=ALU.is_ge, wait=[("v", v1_)], sem="v")
            v3_ = p.op("vector", "scalar_tensor_tensor", rmk[:], eq1[:],
                       -1e30, rsb[:], op0=ALU.mult, op1=ALU.add,
                       wait=[("v", v2_)], sem="v")
            v4_ = p.op("vector", "reduce_max", m2[:], rmk[:], axis=AX.X,
                       wait=[("v", v3_)], sem="v")
            v5_ = p.op("vector", "tensor_single_scalar", eq2[:], rmk[:],
                       m2[:], op=ALU.is_ge, wait=[("v", v4_)], sem="v")
            v6_ = p.op("vector", "tensor_sub", dd[:, 0:1], m1[:], m2[:],
                       wait=[("v", v5_)], sem="v")
            v7_ = p.op("vector", "tensor_scalar_mul", dd[:, 1:2], dd[:, 0:1],
                       -1.0, wait=[("v", v6_)], sem="v")
            a1_ = p.op("scalar", "activation", pp[:], dd[:], AF.Sigmoid,
                       wait=[("v", v7_)], sem="a")
            v8_ = p.op("vector", "tensor_scalar", w8[:], eq1[:], pp[:, 0:1],
                       None, op0=ALU.mult, wait=[("a", a1_)], sem="v")
            v9_ = p.op("vector", "scalar_tensor_tensor", w8[:], eq2[:],
                       pp[:, 1:2], w8[:], op0=ALU.mult, op1=ALU.add,
                       wait=[("v", v8_)], sem="v")
            wtv = p.op("tensor", "transpose", hb.ps_scr[0:8, 0:128],
                       w8[:, 0:8], consts["idf32"][:],
                       wait=[("v", v9_)] + ([p.guards["scr8"]]
                                            if "scr8" in p.guards else []),
                       sem="t")
            wev = p.op("vector", "tensor_copy", wfstg[:, j * 128:(j + 1) * 128],
                       hb.ps_scr[0:8, 0:128],
                       wait=[("t", wtv)] + ([p.guards["wfstg"]]
                                            if ("wfstg" in p.guards and j == 0)
                                            else []),
                       sem="v")
            p.guards["scr8"] = ("v", wev)
        swv = p.op("sync", "dma_start",
                   out=bass.AP(wfmd.tensor, g * 512, [[T, 8], [1, 512]]),
                   in_=wfstg[:], wait=[("v", p.val("v"))], sem="sw", n=16)
        p.guards["wfstg"] = ("sw", swv)
        hb.g_gate = [("v", xv), ("v", p.val("v"))]

    phaseA = [("sx", p.val("sx")), ("sw", p.val("sw"))]

    ya_gate = {}
    wcons = None
    xg_cons = None
    for e in range(10):
        wld = [("g", consts["g_done"])] + list(phaseA)
        if wcons is not None:
            wld.append(("t", wcons))
        if e < 8:
            for kc in range(8):
                p.op("gpsimd", "dma_start", out=w1e[:, kc * 1408:(kc + 1) * 1408],
                     in_=w1d[e, kc * 128:(kc + 1) * 128, :],
                     wait=wld if kc == 0 else (), sem="gld", n=16)
                p.op("gpsimd", "dma_start", out=w2e[:, kc * 1408:(kc + 1) * 1408],
                     in_=w2d[e, kc * 128:(kc + 1) * 128, :], sem="gld", n=16)
            for kc in range(11):
                p.op("gpsimd", "dma_start", out=w3e[:, kc * 1024:(kc + 1) * 1024],
                     in_=w3d[e, kc * 128:(kc + 1) * 128, :], sem="gld", n=16)
        else:
            e2 = e - 8
            for kc in range(8):
                p.op("gpsimd", "dma_start", out=w1e[:, kc * 1408:(kc + 1) * 1408],
                     in_=ws1d[kc * 128:(kc + 1) * 128,
                              e2 * 1408:(e2 + 1) * 1408],
                     wait=wld if kc == 0 else (), sem="gld", n=16)
                p.op("gpsimd", "dma_start", out=w2e[:, kc * 1408:(kc + 1) * 1408],
                     in_=ws2d[kc * 128:(kc + 1) * 128,
                              e2 * 1408:(e2 + 1) * 1408], sem="gld", n=16)
            for kc in range(11):
                p.op("gpsimd", "dma_start", out=w3e[:, kc * 1024:(kc + 1) * 1024],
                     in_=ws3d[e2 * 1408 + kc * 128: e2 * 1408 + (kc + 1) * 128,
                              :], sem="gld", n=16)
        wv_ = p.val("gld")

        for g in range(NG):
            xld = [("gld", wv_)]
            if xg_cons is not None:
                xld.append(("t", xg_cons))
            for kc in range(8):
                p.op("gpsimd", "dma_start",
                     out=hb.xnT[:, kc * 512:(kc + 1) * 512],
                     in_=bass.AP(xnTd.tensor, (kc * 128) * T + g * 512,
                                 [[T, 128], [1, 512]]),
                     wait=xld if kc == 0 else (), sem="gld", n=16)
            xlv = p.val("gld")
            if e < 8:
                p.wait("sync", [("ld", p.val("ld"))])
                wg_ = [p.guards["wrow"]] if "wrow" in p.guards else []
                wlv = p.op("sync", "dma_start", out=wrow[:],
                           in_=bass.AP(wfmd.tensor, e * T + g * 512,
                                       [[512, 1], [1, 512]]),
                           wait=wg_, sem="ld", n=16)
                btv = p.op("tensor", "matmul", hb.bank("ep0"),
                           consts["ones"][0:1, 0:128], wrow[0:1, :],
                           start=True, stop=True,
                           wait=[("ld", wlv)] + hb.fwait("ep0"), sem="t")
                wbv = p.op("vector", "tensor_copy", webc[:], hb.bank("ep0"),
                           wait=[("t", btv)] + ([p.guards["webc"]]
                                                if "webc" in p.guards else []),
                           sem="v")
                hb.free["ep0"] = ("v", wbv)
                p.guards["wrow"] = ("t", btv)
            for hc in range(11):
                tvA = mm_chain(p, hb.bank("A"),
                               [w1e[:, kc * 1408 + hc * 128:
                                    kc * 1408 + (hc + 1) * 128]
                                for kc in range(8)],
                               [hb.xnT[:, kc * 512:(kc + 1) * 512]
                                for kc in range(8)],
                               [("gld", xlv)] + hb.fwait("A"))
                tvB = mm_chain(p, hb.bank("B"),
                               [w2e[:, kc * 1408 + hc * 128:
                                    kc * 1408 + (hc + 1) * 128]
                                for kc in range(8)],
                               [hb.xnT[:, kc * 512:(kc + 1) * 512]
                                for kc in range(8)],
                               hb.fwait("B"))
                sg = [p.guards["stmp"]] if "stmp" in p.guards else []
                av = p.op("scalar", "activation", stmp[:], hb.bank("A"),
                          AF.Silu, wait=[("t", tvA)] + sg, sem="a")
                hb.free["A"] = ("a", av)
                hv = p.op("vector", "tensor_tensor",
                          hsb[:, hc * 512:(hc + 1) * 512], stmp[:],
                          hb.bank("B"), op=ALU.mult,
                          wait=[("a", av), ("t", tvB)] +
                          ([p.guards["hsb"]] if ("hsb" in p.guards and
                                                 hc == 0) else []),
                          sem="v")
                hb.free["B"] = ("v", hv)
                p.guards["stmp"] = ("v", hv)
            h_v = p.val("v")
            xg_cons = p.val("t")
            for mc in range(8):
                tag = "A" if mc % 2 == 0 else "B"
                tv = mm_chain(p, hb.bank(tag),
                              [w3e[:, kc * 1024 + mc * 128:
                                   kc * 1024 + (mc + 1) * 128]
                               for kc in range(11)],
                              [hsb[:, kc * 512:(kc + 1) * 512]
                               for kc in range(11)],
                              [("v", h_v)] + hb.fwait(tag))
                tg = [p.guards["tmp_sb"]] if "tmp_sb" in p.guards else []
                if e < 8:
                    tv2 = p.op("vector", "scalar_tensor_tensor", tmp_sb[:],
                               hb.bank(tag), 1.0, webc[:],
                               op0=ALU.mult, op1=ALU.mult,
                               wait=[("t", tv), ("v", wbv)] + tg, sem="v")
                else:
                    tv2 = p.op("vector", "tensor_copy", tmp_sb[:],
                               hb.bank(tag), wait=[("t", tv)] + tg, sem="v")
                hb.free[tag] = ("v", tv2)
                yw = [ya_gate[g]] if g in ya_gate else []
                yav = p.op("gpsimd", "dma_start",
                           out=bass.AP(yaccd.tensor, (mc * 128) * T + g * 512,
                                       [[T, 128], [1, 512]]),
                           in_=tmp_sb[:],
                           accum_op=(ALU.bypass if e == 0 else ALU.add),
                           wait=[("v", tv2)] + yw, sem="ya", n=16)
                p.guards["tmp_sb"] = ("ya", yav)
                if e < 8:
                    p.guards["webc"] = ("v", tv2)
            ya_gate[g] = ("ya", p.val("ya"))
        wcons = p.val("t")

    ya_all = p.val("ya")

    hb.g_gate = []
    p.wait("sync", [("ld", p.val("ld"))])
    for g in range(NG):
        ldv = hb.group_load(g, xin)
        p.wait("sync", [("ld", p.val("ld"))])
        ylv = p.op("sync", "dma_start",
                   out=hb.yfm[:].rearrange("p (c m) -> p c m", c=8),
                   in_=bass.AP(yaccd.tensor, g * 512,
                               [[T, 128], [128 * T, 8], [1, 512]]),
                   wait=[("ya", ya_all)] + ([p.guards["yfm"]]
                                            if "yfm" in p.guards else []),
                   sem="ld", n=16)
        hb.epilogue(g, [("ld", ylv)], xout, eps5, x_wait=[("ld", ylv)])
        p.guards["yfm"] = ("t", p.val("t"))
    if fused is None:
        return finish(p)
    return None


def run_l3ffn(x, Wg, W1, W2, W3, Ws1, Ws2, Ws3):
    return _launch("l3ffn", build_l3ffn, x,
                   {"Wg": Wg, "W1": W1, "W2": W2, "W3": W3,
                    "Ws1": Ws1, "Ws2": Ws2, "Ws3": Ws3})



L1A_STOP = [None]


def build_l1attn(fused=None):
    if fused is None:
        p = P("l1attn")
        nc = p.nc
        xin = nc.dram_tensor("xin", [T, D], F32, kind="ExternalInput").ap()
        xout = nc.dram_tensor("xout", [T, D], F32, kind="ExternalOutput").ap()
        W = {k: nc.dram_tensor(k2, sh, F32, kind="ExternalInput").ap()
             for k, k2, sh in (
                 ("L1_attn_w_qkv", "w_qkv", [1024, 3072]),
                 ("L1_attn_b_qkv", "b_qkv", [3072]),
                 ("L1_attn_w_out", "w_out", [1024, 1024]),
                 ("L1_attn_b_out", "b_out", [1024]),
                 ("L1_attn_w1_rpe", "w1_rpe", [1, 32]),
                 ("L1_attn_b1_rpe", "b1_rpe", [32]),
                 ("L1_attn_w2_rpe", "w2_rpe", [32, 16]),
                 ("L1_attn_b2_rpe", "b2_rpe", [16]))}
        consts = build_consts(p, f32=True, bf=True, eps=(1e-5, 1e-6))
    else:
        p, consts, xin, xout, W = fused
        nc = p.nc
    wqkvd = W["L1_attn_w_qkv"]; bqkvd = W["L1_attn_b_qkv"]
    wod = W["L1_attn_w_out"]; bod = W["L1_attn_b_out"]
    w1rd = W["L1_attn_w1_rpe"]; b1rd = W["L1_attn_b1_rpe"]
    w2rd = W["L1_attn_w2_rpe"]; b2rd = W["L1_attn_b2_rpe"]

    qdram = p.dram("qfm", [1024, T], BF16).ap()
    kdram = p.dram("kfm", [1024, T], BF16).ap()
    vdram = p.dram("vtm", [T, 1024], BF16).ap()
    mdram = p.dram("mdr", [BPC, 16, 2048], F32).ap()
    qn2d = p.dram("qn2d", [2, 8, T], F32).ap()
    kn2d = p.dram("kn2d", [2, 8, T], F32).ap()
    bm16d = p.dram("bm16d", [16], F32).ap()
    ewided = p.dram("ewided", [16, 128 * 4351 + 8], BF16).ap()
    eps5 = consts["eps1em05"]

    ps_s1 = p.psum("ps_s1", [128, 2048], F32)
    banks = {"A": ps_s1[:, 0:512], "B": ps_s1[:, 512:1024],
             "ep0": ps_s1[:, 1024:1536], "ep1": ps_s1[:, 1536:2048]}
    hb = HBCommon(p, consts, h_cols=0, banks=banks,
                  ep_ap=ps_s1[:, 1024:2048])

    wstg = hb.xg
    wo, _ = load_w_fm(p, "wo_sb", wod, 1024, 1024)
    bo, bov = load_bias_cols(p, "bo", bod, 1024, consts, hb.ps_scr[:])
    kmax2 = p.sbuf("kmax2", [16, 1], F32)
    bmax_r = p.sbuf("bmax_r", [16, 1], F32)
    yfm = p.sbuf("yfm_l1", [128, 8 * 512], F32)
    PT = p.sbuf("PT", [128, 16 * 512], BF16)

    if p.stack is None:
        p.stack = contextlib.ExitStack()
    wstack = contextlib.ExitStack()
    saved_stack = p.stack
    p.stack = wstack

    wq, wqv = load_w_scaled(p, "wq_sb", wqkvd[:, 0:1024], 1024, 1024,
                            1.0 / 8.0, wstg)
    wk, _ = load_w_fm(p, "wk_sb", wqkvd[:, 1024:2048], 1024, 1024)
    wv_, _ = load_w_fm(p, "wv_sb", wqkvd[:, 2048:3072], 1024, 1024)
    wldv = p.val("gld")
    bq, bqv = load_bias_cols(p, "bq", bqkvd[0:1024], 1024, consts,
                             hb.ps_scr[:], wait_extra=[wqv, ("v", bov)])
    bsc = p.op("vector", "tensor_scalar_mul", bq[:], bq[:], 1.0 / 8.0,
               wait=[("v", bqv)], sem="v")
    bk, bkv = load_bias_cols(p, "bk", bqkvd[1024:2048], 1024, consts,
                             hb.ps_scr[:], wait_extra=[("v", bsc)])
    bv, bvv = load_bias_cols(p, "bv", bqkvd[2048:3072], 1024, consts,
                             hb.ps_scr[:], wait_extra=[("v", bkv)])

    h_rpe = bass.AP(yfm, 0, [[4096, 32], [1, 4095]])
    tab = bass.AP(hb.ytm, 0, [[4096, 16], [1, 4095]])
    etab = bass.AP(PT, 0, [[8192, 16], [1, 4095]])
    relrow_h = hb.ytm
    relrow = bass.AP(relrow_h, 0, [[4096, 1], [1, 4095]])
    sgnrow = bass.AP(yfm, 0, [[4096, 1], [1, 4095]])
    w1r = p.sbuf("w1r", [1, 32], F32)
    w2r = p.sbuf("w2r", [32, 16], F32)
    b1c = p.sbuf("b1c", [32, 1], F32)
    b2c = p.sbuf("b2c", [16, 1], F32)
    brow = p.sbuf("brow", [1, 32], F32)
    bmax = p.sbuf("bmax", [16, 1], F32)
    bmr = p.sbuf("bmr", [1, 16], F32)
    hsel, hselv = build_hsel(p, consts)
    ps_qn = p.psum("ps_qn", [128, 512], F32)
    q2tmp = p.sbuf("q2tmp", [128, 512], BF16)
    qnstg = p.sbuf("qnstg", [2, 512], F32)

    iv = p.op("gpsimd", "iota", relrow, [[1, 4095]], base=-2047,
              channel_multiplier=0, allow_small_or_imprecise_dtypes=True,
              sem="g")
    sv_ = p.op("scalar", "sign", sgnrow, relrow,
               wait=[("g", iv)], sem="a")
    av_ = p.op("scalar", "activation", relrow, relrow, AF.Abs,
               wait=[("a", sv_)], sem="a")
    lv_ = p.op("scalar", "activation", relrow, relrow, AF.Ln,
               bias=consts["ones"][0:1, 0:1], wait=[("a", av_)], sem="a")
    rv_ = p.op("vector", "tensor_mul", relrow, relrow, sgnrow,
               wait=[("a", lv_)], sem="v")
    p.wait("sync", [("ld", p.val("ld"))])
    l1 = p.op("sync", "dma_start", out=w1r[:],
              in_=w1rd, sem="ld", n=16)
    l1b = p.op("sync", "dma_start", out=brow[:],
               in_=b1rd.rearrange("(a n) -> a n", a=1), sem="ld", n=16)
    l2 = p.op("sync", "dma_start", out=w2r[:], in_=w2rd, sem="ld", n=16)
    l2b = p.op("sync", "dma_start", out=bmr[:, 0:16],
               in_=b2rd.rearrange("(a n) -> a n", a=1), sem="ld", n=16)
    t1 = p.op("tensor", "transpose", hb.ps_scr[0:32, 0:1], brow[:],
              consts["idf32"][0:1, 0:1],
              wait=[("ld", l2b), ("g", consts["g_done"]), ("v", bvv)],
              sem="t")
    e1 = p.op("vector", "tensor_copy", b1c[:], hb.ps_scr[0:32, 0:1],
              wait=[("t", t1)], sem="v")
    t2 = p.op("tensor", "transpose", hb.ps_scr[0:16, 1:2], bmr[:, 0:16],
              consts["idf32"][0:1, 0:1], wait=[("v", e1)], sem="t")
    e2 = p.op("vector", "tensor_copy", b2c[:], hb.ps_scr[0:16, 1:2],
              wait=[("t", t2)], sem="v")
    for i in range(8):
        w = 512 if i < 7 else 4095 - 7 * 512
        tv = p.op("tensor", "matmul", hb.ps_scr[0:32, 512:512 + w],
                  w1r[:], bass.AP(relrow_h, i * 512, [[4096, 1], [1, w]]),
                  start=True, stop=True,
                  wait=[("v", rv_), ("v", e2), ("ld", l2b)], sem="t")
        p.op("scalar", "activation",
             bass.AP(yfm, i * 512, [[4096, 32], [1, w]]),
             hb.ps_scr[0:32, 512:512 + w], AF.Relu, bias=b1c[:],
             wait=[("t", tv)], sem="a")
        hwv = p.val("a")
        p.wait("tensor", [("a", hwv)])
    for i in range(8):
        w = 512 if i < 7 else 4095 - 7 * 512
        tv = p.op("tensor", "matmul", hb.ps_scr[0:16, 512:512 + w],
                  w2r[:], bass.AP(yfm, i * 512, [[4096, 32], [1, w]]),
                  start=True, stop=True, sem="t")
        p.op("scalar", "activation",
             bass.AP(hb.ytm, i * 512, [[4096, 16], [1, w]]),
             hb.ps_scr[0:16, 512:512 + w], AF.Identity, bias=b2c[:],
             wait=[("t", tv)], sem="a")
        tbv = p.val("a")
        p.wait("tensor", [("a", tbv)])
    bmv = p.op("vector", "reduce_max", bmax[:], tab, axis=AX.X,
               wait=[("a", tbv)], sem="v")
    ev_ = p.op("scalar", "activation", etab, tab, AF.Exp,
               wait=[("a", tbv)], sem="a")
    t3 = p.op("tensor", "transpose", hb.ps_scr[0:1, 0:16], bmax[:],
              consts["idf32"][0:16, 0:16], wait=[("v", bmv)], sem="t")
    e3 = p.op("vector", "tensor_copy", bmr[:, 0:16], hb.ps_scr[0:1, 0:16],
              wait=[("t", t3)], sem="v")
    bs1 = p.op("sync", "dma_start",
               out=bm16d.rearrange("(a b) -> a b", a=1),
               in_=bmr[0:1, 0:16],
               wait=[("v", e3)], sem="se", n=16)
    p.wait("sync", [("se", p.val("se"))])
    for r_ in range(16):
        h_ = 2 * (r_ % 8) + r_ // 8
        e4 = p.op("sync", "dma_start", out=bmax_r[r_:r_ + 1, 0:1],
                  in_=bass.AP(bm16d.tensor, h_, [[1, 1], [1, 1]]),
                  sem="ld", n=16)
    etabd = p.dram("etabd", [16, 4095], BF16).ap()
    et_s = p.op("sync", "dma_start", out=etabd, in_=etab,
                wait=[("a", ev_)], sem="se", n=16)
    p.wait("sync", [("se", et_s)])
    ew_v = None
    for h in range(16):
        dst = bass.AP(ewided.tensor, h * (128 * 4351 + 8),
                      [[4351, 128], [1, 4095]])
        srcb = bass.AP(etabd.tensor, h * 4095, [[0, 128], [1, 4095]])
        ew_v = p.op("sync", "dma_start", out=dst, in_=srcb,
                    sem="se", n=16)

    pstg = p.sbuf("pstg", [128, 512], BF16)
    vtmp = p.sbuf("vtmp", [128, 512], BF16)
    vacc = PT[:, 0:4096]
    qnt = bass.AP(hb.ytm, 0, [[4096, 16], [1, 2048]])
    knt = bass.AP(hb.ytm, 2048, [[4096, 16], [1, 2048]])
    mnt = bass.AP(hb.ytm, 32 * 4096, [[4096, 16], [1, 2048]])
    p.stack = saved_stack

    hb.g_gate = [("v", bvv), ("v", bov), ("gld", wldv), ("se", ew_v),
                 ("ld", e4), ("g", hselv[1])]
    if L1A_STOP[0] == "prep":
        p.wait("sync", [("se", p.val("se")), ("v", p.val("v")),
                        ("gld", p.val("gld"))])
        p.verify()
        return p.finalize()

    for g in range(NG):
        ldv = hb.group_load(g, xin)
        xv = hb.sweep(g, ldv, eps5)
        for which, wsb, bsb_, sem_s, dr, nd in (
                ("q", wq, bq, "qs", qdram, qn2d),
                ("k", wk, bk, "ks", kdram, kn2d)):
            for mc in range(8):
                tag = "A" if mc % 2 == 0 else "B"
                ps = hb.bank(tag)
                tv = mm_chain(p, ps,
                              [wsb[:, kc * 1024 + mc * 128:
                                   kc * 1024 + (mc + 1) * 128]
                               for kc in range(8)],
                              [hb.xnT[:, kc * 512:(kc + 1) * 512]
                               for kc in range(8)],
                              [("v", xv)] + hb.fwait(tag))
                pw = [p.guards["pstg"]] if "pstg" in p.guards else []
                av = p.op("scalar", "activation", pstg[:], ps, AF.Identity,
                          bias=bsb_[:, mc:mc + 1],
                          wait=[("t", tv)] + pw, sem="a")
                hb.free[tag] = ("a", av)
                psv = p.op("sync", "dma_start",
                           out=bass.AP(dr.tensor, (mc * 128) * T + g * 512,
                                       [[T, 128], [1, 512]]),
                           in_=pstg[:], wait=[("a", av)], sem=sem_s, n=16)
                p.guards["pstg"] = (sem_s, psv)
                qk_norm_chunk(p, hsel, ps_qn, q2tmp, qnstg, pstg[:],
                              [("a", av)],
                              bass.AP(nd.tensor, mc * T + g * 512,
                                      [[8 * T, 2], [1, 512]]), which)
        vacc_ev = None
        for mc in range(8):
            tag = "A" if mc % 2 == 0 else "B"
            ps = hb.bank(tag)
            tv = mm_chain(p, ps,
                          [wv_[:, kc * 1024 + mc * 128:
                               kc * 1024 + (mc + 1) * 128]
                           for kc in range(8)],
                          [hb.xnT[:, kc * 512:(kc + 1) * 512]
                           for kc in range(8)],
                          [("v", xv)] + hb.fwait(tag))
            vw = [p.guards["vtmp"]] if "vtmp" in p.guards else []
            av = p.op("scalar", "activation", vtmp[:], ps, AF.Identity,
                      bias=bv[:, mc:mc + 1], wait=[("t", tv)] + vw, sem="a")
            hb.free[tag] = ("a", av)
            tw = [("a", av)] + hb.fwait("tbf")
            for jj in range(4):
                p.op("tensor", "transpose",
                     hb.ps_tbf[:, jj * 128:(jj + 1) * 128],
                     vtmp[:, jj * 128:(jj + 1) * 128],
                     consts["idbf"][:],
                     wait=tw if jj == 0 else (),
                     sem="t" if jj == 3 else None)
            ttv = p.val("t")
            p.guards["vtmp"] = ("t", ttv)
            dv = bass.AP(PT, mc * 128, [[8 * 1024, 128], [1024, 4], [1, 128]])
            pin = bass.AP(hb.ps_tbf, 0, [[1024, 128], [128, 4], [1, 128]])
            vw2 = [p.guards["vacc"]] if "vacc" in p.guards else []
            vacc_ev = p.op("vector", "tensor_copy", dv, pin,
                           wait=[("t", ttv)] + vw2, sem="v")
            hb.free["tbf"] = ("v", vacc_ev)
        for jj in range(4):
            vsv = p.op("sync", "dma_start",
                       out=vdram[g * 512 + jj * 128: g * 512 + (jj + 1) * 128, :],
                       in_=vacc[:, jj * 1024:(jj + 1) * 1024],
                       wait=[("v", vacc_ev)], sem="sv", n=16)
        p.guards["vacc"] = ("sv", vsv)
        hb.g_gate = [("v", xv)]

    p.barrier()
    wstack.close()

    khat2 = [p.sbuf("khat0", [65, 2048], BF16),
             p.sbuf("khat1", [65, 2048], BF16)]
    qhat2 = [p.sbuf("qhat0", [65, 2048], BF16),
             p.sbuf("qhat1", [65, 2048], BF16)]
    v12 = [p.sbuf("v1_0", [128, 16 * 65], BF16),
           p.sbuf("v1_1", [128, 16 * 65], BF16)]
    wide2 = [p.sbuf("wide0", [128, 3968], BF16),
             p.sbuf("wide1", [128, 3968], BF16)]
    lr2 = [p.sbuf("lr0", [65, 512], F32), p.sbuf("lr1", [65, 512], F32)]
    yst2 = [p.sbuf("yst0", [64, 512], BF16), p.sbuf("yst1", [64, 512], BF16)]
    invl4 = p.sbuf("invl4", [128, 4], F32)
    cbf = p.sbuf("cbf", [128, 4 * 4096], BF16)
    ps_av = p.psum("ps_av", [128, 512], F32)
    avbanks = [ps_av[0:65, 0:512], ps_s1[0:65, 1536:2048]]
    avkeys = ("avA", "ep1")
    hb.free["avA"] = None
    itc = [0]
    pend_tail = [None]

    p.op("gpsimd", "memset", khat2[0][64:65, :], 1.0, sem="g")
    p.op("gpsimd", "memset", khat2[1][64:65, :], 1.0, sem="g")
    p.op("gpsimd", "memset", v12[0][:], 1.0, sem="g")
    p.op("gpsimd", "memset", v12[1][:], 1.0, sem="g")
    gB_done = p.val("g")
    p.guards["pbt0"] = ("g", gB_done)
    p.guards["pbt1"] = ("g", gB_done)

    ldsem = ("gld", "gl2")

    def emit_loads(par, s_, h, m_done):
        sm = ldsem[par]
        kg = []
        for key in ("pbt%d" % par, "pbv%d" % par):
            if key in p.guards:
                kg.append(p.guards[key])
        p.op("gpsimd", "dma_start", out=khat2[par][0:64, :],
             in_=bass.AP(kdram.tensor, (h * 64) * T + s_ * 2048,
                         [[T, 64], [1, 2048]]),
             wait=kg, sem=sm, n=16)
        p.op("gpsimd", "dma_start", out=qhat2[par][0:64, :],
             in_=bass.AP(qdram.tensor, (h * 64) * T + s_ * 2048,
                         [[T, 64], [1, 2048]]),
             sem=sm, n=16)
        r_ = (h % 2) * 8 + h // 2
        p.op("gpsimd", "dma_start", out=qhat2[par][64:65, :],
             in_=bass.AP(mdram.tensor, (s_ * 16 + r_) * 2048,
                         [[2048, 1], [1, 2048]]),
             wait=[("sm", m_done)], sem=sm, n=16)
        p.op("gpsimd", "dma_start", out=wide2[par][:],
             in_=bass.AP(ewided.tensor, h * (128 * 4351 + 8) + 127,
                         [[4350, 128], [1, 3968]]),
             sem=sm, n=16)
        vlv = None
        for jc in range(16):
            vlv = p.op("gpsimd", "dma_start",
                       out=v12[par][:, jc * 65: jc * 65 + 64],
                       in_=vdram[s_ * 2048 + jc * 128:
                                 s_ * 2048 + (jc + 1) * 128,
                                 h * 64:(h + 1) * 64],
                       sem=sm, n=16)
        return (sm, vlv)

    for s in range(BPC):
        mhat_rows(p, hb, qn2d, kn2d, mdram, s, qnt, knt, kmax2, mnt,
                  extra=True, ex_scalar=bmax_r[:])
        m_done = p.val("sm")
        if L1A_STOP[0] == "pass1":
            continue

        pend = {0: emit_loads(0, s, 0, m_done)}
        for h in range(16):
            par = h % 2
            if h + 1 < 16:
                pend[1 - par] = emit_loads(1 - par, s, h + 1, m_done)
            khat, qhat, v1 = khat2[par], qhat2[par], v12[par]
            wide = wide2[par]
            smp, vlv = pend[par]
            first = [("g", gB_done)] if (s == 0 and h == 0) else []
            for ib in range(4):
                bk = itc[0] % 2
                avps = avbanks[bk]
                avkey = avkeys[bk]
                ptg = [p.guards["PT"]] if "PT" in p.guards else []
                ebvs = []
                for jc in range(16):
                    tag = "A" if jc % 2 == 0 else "B"
                    ps = hb.bank(tag)
                    tv = p.op("tensor", "matmul", ps,
                              khat[:, jc * 128:(jc + 1) * 128],
                              qhat[:, ib * 512:(ib + 1) * 512],
                              start=True, stop=True,
                              wait=[(smp, vlv)] + first + hb.fwait(tag) +
                              (ptg if jc == 0 else []),
                              sem="t")
                    av = p.op("scalar", "activation",
                              PT[:, jc * 512:(jc + 1) * 512], ps, AF.Exp,
                              wait=[("t", tv)], sem="a")
                    hb.free[tag] = ("a", av)
                    off = 1920 - jc * 128 + ib * 512
                    ebv = p.op("vector", "tensor_mul",
                               PT[:, jc * 512:(jc + 1) * 512],
                               PT[:, jc * 512:(jc + 1) * 512],
                               wide[:, off:off + 512],
                               wait=[("a", av)], sem="v")
                    ebvs.append(ebv)
                    if jc >= 3:
                        j2 = jc - 3
                        p.op("tensor", "matmul", avps,
                             v1[:, j2 * 65:(j2 + 1) * 65],
                             PT[:, j2 * 512:(j2 + 1) * 512],
                             start=(j2 == 0), stop=False,
                             wait=[("v", ebvs[j2])] +
                             (hb.fwait(avkey) if j2 == 0 else []))
                for j2 in (13, 14, 15):
                    p.op("tensor", "matmul", avps,
                         v1[:, j2 * 65:(j2 + 1) * 65],
                         PT[:, j2 * 512:(j2 + 1) * 512],
                         start=False, stop=(j2 == 15),
                         wait=[("v", ebvs[j2])],
                         sem="t" if j2 == 15 else None)
                tv = p.val("t")
                p.guards["PT"] = ("t", tv)
                tp_ = itc[0] % 2
                lw = ([p.guards["lrp%d" % tp_]]
                      if ("lrp%d" % tp_) in p.guards else [])
                lev = p.op("vector", "tensor_copy", lr2[tp_][64:65, :],
                           avps[64:65, :], wait=[("t", tv)] + lw, sem="v")
                yw = ([p.guards["ystp%d" % tp_]]
                      if ("ystp%d" % tp_) in p.guards else [])
                ysv = p.op("vector", "tensor_copy", yst2[tp_][:],
                           avps[0:64, :], wait=[("t", tv)] + yw, sem="v")
                hb.free[avkey] = ("v", ysv)
                if pend_tail[0] is not None:
                    pend_tail[0]()

                def mk_tail(tp_=tp_, lev=lev, ysv=ysv, ib=ib, h=h):
                    def tail():
                        sg = [p.guards["scr"]] if "scr" in p.guards else []
                        for j in range(4):
                            p.op("tensor", "transpose",
                                 hb.ps_scr[:, 16 + j:16 + j + 1],
                                 lr2[tp_][64:65, j * 128:(j + 1) * 128],
                                 consts["idf32"][64:65, 64:65],
                                 wait=[("v", lev)] + sg if j == 0 else (),
                                 sem="t" if j == 3 else None)
                        ltv = p.val("t")
                        p.guards["lrp%d" % tp_] = ("t", ltv)
                        ig = ([p.guards["invg"]]
                              if "invg" in p.guards else [])
                        inv_v = p.op("vector", "reciprocal", invl4[:],
                                     bass.AP(hb.ps_scr, 16,
                                             [[1024, 128], [1, 4]]),
                                     wait=[("t", ltv)] + ig, sem="v")
                        p.guards["scr"] = ("v", inv_v)
                        tw = [("v", ysv)] + hb.fwait("tbf")
                        for j in range(4):
                            p.op("tensor", "transpose",
                                 hb.ps_tbf[:, j * 64:(j + 1) * 64],
                                 yst2[tp_][:, j * 128:(j + 1) * 128],
                                 consts["idbf"][0:64, 0:64],
                                 wait=tw if j == 0 else (),
                                 sem="t" if j == 3 else None)
                        ytv = p.val("t")
                        p.guards["ystp%d" % tp_] = ("t", ytv)
                        for j in range(4):
                            p.op("vector", "tensor_scalar",
                                 cbf[:, ib * 4096 + j * 1024 + h * 64:
                                     ib * 4096 + j * 1024 + (h + 1) * 64],
                                 hb.ps_tbf[:, j * 64:(j + 1) * 64],
                                 invl4[:, j:j + 1], None, op0=ALU.mult,
                                 wait=([("t", ytv), ("v", inv_v)]
                                       if j == 0 else ()),
                                 sem="v" if j == 3 else None)
                        hb.free["tbf"] = ("v", p.val("v"))
                        p.guards["invg"] = ("v", p.val("v"))
                    return tail
                pend_tail[0] = mk_tail()
                itc[0] += 1
            p.guards["pbt%d" % par] = ("t", p.val("t"))
            p.guards["pbv%d" % par] = ("v", p.val("v"))
        if pend_tail[0] is not None:
            pend_tail[0]()
            pend_tail[0] = None
        heads_v = p.val("v")

        for ib in range(4):
            g = s * 4 + ib
            ldv = hb.group_load(g, xin)
            src_tiles = [cbf[:, ib * 4096 + j * 1024:
                             ib * 4096 + (j + 1) * 1024] for j in range(4)]

            def dview(j):
                return bass.AP(hb.xnT, j * 128,
                               [[8 * 512, 128], [512, 8], [1, 128]])
            cxv = transpose_to_fm(p, hb.ps_tbf, consts["idbf"], src_tiles,
                                  dview, 4,
                                  wait_src=[[("v", heads_v)] +
                                            hb.fwait("tbf")
                                            for j in range(4)])
            hb.free["tbf"] = ("v", cxv)
            for mc in range(8):
                tag = "A" if mc % 2 == 0 else "B"
                ps = hb.bank(tag)
                tv = mm_chain(p, ps,
                              [wo[:, kc * 1024 + mc * 128:
                                   kc * 1024 + (mc + 1) * 128]
                               for kc in range(8)],
                              [hb.xnT[:, kc * 512:(kc + 1) * 512]
                               for kc in range(8)],
                              [("v", cxv)] + hb.fwait(tag))
                av = p.op("scalar", "activation",
                          yfm[:, mc * 512:(mc + 1) * 512], ps, AF.Identity,
                          bias=bo[:, mc:mc + 1], wait=[("t", tv)], sem="a")
                hb.free[tag] = ("a", av)
            ya = p.val("a")
            hb.yfm = yfm
            stv = hb.epilogue(g, [("a", ya)], xout, eps5,
                              x_wait=[("ld", ldv)])
            p.guards["xng"] = ("st", stv[-1])
    if L1A_STOP[0] == "pass1":
        p.wait("sync", [("sm", p.val("sm"))])
        p.verify()
        return p.finalize()
    if fused is None:
        return finish(p)
    return None


def run_l1attn(x, w_qkv, b_qkv, w_out, b_out, w1_rpe, b1_rpe, w2_rpe, b2_rpe):
    return _launch("l1attn", build_l1attn, x,
                   {"w_qkv": w_qkv, "b_qkv": b_qkv, "w_out": w_out,
                    "b_out": b_out, "w1_rpe": w1_rpe, "b1_rpe": b1_rpe,
                    "w2_rpe": w2_rpe, "b2_rpe": b2_rpe})



WSPECS = [
    ("L0_attn_ln_scale", [1024]), ("L0_attn_ln_bias", [1024]),
    ("L0_attn_wq", [1024, 1024]), ("L0_attn_wk", [1024, 1024]),
    ("L0_attn_wv", [1024, 1024]),
    ("L0_ffn_weight", [1024, 1024]), ("L0_ffn_bias", [1024]),
    ("L1_attn_w_qkv", [1024, 3072]), ("L1_attn_b_qkv", [3072]),
    ("L1_attn_w_out", [1024, 1024]), ("L1_attn_b_out", [1024]),
    ("L1_attn_w1_rpe", [1, 32]), ("L1_attn_b1_rpe", [32]),
    ("L1_attn_w2_rpe", [32, 16]), ("L1_attn_b2_rpe", [16]),
    ("L1_ffn_w1", [4096, 1024]), ("L1_ffn_b1", [4096]),
    ("L1_ffn_w2", [1024, 4096]), ("L1_ffn_b2", [1024]),
    ("L2_attn_ln_scale", [1024]), ("L2_attn_ln_bias", [1024]),
    ("L2_attn_W_hidden", [1024, 4096]), ("L2_attn_b_hidden", [4096]),
    ("L2_attn_W_qk", [1024, 128]), ("L2_attn_b_qk", [128]),
    ("L2_attn_os_weight", [2, 128]), ("L2_attn_os_bias", [2, 128]),
    ("L2_attn_W_out", [2048, 1024]), ("L2_attn_b_out", [1024]),
    ("L2_ffn_U1", [1024, 256]), ("L2_ffn_V1", [256, 4096]),
    ("L2_ffn_b1", [4096]), ("L2_ffn_U2", [4096, 256]),
    ("L2_ffn_V2", [256, 1024]), ("L2_ffn_b2", [1024]),
    ("L3_attn_wq", [1024, 1024]), ("L3_attn_wk", [256, 1024]),
    ("L3_attn_wv", [256, 1024]), ("L3_attn_wo", [1024, 1024]),
    ("L3_ffn_Wg", [1024, 8]), ("L3_ffn_W1", [8, 1024, 1408]),
    ("L3_ffn_W2", [8, 1024, 1408]), ("L3_ffn_W3", [8, 1408, 1024]),
    ("L3_ffn_Ws1", [1024, 2816]), ("L3_ffn_Ws2", [1024, 2816]),
    ("L3_ffn_Ws3", [2816, 1024]),
]

FUSED_BLOCKS = [build_l0attn, build_l0ffn, build_l1attn, build_l1ffn,
                build_l2attn, build_l2ffn, build_l3attn, build_l3ffn]


def build_fused():
    import contextlib
    p = P("fused")
    nc = p.nc
    xin = nc.dram_tensor("xin", [T, D], F32, kind="ExternalInput").ap()
    xout = nc.dram_tensor("xout", [T, D], F32, kind="ExternalOutput").ap()
    W = {name: nc.dram_tensor(name, shape, F32, kind="ExternalInput").ap()
         for name, shape in WSPECS}
    xb = [p.dram("xbuf0", [T, D], F32).ap(),
          p.dram("xbuf1", [T, D], F32).ap()]
    consts = build_consts(p, f32=True, bf=True, eps=(1e-5, 1e-6))
    cur = xin
    for i, bfn in enumerate(FUSED_BLOCKS):
        dst = xout if i == len(FUSED_BLOCKS) - 1 else xb[i % 2]
        p.barrier()
        p.prefix = "b%d_" % i
        with contextlib.ExitStack() as st:
            p.stack = st
            bfn(fused=(p, consts, cur, dst, W))
            p.stack = None
        cur = dst
    p.prefix = ""
    return finish(p)


def kernel_fused(**inputs):
    nc = _get_prog("fused", build_fused)
    x = np.ascontiguousarray(inputs["L0_attn_x"], dtype=np.float32)
    xs = x.reshape(NCORES, T, D)
    in_maps = []
    for c in range(NCORES):
        m = {"xin": xs[c]}
        for name, _ in WSPECS:
            m[name] = inputs[name]
        in_maps.append(m)
    res = run_bass_kernel_spmd(nc, in_maps, core_ids=list(range(NCORES)))
    out = np.stack([r["xout"] for r in res.results])
    return np.ascontiguousarray(out.reshape(B, S, D), dtype=np.float32)


def kernel(**inputs):
    return kernel_fused(**inputs)


# version 10
# speedup vs baseline: 1.6497x; 1.0097x over previous
import contextlib
import numpy as np
import concourse.bass as bass
import concourse.mybir as mybir
from concourse.bass_utils import run_bass_kernel_spmd

F32 = mybir.dt.float32
BF16 = mybir.dt.bfloat16
AF = mybir.ActivationFunctionType
ALU = mybir.AluOpType
AX = mybir.AxisListType

NCORES = 8
B, S, D = 32, 2048, 1024
BPC = B // NCORES
T = BPC * S
NT = T // 128
NG = T // 512
MAGIC = 12582912.0

ENGINES = ("sync", "vector", "scalar", "tensor", "gpsimd")


class P:

    def __init__(self, name="prog"):
        self.nc = bass.Bass("TRN2", target_bir_lowering=False)
        self.streams = {k: [] for k in ENGINES}
        self.sems = {}
        self.cnt = {}
        self.trace = {k: [] for k in ENGINES}
        self.guards = {}
        self.prefix = ""
        self.stack = None

    def barrier(self):
        waits = [(s, v) for (s, v) in self.cnt.items() if v > 0]
        for eng in ENGINES:
            self.wait(eng, waits)
        self.guards.clear()

    def sem(self, name):
        if name not in self.sems:
            self.sems[name] = self.nc.alloc_semaphore(name)
            self.cnt[name] = 0
        return self.sems[name]

    def val(self, name):
        self.sem(name)
        return self.cnt[name]

    def op(self, eng, method, *args, wait=(), sem=None, n=1, **kw):
        target = None
        if sem is not None:
            self.sem(sem)
            self.cnt[sem] += n
            target = self.cnt[sem]
        wait = tuple(wait)
        semh = self.sems[sem] if sem is not None else None

        def fn(e, method=method, args=args, kw=kw, wait=wait, semh=semh, n=n):
            for (s, v) in wait:
                e.wait_ge(self.sems[s], v)
            inst = getattr(e, method)(*args, **kw)
            if semh is not None:
                inst.then_inc(semh, n)
        self.streams[eng].append(fn)
        self.trace[eng].append((wait, ((sem, n),) if sem else (), method))
        return target

    def wait(self, eng, wait):
        wait = tuple(wait)

        def fn(e, wait=wait):
            for (s, v) in wait:
                e.wait_ge(self.sems[s], v)
        self.streams[eng].append(fn)
        self.trace[eng].append((wait, (), "wait"))

    def verify(self):
        pos = {k: 0 for k in ENGINES}
        cur = {k: 0 for k in self.cnt}
        progress = True
        while progress:
            progress = False
            for eng in ENGINES:
                while pos[eng] < len(self.trace[eng]):
                    waits, incs, label = self.trace[eng][pos[eng]]
                    if all(cur.get(s, 0) >= v for (s, v) in waits):
                        for (s, n) in incs:
                            cur[s] = cur.get(s, 0) + n
                        pos[eng] += 1
                        progress = True
                    else:
                        break
        stuck = {k: (pos[k], len(self.trace[k])) for k in ENGINES
                 if pos[k] < len(self.trace[k])}
        if stuck:
            msg = []
            for k, (i, n) in stuck.items():
                waits, incs, label = self.trace[k][i]
                blocked = [(s, v, cur.get(s, 0)) for (s, v) in waits
                           if cur.get(s, 0) < v]
                msg.append(f"{k}@{i}/{n} op={label} blocked on {blocked}")
            raise RuntimeError("DEADLOCK:\n" + "\n".join(msg))

    def sbuf(self, name, shape, dtype, **kw):
        name = self.prefix + name
        if self.stack is not None:
            return self.stack.enter_context(
                self.nc.sbuf_tensor(name, shape, dtype, **kw))
        return self.nc.alloc_sbuf_tensor(name, shape, dtype, **kw)

    def psum(self, name, shape, dtype=F32):
        name = self.prefix + name
        if self.stack is not None:
            return self.stack.enter_context(
                self.nc.psum_tensor(name, shape, dtype))
        return self.nc.alloc_psum_tensor(name, shape, dtype)

    def dram(self, name, shape, dtype, kind="Internal"):
        return self.nc.dram_tensor(self.prefix + name, shape, dtype,
                                   kind=kind)

    def finalize(self):
        nc = self.nc
        with nc.Block() as block:
            @block.sync
            def _(e):
                for fn in self.streams["sync"]:
                    fn(e)

            @block.vector
            def _(e):
                for fn in self.streams["vector"]:
                    fn(e)

            @block.scalar
            def _(e):
                for fn in self.streams["scalar"]:
                    fn(e)

            @block.tensor
            def _(e):
                for fn in self.streams["tensor"]:
                    fn(e)

            @block.gpsimd
            def _(e):
                for fn in self.streams["gpsimd"]:
                    fn(e)
        return nc



def build_consts(p, f32=True, bf=True, eps=()):
    c = {}
    ones = p.sbuf("c_ones", [128, 128], F32)
    mv = p.op("gpsimd", "memset", ones[:], 1.0, sem="g")
    if f32:
        idf = p.sbuf("c_idf32", [128, 128], F32)
        p.op("gpsimd", "affine_select", idf[:], ones[:], [[-1, 128]],
             ALU.is_equal, 0.0, base=0, channel_multiplier=1,
             wait=[("g", mv)], sem="g")
        c["idf32"] = idf
    if bf:
        idb = p.sbuf("c_idbf", [128, 128], BF16)
        p.op("gpsimd", "affine_select", idb[:], ones[:], [[-1, 128]],
             ALU.is_equal, 0.0, base=0, channel_multiplier=1,
             wait=[("g", mv)], sem="g")
        c["idbf"] = idb
    for e in eps:
        t = p.sbuf(f"c_eps_{e:.0e}".replace("-", "m"), [128, 1], F32)
        p.op("gpsimd", "memset", t[:], float(e), sem="g")
        c[f"eps{e:.0e}".replace("-", "m")] = t
    c["ones"] = ones
    c["g_done"] = p.val("g")
    return c


def load_w_fm(p, name, dram_ap, kin, mout, out_dtype=BF16):
    nkc = kin // 128
    sb = p.sbuf(name, [128, nkc * mout], out_dtype)
    for kc in range(nkc):
        p.op("gpsimd", "dma_start", sem="gld", n=16,
             out=sb[:, kc * mout:(kc + 1) * mout],
             in_=dram_ap[kc * 128:(kc + 1) * 128, :])
    return sb, p.val("gld")


def load_bias_cols(p, name, dram_ap, nfeat, consts, ps_f32, wait_extra=()):
    nchunk = max(1, nfeat // 128)
    rows = p.sbuf(name + "_rows", [nchunk, 128], F32)
    sb = p.sbuf(name, [128, nchunk], F32)
    p.wait("sync", [("ld", p.val("ld"))])
    lv = p.op("sync", "dma_start", sem="ld", n=16, out=rows[:],
              in_=dram_ap.rearrange("(c p) -> c p", p=128))
    tv = p.op("tensor", "transpose", ps_f32[:, 0:nchunk], rows[:],
              consts["idf32"][0:nchunk, 0:nchunk],
              wait=[("ld", lv), ("g", consts["g_done"])] + list(wait_extra),
              sem="t")
    vv = p.op("vector", "tensor_copy", sb[:], ps_f32[:, 0:nchunk],
              wait=[("t", tv)], sem="v")
    return sb, vv


class LNState:
    def __init__(self, p, tag):
        self.sum4 = p.sbuf(f"{tag}_sum4", [128, 4], F32)
        self.sq4 = p.sbuf(f"{tag}_sq4", [128, 4], F32)
        self.mean4 = p.sbuf(f"{tag}_mean4", [128, 4], F32)
        self.m2 = p.sbuf(f"{tag}_m2", [128, 4], F32)
        self.var4 = p.sbuf(f"{tag}_var4", [128, 4], F32)
        self.sd4 = p.sbuf(f"{tag}_sd4", [128, 4], F32)
        self.rstd4 = p.sbuf(f"{tag}_rstd4", [128, 4], F32)


def ln_stats(p, st, scr, x_tiles, eps_ap, wait_x, gate=(), nd=1024):
    nj = len(x_tiles)
    for j, xt in enumerate(x_tiles):
        p.op("vector", "reduce_sum", st.sum4[:, j:j + 1], xt, axis=AX.X,
             wait=tuple(wait_x[j]) + tuple(gate), sem="v")
        sg = [p.guards["scr"]] if "scr" in p.guards else []
        av = p.op("scalar", "activation", scr[:, 0:nd], xt, AF.Square,
                  wait=tuple(wait_x[j]) + tuple(gate) + tuple(sg), sem="a",
                  accum_out=st.sq4[:, j:j + 1])
        p.guards["scr"] = ("a", av)
    v_sums = p.val("v")
    a_sq = p.val("a")
    p.op("vector", "tensor_scalar_mul", st.mean4[:, 0:nj], st.sum4[:, 0:nj],
         1.0 / nd, wait=[("v", v_sums), ("a", a_sq)], sem="v")
    vm = p.val("v")
    p.op("vector", "tensor_mul", st.m2[:, 0:nj], st.mean4[:, 0:nj],
         st.mean4[:, 0:nj], wait=[("v", vm)], sem="v")
    vm2 = p.val("v")
    p.op("vector", "scalar_tensor_tensor", st.var4[:, 0:nj], st.sq4[:, 0:nj],
         1.0 / nd, st.m2[:, 0:nj], op0=ALU.mult, op1=ALU.subtract,
         wait=[("v", vm2)], sem="v")
    vv = p.val("v")
    p.op("scalar", "activation", st.sd4[:, 0:nj], st.var4[:, 0:nj], AF.Sqrt,
         bias=eps_ap[:], wait=[("v", vv)], sem="a")
    va = p.val("a")
    p.op("vector", "reciprocal", st.rstd4[:, 0:nj], st.sd4[:, 0:nj],
         wait=[("a", va)], sem="v")
    return p.val("v")


def transpose_to_fm(p, ps_tbf, id_bf, src_bf_tiles, dst_view_of, nj,
                    wait_src):
    ev = None
    for j in range(nj):
        tw = list(wait_src[j])
        if ev is not None:
            tw.append(("v", ev))
        for c in range(8):
            p.op("tensor", "transpose",
                 ps_tbf[:, c * 128:(c + 1) * 128],
                 src_bf_tiles[j][:, c * 128:(c + 1) * 128],
                 id_bf[:],
                 wait=tuple(tw) if c == 0 else (),
                 sem="t" if c == 7 else None)
        tv = p.val("t")
        pin = ps_tbf[:].rearrange("p (c m) -> p c m", c=8)
        ev = p.op("vector", "tensor_copy", dst_view_of(j), pin,
                  wait=[("t", tv)], sem="v")
    return ev


def mm_chain(p, psum_ap, lhsT_list, rhs_list, wait_first):
    n = len(lhsT_list)
    for i in range(n):
        p.op("tensor", "matmul", psum_ap, lhsT_list[i], rhs_list[i],
             start=(i == 0), stop=(i == n - 1),
             wait=wait_first if i == 0 else (),
             sem="t" if i == n - 1 else None)
    return p.val("t")



class HBCommon:

    def __init__(self, p, consts, h_cols=0, yfm_cols=8 * 512,
                 banks=None, ep_ap=None):
        self.p = p
        self.consts = consts
        self.xg = p.sbuf("hb_xg", [128, 4 * 1024], F32)
        self.xnb = p.sbuf("hb_xnb", [128, 4 * 1024], BF16)
        self.xnT = p.sbuf("hb_xnT", [128, 8 * 512], BF16)
        self.yfm = (p.sbuf("hb_yfm", [128, yfm_cols], F32)
                    if yfm_cols else None)
        self.ytm = p.sbuf("hb_ytm", [128, 4 * 1024], F32)
        self.h = p.sbuf("hb_h", [128, h_cols], BF16) if h_cols else None
        self.st_in = LNState(p, "hbi")
        self.st_ep = LNState(p, "hbe")
        self.ps_tbf = p.psum("ps_tbf", [128, 1024], BF16)
        self.ps_scr = p.psum("ps_scr", [128, 1024], F32)
        if banks is None:
            psA = p.psum("psA", [128, 512], F32)
            psB = p.psum("psB", [128, 512], F32)
            ps_ep = p.psum("ps_ep", [128, 1024], F32)
            self.ep_ap = ps_ep[:]
            self._banks = {"A": psA[:], "B": psB[:],
                           "ep0": ps_ep[:, 0:512], "ep1": ps_ep[:, 512:1024]}
        else:
            self._banks = banks
            self.ep_ap = ep_ap
        self.free = {"A": None, "B": None, "ep0": None, "ep1": None,
                     "tbf": None}
        self.g_gate = []

    def bank(self, tag):
        return self._banks[tag]

    def fwait(self, *tags):
        return [self.free[t] for t in tags if self.free[t] is not None]

    def group_load(self, g, xin):
        p = self.p
        src = xin[g * 512:(g + 1) * 512, :].rearrange("(j p) d -> p j d", p=128)
        dst = self.xg[:].rearrange("p (j d) -> p j d", j=4)
        ldv = p.op("sync", "dma_start", out=dst, in_=src, sem="ld", n=16,
                   wait=self.g_gate)
        self.xt = [self.xg[:, j * 1024:(j + 1) * 1024] for j in range(4)]
        return ldv

    def sweep(self, g, ldv, eps_ap):
        p = self.p
        c = self.consts
        sv = ln_stats(p, self.st_in, self.ps_scr[:], self.xt, eps_ap,
                      [[("ld", ldv), ("g", c["g_done"])]] * 4)
        for j in range(4):
            p.op("vector", "tensor_scalar",
                 self.xnb[:, j * 1024:(j + 1) * 1024], self.xt[j],
                 self.st_in.mean4[:, j:j + 1], self.st_in.rstd4[:, j:j + 1],
                 op0=ALU.subtract, op1=ALU.mult,
                 wait=[("v", sv)] if j == 0 else (), sem="v")
        nv = p.val("v")
        src_tiles = [self.xnb[:, j * 1024:(j + 1) * 1024] for j in range(4)]

        def dview(j):
            return bass.AP(self.xnT, j * 128,
                           [[8 * 512, 128], [512, 8], [1, 128]])
        ev = transpose_to_fm(
            p, self.ps_tbf, c["idbf"], src_tiles, dview, 4,
            wait_src=[[("v", nv)] + self.fwait("tbf")] * 4)
        self.free["tbf"] = ("v", ev)
        return ev

    def stage(self, dst_sb, src_sb, w_sb, nkc, nmc, func, bias_sb, ready,
              kstride=512, mtot=None):
        p = self.p
        mtot = mtot if mtot is not None else nmc * 128
        for mc in range(nmc):
            tag = "A" if mc % 2 == 0 else "B"
            ps = self.bank(tag)
            w0 = list(ready) + self.fwait(tag)
            tv = mm_chain(
                p, ps,
                [w_sb[:, kc * mtot + mc * 128: kc * mtot + (mc + 1) * 128]
                 for kc in range(nkc)],
                [src_sb[:, kc * kstride:(kc + 1) * kstride]
                 for kc in range(nkc)],
                w0)
            kw = {}
            if bias_sb is not None:
                kw["bias"] = bias_sb[:, mc:mc + 1]
            av = p.op("scalar", "activation",
                      dst_sb[:, mc * 512:(mc + 1) * 512], ps, func,
                      wait=[("t", tv)], sem="a", **kw)
            self.free[tag] = ("a", av)
        return p.val("a")

    def epilogue(self, g, y_ready, xout, eps_ap, extra_add=None, x_wait=()):
        p = self.p
        c = self.consts
        ep_evs = []
        for j in range(4):
            tw = list(y_ready) + self.fwait("ep0", "ep1")
            for cc in range(8):
                p.op("tensor", "transpose",
                     self.ep_ap[:, cc * 128:(cc + 1) * 128],
                     self.yfm[:, cc * 512 + j * 128: cc * 512 + (j + 1) * 128],
                     c["idf32"][:],
                     wait=tw if cc == 0 else (),
                     sem="t" if cc == 7 else None)
            tv = p.val("t")
            ev = p.op("vector", "tensor_copy",
                      self.ytm[:, j * 1024:(j + 1) * 1024],
                      self.ep_ap[:], wait=[("t", tv)], sem="v")
            self.free["ep0"] = self.free["ep1"] = ("v", ev)
            ep_evs.append(ev)

        yt = [self.ytm[:, j * 1024:(j + 1) * 1024] for j in range(4)]
        if extra_add is not None:
            for j in range(4):
                ap, waits = extra_add(j)
                ev = p.op("vector", "tensor_add", yt[j], yt[j], ap,
                          wait=[("v", ep_evs[j])] + list(waits), sem="v")
                ep_evs[j] = ev
        sv2 = ln_stats(p, self.st_ep, self.ps_scr[:], yt, eps_ap,
                       [[("v", ep_evs[j])] for j in range(4)])
        st_vals = []
        for j in range(4):
            p.op("vector", "tensor_scalar", yt[j], yt[j],
                 self.st_ep.mean4[:, j:j + 1], self.st_ep.rstd4[:, j:j + 1],
                 op0=ALU.subtract, op1=ALU.mult,
                 wait=[("v", sv2)] if j == 0 else (), sem="v")
            uv = p.val("v")
            addv = p.op("vector", "tensor_add", self.xt[j], self.xt[j], yt[j],
                        wait=[("v", uv)] + list(x_wait), sem="v")
            stv = p.op("sync", "dma_start",
                       out=xout[g * 512 + j * 128: g * 512 + (j + 1) * 128, :],
                       in_=self.xt[j], wait=[("v", addv)], sem="st", n=16)
            st_vals.append(stv)
        self.g_gate = [("st", st_vals[-1])]
        return st_vals


def finish(p):
    p.wait("sync", [("st", p.val("st"))])
    p.verify()
    return p.finalize()



def build_l2ffn(fused=None):
    if fused is None:
        p = P("l2ffn")
        nc = p.nc
        xin = nc.dram_tensor("xin", [T, D], F32, kind="ExternalInput").ap()
        xout = nc.dram_tensor("xout", [T, D], F32, kind="ExternalOutput").ap()
        W = {k: nc.dram_tensor(k2, sh, F32, kind="ExternalInput").ap()
             for k, k2, sh in (
                 ("L2_ffn_U1", "U1", [1024, 256]),
                 ("L2_ffn_V1", "V1", [256, 4096]),
                 ("L2_ffn_b1", "b1", [4096]),
                 ("L2_ffn_U2", "U2", [4096, 256]),
                 ("L2_ffn_V2", "V2", [256, 1024]),
                 ("L2_ffn_b2", "b2", [1024]))}
        consts = build_consts(p, f32=True, bf=True, eps=(1e-5, 1e-6))
    else:
        p, consts, xin, xout, W = fused
        nc = p.nc
    u1d = W["L2_ffn_U1"]; v1d = W["L2_ffn_V1"]; b1d = W["L2_ffn_b1"]
    u2d = W["L2_ffn_U2"]; v2d = W["L2_ffn_V2"]; b2d = W["L2_ffn_b2"]
    hb = HBCommon(p, consts, h_cols=32 * 512)
    eps5 = consts["eps1em05"]

    u1, _ = load_w_fm(p, "u1", u1d, 1024, 256)
    v1, _ = load_w_fm(p, "v1", v1d, 256, 4096)
    u2, _ = load_w_fm(p, "u2", u2d, 4096, 256)
    v2, _ = load_w_fm(p, "v2", v2d, 256, 1024)
    wv = p.val("gld")
    b1sb, b1v = load_bias_cols(p, "b1sb", b1d, 4096, consts, hb.ps_scr[:])
    b2sb, b2v = load_bias_cols(p, "b2sb", b2d, 1024, consts, hb.ps_scr[:],
                               wait_extra=[("v", b1v)])

    p1sb = p.sbuf("p1sb", [128, 2 * 512], BF16)
    qsb = p.sbuf("qsb", [128, 2 * 512], BF16)
    hb.g_gate = [("v", b2v), ("gld", wv)]

    for g in range(NG):
        ldv = hb.group_load(g, xin)
        xv = hb.sweep(g, ldv, eps5)
        p1a = hb.stage(p1sb, hb.xnT, u1, 8, 2, AF.Copy, None, [("v", xv)],
                       mtot=256)
        ha = hb.stage(hb.h, p1sb, v1, 2, 32, AF.Gelu_apprx_tanh, b1sb,
                      [("a", p1a)], mtot=4096)
        qa = hb.stage(qsb, hb.h, u2, 32, 2, AF.Copy, None, [("a", ha)],
                      mtot=256)
        ya = hb.stage(hb.yfm, qsb, v2, 2, 8, AF.Identity, b2sb, [("a", qa)],
                      mtot=1024)
        hb.epilogue(g, [("a", ya)], xout, eps5)
    if fused is None:
        return finish(p)
    return None



def prep_binarized_T(p, tag, wdram, nrows, ncols, consts, stage_f32, sgn_bf,
                     alpha, ps_tbf, dst_sb=None, dst_sb_width=0,
                     dst_dram=None, stg_dma=None, init_gate=()):
    nrc = nrows // 128
    ndc = ncols // 128
    prev_done = None
    last = None
    for rc in range(nrc):
        w0 = [("g", consts["g_done"])] + list(init_gate)
        if prev_done is not None:
            w0.append(prev_done)
        ldv = p.op("sync", "dma_start", out=stage_f32[:, 0:ncols],
                   in_=wdram[rc * 128:(rc + 1) * 128, :], sem="ld", n=16,
                   wait=w0)
        rv = p.op("vector", "tensor_reduce", alpha[:], stage_f32[:, 0:ncols],
                  axis=AX.X, op=ALU.add, apply_absolute_value=True,
                  wait=[("ld", ldv)], sem="v")
        av = p.op("scalar", "sign", sgn_bf[:, 0:ncols], stage_f32[:, 0:ncols],
                  wait=[("ld", ldv)], sem="a")
        bv = p.op("vector", "tensor_scalar", sgn_bf[:, 0:ncols],
                  sgn_bf[:, 0:ncols], alpha[:], 1.0 / ncols,
                  op0=ALU.mult, op1=ALU.mult,
                  wait=[("v", rv), ("a", av)], sem="v")
        ev = None
        for base in range(0, ndc, 8):
            nb = min(8, ndc - base)
            tw = [("v", bv)]
            if ev is not None:
                tw.append(ev)
            for i in range(nb):
                p.op("tensor", "transpose",
                     ps_tbf[:, i * 128:(i + 1) * 128],
                     sgn_bf[:, (base + i) * 128:(base + i + 1) * 128],
                     consts["idbf"][:],
                     wait=tw if i == 0 else (),
                     sem="t" if i == nb - 1 else None)
            tv = p.val("t")
            pin = bass.AP(ps_tbf, 0, [[1024, 128], [128, nb], [1, 128]])
            if dst_sb is not None:
                dv = bass.AP(dst_sb, base * nrows + rc * 128,
                             [[dst_sb_width, 128], [nrows, nb], [1, 128]])
                vv = p.op("vector", "tensor_copy", dv, pin,
                          wait=[("t", tv)], sem="v")
                ev = ("v", vv)
            else:
                vv = p.op("vector", "tensor_copy", stg_dma[:, 0:nb * 128], pin,
                          wait=[("t", tv)], sem="v")
                dd = bass.AP(dst_dram.tensor, base * 128 * nrows + rc * 128,
                             [[nrows, 128], [128 * nrows, nb], [1, 128]])
                sv = p.op("sync", "dma_start", out=dd,
                          in_=bass.AP(stg_dma, 0,
                                      [[1024, 128], [128, nb], [1, 128]]),
                          wait=[("v", vv)], sem="st", n=16)
                ev = ("st", sv)
            last = ev
        prev_done = last
    return last


def build_l1ffn(fused=None):
    if fused is None:
        p = P("l1ffn")
        nc = p.nc
        xin = nc.dram_tensor("xin", [T, D], F32, kind="ExternalInput").ap()
        xout = nc.dram_tensor("xout", [T, D], F32, kind="ExternalOutput").ap()
        W = {k: nc.dram_tensor(k2, sh, F32, kind="ExternalInput").ap()
             for k, k2, sh in (
                 ("L1_ffn_w1", "w1", [4096, 1024]),
                 ("L1_ffn_b1", "b1", [4096]),
                 ("L1_ffn_w2", "w2", [1024, 4096]),
                 ("L1_ffn_b2", "b2", [1024]))}
        consts = build_consts(p, f32=True, bf=True, eps=(1e-5, 1e-6))
    else:
        p, consts, xin, xout, W = fused
        nc = p.nc
    w1d = W["L1_ffn_w1"]; b1d = W["L1_ffn_b1"]
    w2d = W["L1_ffn_w2"]; b2d = W["L1_ffn_b2"]
    w2T_dram = p.dram("w2T", [32, 128, 1024], BF16).ap()
    hb = HBCommon(p, consts, h_cols=32 * 512)
    eps5 = consts["eps1em05"]

    w1T = p.sbuf("w1T", [128, 8 * 4096], BF16)
    alpha = p.sbuf("alpha", [128, 1], F32)
    stg_dma = p.sbuf("stg_dma", [128, 1024], BF16)
    r_w1 = prep_binarized_T(p, "w1", w1d, 4096, 1024, consts, hb.xg,
                            hb.xnb, alpha, hb.ps_tbf,
                            dst_sb=w1T, dst_sb_width=8 * 4096)
    r_w2 = prep_binarized_T(p, "w2", w2d, 1024, 4096, consts, hb.xg,
                            hb.xnb, alpha, hb.ps_tbf,
                            dst_dram=w2T_dram, stg_dma=stg_dma,
                            init_gate=[r_w1])
    b1sb, b1v = load_bias_cols(p, "b1sb", b1d, 4096, consts, hb.ps_scr[:])
    b2sb, b2v = load_bias_cols(p, "b2sb", b2d, 1024, consts, hb.ps_scr[:],
                               wait_extra=[("v", b1v)])
    hb.g_gate = [r_w1, r_w2, ("v", b2v)]
    hb.free["tbf"] = r_w1

    w2buf = [p.sbuf(f"w2buf{i}", [128, 1024], BF16) for i in range(2)]
    w2v = [None, None]
    w2cons = [None, None]

    for g in range(NG):
        ldv = hb.group_load(g, xin)
        xv = hb.sweep(g, ldv, eps5)
        ha = hb.stage(hb.h, hb.xnT, w1T, 8, 32, AF.Gelu, b1sb, [("v", xv)],
                      mtot=4096)
        for mset in range(2):
            tags = ["A", "B", "ep0", "ep1"]
            bank_free = hb.fwait(*tags)
            for kc in range(32):
                bi = kc % 2
                w0 = [r_w2]
                if w2cons[bi] is not None:
                    w0.append(("t", w2cons[bi]))
                w2v[bi] = p.op("gpsimd", "dma_start", out=w2buf[bi][:, 0:512],
                               in_=bass.AP(w2T_dram.tensor,
                                           kc * 128 * 1024 + mset * 512,
                                           [[1024, 128], [1, 512]]),
                               wait=w0, sem=f"gw{bi}", n=16)
                for i in range(4):
                    mc = mset * 4 + i
                    w1_ = []
                    if i == 0:
                        w1_ = [(f"gw{bi}", w2v[bi]), ("a", ha)]
                        if kc == 0:
                            w1_ += bank_free
                    p.op("tensor", "matmul", hb.bank(tags[i]),
                         w2buf[bi][:, i * 128:(i + 1) * 128],
                         hb.h[:, kc * 512:(kc + 1) * 512],
                         start=(kc == 0), stop=(kc == 31),
                         wait=w1_, sem="t" if i == 3 else None)
                w2cons[bi] = p.val("t")
            tv = p.val("t")
            for i in range(4):
                mc = mset * 4 + i
                av = p.op("scalar", "activation",
                          hb.yfm[:, mc * 512:(mc + 1) * 512],
                          hb.bank(tags[i]), AF.Identity,
                          bias=b2sb[:, mc:mc + 1],
                          wait=[("t", tv)], sem="a")
                hb.free[tags[i]] = ("a", av)
        ya = p.val("a")
        hb.epilogue(g, [("a", ya)], xout, eps5)
    if fused is None:
        return finish(p)
    return None



_prog_cache = {}


def _get_prog(name, builder):
    if name not in _prog_cache:
        _prog_cache[name] = builder()
    return _prog_cache[name]


def _launch(name, builder, x, weights):
    nc = _get_prog(name, builder)
    xs = np.ascontiguousarray(x.reshape(NCORES, T, D))
    in_maps = [{"xin": xs[c], **weights} for c in range(NCORES)]
    res = run_bass_kernel_spmd(nc, in_maps, core_ids=list(range(NCORES)))
    out = np.stack([r["xout"] for r in res.results])
    return out.reshape(B, S, D)


def run_l2ffn(x, U1, V1, b1, U2, V2, b2):
    return _launch("l2ffn", build_l2ffn, x,
                   {"U1": U1, "V1": V1, "b1": b1, "U2": U2, "V2": V2,
                    "b2": b2})


def run_l1ffn(x, w1, b1, w2, b2):
    return _launch("l1ffn", build_l1ffn, x,
                   {"w1": w1, "b1": b1, "w2": w2, "b2": b2})



def mx_quant(p, dst_bf, x_ap, q, wait0, W=1024):
    A = ALU
    nb = W // 32
    xv = x_ap.rearrange("p (b i) -> p b i", i=32)
    s_ = q["s"][:, 0:nb]
    si_ = q["si"][:, 0:nb]
    t1_ = q["qt1"][:, 0:W]
    t2_ = q["qt2"][:, 0:W]
    t3_ = q["qt3"][:, 0:W]
    t4_ = q["qt4"][:, 0:W]
    rv = p.op("vector", "tensor_reduce", s_, xv, axis=AX.X, op=A.max,
              apply_absolute_value=True, wait=wait0, sem="v")
    sv = p.op("vector", "tensor_scalar", s_, s_, 1.0 / 7.5,
              1e-30, op0=A.mult, op1=A.max, wait=[("v", rv)], sem="v")
    sbc = s_.broadcast_to([128, nb, 32])
    rv2 = p.op("vector", "reciprocal", si_, s_,
               wait=[("v", sv)], sem="v")
    sibc = si_.broadcast_to([128, nb, 32])
    yv = p.op("vector", "tensor_tensor",
              t1_.rearrange("p (b i) -> p b i", i=32), xv, sibc,
              op=A.mult, wait=[("v", rv2)], sem="v")
    av = p.op("scalar", "activation", t2_, t1_, AF.Abs,
              wait=[("v", yv)], sem="a")
    v2 = p.op("vector", "tensor_single_scalar", t3_, t2_, 2.0,
              op=A.is_ge, wait=[("a", av)], sem="v")
    v4 = p.op("vector", "tensor_single_scalar", t4_, t2_, 4.0,
              op=A.is_ge, wait=[("v", v2)], sem="v")
    m1 = p.op("vector", "tensor_scalar", t2_, t3_, -4.0, 8.0,
              op0=A.mult, op1=A.add, wait=[("v", v4)], sem="v")
    mv = p.op("vector", "scalar_tensor_tensor", t2_, t4_,
              -2.0, t2_, op0=A.mult, op1=A.add, wait=[("v", m1)],
              sem="v")
    t1 = p.op("vector", "tensor_mul", t1_, t1_, t2_,
              wait=[("v", mv)], sem="v")
    t2v = p.op("vector", "tensor_scalar", t1_, t1_, MAGIC,
               -MAGIC, op0=A.add, op1=A.add, wait=[("v", t1)], sem="v")
    i1 = p.op("vector", "tensor_scalar", t2_, t3_, 0.125,
              0.125, op0=A.mult, op1=A.add, wait=[("v", t2v)], sem="v")
    i2 = p.op("vector", "scalar_tensor_tensor", t2_, t4_,
              0.25, t2_, op0=A.mult, op1=A.add, wait=[("v", i1)],
              sem="v")
    g1 = p.op("vector", "tensor_mul", t1_, t1_, t2_,
              wait=[("v", i2)], sem="v")
    dv = p.op("vector", "tensor_tensor",
              dst_bf.rearrange("p (b i) -> p b i", i=32),
              t1_.rearrange("p (b i) -> p b i", i=32), sbc,
              op=A.mult, wait=[("v", g1)], sem="v")
    return dv


def build_l0ffn(fused=None):
    if fused is None:
        p = P("l0ffn")
        nc = p.nc
        xin = nc.dram_tensor("xin", [T, D], F32, kind="ExternalInput").ap()
        xout = nc.dram_tensor("xout", [T, D], F32, kind="ExternalOutput").ap()
        W = {k: nc.dram_tensor(k2, sh, F32, kind="ExternalInput").ap()
             for k, k2, sh in (
                 ("L0_ffn_weight", "weight", [1024, 1024]),
                 ("L0_ffn_bias", "bias", [1024]))}
        consts = build_consts(p, f32=True, bf=True, eps=(1e-5, 1e-6))
    else:
        p, consts, xin, xout, W = fused
        nc = p.nc
    wd = W["L0_ffn_weight"]; bd = W["L0_ffn_bias"]
    hb = HBCommon(p, consts, h_cols=0)
    eps5 = consts["eps1em05"]

    q = {
        "qt1": p.sbuf("qt1", [128, 2048], F32),
        "qt2": p.sbuf("qt2", [128, 2048], F32),
        "qt3": p.sbuf("qt3", [128, 2048], F32),
        "qt4": p.sbuf("qt4", [128, 2048], F32),
        "s": p.sbuf("qs", [128, 64], F32),
        "si": p.sbuf("qsi", [128, 64], F32),
    }
    xnf = p.sbuf("xnf", [128, 2048], F32)
    wdT = p.sbuf("wdT", [128, 8 * 1024], BF16)

    r_w = None
    for rc in range(8):
        w0 = [("g", consts["g_done"])]
        if r_w is not None:
            w0.append(r_w)
        ldv = p.op("sync", "dma_start", out=hb.xg[:, 0:1024],
                   in_=wd[rc * 128:(rc + 1) * 128, :], sem="ld", n=16,
                   wait=w0)
        qv = mx_quant(p, hb.xnb[:, 0:1024], hb.xg[:, 0:1024], q,
                      [("ld", ldv)])
        tw = [("v", qv)]
        if r_w is not None:
            tw.append(r_w)
        for i in range(8):
            p.op("tensor", "transpose",
                 hb.ps_tbf[:, i * 128:(i + 1) * 128],
                 hb.xnb[:, i * 128:(i + 1) * 128],
                 consts["idbf"][:],
                 wait=tw if i == 0 else (), sem="t" if i == 7 else None)
        tv = p.val("t")
        pin = bass.AP(hb.ps_tbf, 0, [[1024, 128], [128, 8], [1, 128]])
        dv = bass.AP(wdT, rc * 128, [[8 * 1024, 128], [1024, 8], [1, 128]])
        vv = p.op("vector", "tensor_copy", dv, pin, wait=[("t", tv)], sem="v")
        r_w = ("v", vv)
    hb.free["tbf"] = r_w

    p.wait("sync", [("ld", p.val("ld"))])
    bsb, bv = load_bias_cols(p, "bsb", bd, 1024, consts, hb.ps_scr[:])

    xgB = p.sbuf("xgB", [128, 4 * 1024], F32)
    xgs = [hb.xg, xgB]
    xt_of = {}
    ld_of = {}
    st_hist = {}
    init_gate = [r_w, ("v", bv)]

    def do_load(g):
        hb.xg = xgs[g % 2]
        if g == 0:
            gate = list(init_gate)
        elif (g - 1) in st_hist:
            gate = [("st", st_hist[g - 1])]
        else:
            gate = []
        src = xin[g * 512:(g + 1) * 512, :].rearrange("(j p) d -> p j d",
                                                      p=128)
        dst = hb.xg[:].rearrange("p (j d) -> p j d", j=4)
        ld_of[g] = p.op("sync", "dma_start", out=dst, in_=src,
                        sem=f"ld{g % 2}", n=16, wait=gate)
        xt_of[g] = [hb.xg[:, j * 1024:(j + 1) * 1024] for j in range(4)]

    def do_sweep(g):
        hb.xt = xt_of[g]
        sv = ln_stats(p, hb.st_in, hb.ps_scr[:], hb.xt, eps5,
                      [[(f"ld{g % 2}", ld_of[g]),
                        ("g", consts["g_done"])]] * 4)
        qvs = []
        for jj in range(2):
            for j2 in (0, 1):
                j = jj * 2 + j2
                nv = p.op("vector", "tensor_scalar",
                          xnf[:, j2 * 1024:(j2 + 1) * 1024], hb.xt[j],
                          hb.st_in.mean4[:, j:j + 1],
                          hb.st_in.rstd4[:, j:j + 1],
                          op0=ALU.subtract, op1=ALU.mult,
                          wait=[("v", sv)] if j == 0 else (), sem="v")
            qv = mx_quant(p, hb.xnb[:, jj * 2048:(jj + 1) * 2048], xnf[:], q,
                          [("v", nv)], W=2048)
            qvs.extend([qv, qv])
        src_tiles = [hb.xnb[:, j * 1024:(j + 1) * 1024] for j in range(4)]

        def dview(j):
            return bass.AP(hb.xnT, j * 128,
                           [[8 * 512, 128], [512, 8], [1, 128]])
        xv = transpose_to_fm(p, hb.ps_tbf, consts["idbf"], src_tiles, dview,
                             4, wait_src=[[("v", qvs[j])] + hb.fwait("tbf")
                                          for j in range(4)])
        hb.free["tbf"] = ("v", xv)
        return xv

    do_load(0)
    xv = do_sweep(0)
    for g in range(NG):
        if g + 1 < NG:
            do_load(g + 1)
        hb.xt = xt_of[g]
        ya = hb.stage(hb.yfm, hb.xnT, wdT, 8, 8, AF.Identity, bsb,
                      [("v", xv)], mtot=1024)
        if g + 1 < NG:
            xv = do_sweep(g + 1)
        hb.xt = xt_of[g]
        sts = hb.epilogue(g, [("a", ya)], xout, eps5)
        st_hist[g] = sts[-1]
    if fused is None:
        return finish(p)
    return None


def run_l0ffn(x, weight, bias):
    return _launch("l0ffn", build_l0ffn, x, {"weight": weight, "bias": bias})



def bcast_row(p, name, dram_vec, consts, ps2, n=1024, wait_ps=()):
    row = p.sbuf(name + "_row", [1, n], F32)
    dst = p.sbuf(name, [128, n], F32)
    p.wait("sync", [("ld", p.val("ld"))])
    lv = p.op("sync", "dma_start", sem="ld", n=16, out=row[:],
              in_=dram_vec.rearrange("(a n) -> a n", a=1))
    last_v = None
    for i in range(0, n, 512):
        w = min(512, n - i)
        tv = p.op("tensor", "matmul", ps2[:, i % 1024: i % 1024 + w],
                  consts["ones"][0:1, 0:128], row[0:1, i:i + w],
                  start=True, stop=True,
                  wait=[("ld", lv), ("g", consts["g_done"])] + list(wait_ps),
                  sem="t")
        last_v = p.op("vector", "tensor_copy", dst[:, i:i + w],
                      ps2[:, i % 1024: i % 1024 + w], wait=[("t", tv)],
                      sem="v")
    return dst, ("v", last_v)


def build_l2attn(fused=None):
    if fused is None:
        p = P("l2attn")
        nc = p.nc
        xin = nc.dram_tensor("xin", [T, D], F32, kind="ExternalInput").ap()
        xout = nc.dram_tensor("xout", [T, D], F32, kind="ExternalOutput").ap()
        W = {k: nc.dram_tensor(k2, sh, F32, kind="ExternalInput").ap()
             for k, k2, sh in (
                 ("L2_attn_ln_scale", "ln_scale", [1024]),
                 ("L2_attn_ln_bias", "ln_bias", [1024]),
                 ("L2_attn_W_hidden", "W_hidden", [1024, 4096]),
                 ("L2_attn_b_hidden", "b_hidden", [4096]),
                 ("L2_attn_W_qk", "W_qk", [1024, 128]),
                 ("L2_attn_b_qk", "b_qk", [128]),
                 ("L2_attn_os_weight", "os_weight", [2, 128]),
                 ("L2_attn_os_bias", "os_bias", [2, 128]),
                 ("L2_attn_W_out", "W_out", [2048, 1024]),
                 ("L2_attn_b_out", "b_out", [1024]))}
        consts = build_consts(p, f32=True, bf=True, eps=(1e-5, 1e-6))
    else:
        p, consts, xin, xout, W = fused
        nc = p.nc
    gmd = W["L2_attn_ln_scale"]; btd = W["L2_attn_ln_bias"]
    whd = W["L2_attn_W_hidden"]; bhd = W["L2_attn_b_hidden"]
    wqkd = W["L2_attn_W_qk"]; bqkd = W["L2_attn_b_qk"]
    oswd = W["L2_attn_os_weight"]; osbd = W["L2_attn_os_bias"]
    wod = W["L2_attn_W_out"]; bod = W["L2_attn_b_out"]

    vdram = p.dram("vtm", [T, 2048], BF16).ap()
    gdram = p.dram("gfm", [2048, T], BF16).ap()
    qdram = p.dram("qfm", [128, T], BF16).ap()
    kdram = p.dram("kfm", [128, T], BF16).ap()
    hb = HBCommon(p, consts, h_cols=0)
    eps5 = consts["eps1em05"]

    if p.stack is None:
        p.stack = contextlib.ExitStack()
    bo, bov = load_bias_cols(p, "bo", bod, 1024, consts, hb.ps_scr[:])
    wout, _ = load_w_fm(p, "wout", wod, 2048, 1024)
    vacc = p.sbuf("vacc", [128, 4 * 2048], BF16)

    wstack = contextlib.ExitStack()
    saved_stack = p.stack
    p.stack = wstack
    gbc, gv = bcast_row(p, "gbc", gmd, consts, hb.ps_scr[:],
                        wait_ps=[("v", bov)])
    bbc, bv_ = bcast_row(p, "bbc", btd, consts, hb.ps_scr[:], wait_ps=[gv])
    bh, bhv = load_bias_cols(p, "bh", bhd, 4096, consts, hb.ps_scr[:],
                             wait_extra=[bv_])
    bqk, bqkv = load_bias_cols(p, "bqk", bqkd, 128, consts, hb.ps_scr[:],
                               wait_extra=[("v", bhv)])
    osw0, v1 = load_bias_cols(p, "osw0", oswd[0, :], 128, consts,
                              hb.ps_scr[:], wait_extra=[("v", bqkv)])
    osw1, v2 = load_bias_cols(p, "osw1", oswd[1, :], 128, consts,
                              hb.ps_scr[:], wait_extra=[("v", v1)])
    osb0, v3 = load_bias_cols(p, "osb0", osbd[0, :], 128, consts,
                              hb.ps_scr[:], wait_extra=[("v", v2)])
    osb1, v4 = load_bias_cols(p, "osb1", osbd[1, :], 128, consts,
                              hb.ps_scr[:], wait_extra=[("v", v3)])
    wqk, _ = load_w_fm(p, "wqk", wqkd, 1024, 128)
    wv = p.val("gld")

    xnf = p.sbuf("xnf", [128, 1024], F32)
    xnf2 = p.sbuf("xnf2", [128, 1024], F32)
    st_in2 = LNState(p, "hbi2")
    whbuf = [p.sbuf(f"whbuf{i}", [128, 8 * 128], BF16) for i in range(2)]
    vtmp = p.sbuf("vtmp", [128, 512], BF16)
    gstg = p.sbuf("gstg", [128, 512], BF16)
    qks = p.sbuf("qks", [128, 512], F32)
    qkbf = p.sbuf("qkbf", [128, 512], BF16)
    p.stack = saved_stack

    hb.g_gate = [("v", v4), ("gld", wv)]

    whv = [None, None]
    whcons = [None, None]
    for g in range(NG):
        ldv = hb.group_load(g, xin)
        sv = ln_stats(p, hb.st_in, hb.ps_scr[:], hb.xt, eps5,
                      [[("ld", ldv), ("g", consts["g_done"])]] * 4)
        xn_vals = []
        for j in range(4):
            n1 = p.op("vector", "tensor_scalar", xnf[:], hb.xt[j],
                      hb.st_in.mean4[:, j:j + 1], hb.st_in.rstd4[:, j:j + 1],
                      op0=ALU.subtract, op1=ALU.mult,
                      wait=[("v", sv)] if j == 0 else (), sem="v")
            sv2 = ln_stats(p, st_in2, hb.ps_scr[:], [xnf[:]], eps5,
                           [[("v", n1)]])
            n2 = p.op("vector", "tensor_scalar", xnf2[:], xnf[:],
                      st_in2.mean4[:, 0:1], st_in2.rstd4[:, 0:1],
                      op0=ALU.subtract, op1=ALU.mult,
                      wait=[("v", sv2)], sem="v")
            n3 = p.op("vector", "tensor_mul", xnf2[:], xnf2[:], gbc[:],
                      wait=[("v", n2)], sem="v")
            n4 = p.op("vector", "tensor_add",
                      hb.xnb[:, j * 1024:(j + 1) * 1024], xnf2[:], bbc[:],
                      wait=[("v", n3)], sem="v")
            xn_vals.append(n4)
        src_tiles = [hb.xnb[:, j * 1024:(j + 1) * 1024] for j in range(4)]

        def dview(j):
            return bass.AP(hb.xnT, j * 128,
                           [[8 * 512, 128], [512, 8], [1, 128]])
        xv = transpose_to_fm(p, hb.ps_tbf, consts["idbf"], src_tiles, dview,
                             4, wait_src=[[("v", xn_vals[j])] + hb.fwait("tbf")
                                          for j in range(4)])
        hb.free["tbf"] = ("v", xv)

        vacc_evs = []
        for mc in range(32):
            bi = mc % 2
            w0 = []
            if whcons[bi] is not None:
                w0.append(("t", whcons[bi]))
            whv[bi] = p.op("gpsimd", "dma_start", out=whbuf[bi][:],
                           in_=bass.AP(whd.tensor, mc * 128,
                                       [[4096, 128], [128 * 4096, 8],
                                        [1, 128]]),
                           wait=w0, sem=f"wh{bi}", n=16)
            tag = "A" if mc % 2 == 0 else "B"
            ps = hb.bank(tag)
            w1_ = [(f"wh{bi}", whv[bi]), ("v", xv)] + hb.fwait(tag)
            tv = mm_chain(p, ps,
                          [whbuf[bi][:, kc * 128:(kc + 1) * 128]
                           for kc in range(8)],
                          [hb.xnT[:, kc * 512:(kc + 1) * 512]
                           for kc in range(8)], w1_)
            whcons[bi] = tv
            if mc < 16:
                av = p.op("scalar", "activation", vtmp[:], ps, AF.Silu,
                          bias=bh[:, mc:mc + 1], wait=[("t", tv)], sem="a")
                hb.free[tag] = ("a", av)
                tw = [("a", av)] + hb.fwait("tbf")
                for jj in range(4):
                    p.op("tensor", "transpose",
                         hb.ps_tbf[:, jj * 128:(jj + 1) * 128],
                         vtmp[:, jj * 128:(jj + 1) * 128],
                         consts["idbf"][:],
                         wait=tw if jj == 0 else (),
                         sem="t" if jj == 3 else None)
                ttv = p.val("t")
                dv = bass.AP(vacc, mc * 128,
                             [[4 * 2048, 128], [2048, 4], [1, 128]])
                pin = bass.AP(hb.ps_tbf, 0, [[1024, 128], [128, 4], [1, 128]])
                vw = [p.guards["vacc"]] if "vacc" in p.guards else []
                ev = p.op("vector", "tensor_copy", dv, pin,
                          wait=[("t", ttv)] + vw, sem="v")
                hb.free["tbf"] = ("v", ev)
                vacc_evs.append(ev)
            else:
                gw = [p.guards["gstg"]] if "gstg" in p.guards else []
                av = p.op("scalar", "activation", gstg[:], ps, AF.Silu,
                          bias=bh[:, mc:mc + 1], wait=[("t", tv)] + gw,
                          sem="a")
                hb.free[tag] = ("a", av)
                gsv = p.op("sync", "dma_start",
                     out=bass.AP(gdram.tensor, ((mc - 16) * 128) * T + g * 512,
                                 [[T, 128], [1, 512]]),
                     in_=gstg[:], wait=[("a", av)], sem="sg", n=16)
                p.guards["gstg"] = ("sg", gsv)
        for jj in range(4):
            vsv = p.op("sync", "dma_start",
                 out=vdram[g * 512 + jj * 128: g * 512 + (jj + 1) * 128, :],
                 in_=vacc[:, jj * 2048:(jj + 1) * 2048],
                 wait=[("v", vacc_evs[-1])], sem="sv", n=16)
        p.guards["vacc"] = ("sv", vsv)

        tagq = "A"
        w1_ = [("v", xv)] + hb.fwait(tagq)
        tv = mm_chain(p, hb.bank(tagq),
                      [wqk[:, kc * 128:(kc + 1) * 128] for kc in range(8)],
                      [hb.xnT[:, kc * 512:(kc + 1) * 512] for kc in range(8)],
                      w1_)
        av = p.op("scalar", "activation", qks[:], hb.bank(tagq), AF.Silu,
                  bias=bqk[:, 0:1], wait=[("t", tv)], sem="a")
        hb.free[tagq] = ("a", av)
        qkw = [p.guards["qkbf"]] if "qkbf" in p.guards else []
        qv = p.op("vector", "tensor_scalar", qkbf[:], qks[:], osw0[:, 0:1],
                  osb0[:, 0:1], op0=ALU.mult, op1=ALU.add,
                  wait=[("a", av)] + qkw, sem="v")
        sq = p.op("sync", "dma_start",
                  out=bass.AP(qdram.tensor, g * 512, [[T, 128], [1, 512]]),
                  in_=qkbf[:], wait=[("v", qv)], sem="sq", n=16)
        kv = p.op("vector", "tensor_scalar", qkbf[:], qks[:], osw1[:, 0:1],
                  osb1[:, 0:1], op0=ALU.mult, op1=ALU.add,
                  wait=[("sq", sq)], sem="v")
        sk = p.op("sync", "dma_start",
                  out=bass.AP(kdram.tensor, g * 512, [[T, 128], [1, 512]]),
                  in_=qkbf[:], wait=[("v", kv)], sem="sq", n=16)
        p.guards["qkbf"] = ("sq", sk)
        hb.g_gate = [("v", xn_vals[-1])]

    p.barrier()
    wstack.close()
    vq2 = [p.sbuf("vquart0", [128, 16 * 512], BF16),
           p.sbuf("vquart1", [128, 16 * 512], BF16)]
    attnT = vacc
    simtmp = p.sbuf("simtmp", [128, 512], F32)
    ogsb = p.sbuf("ogsb", [128, 16 * 512], BF16)
    gatesb = p.sbuf("gatesb", [128, 4 * 512], BF16)
    qsb = p.sbuf("qsb", [128, 2048], BF16)
    ksb = p.sbuf("ksb", [128, 2048], BF16)
    phaseA_done = [("sv", p.val("sv")), ("sg", p.val("sg")),
                   ("sq", p.val("sq"))]
    vstate = {0: None, 1: None}
    vld = {0: None, 1: None}
    vcons = {0: None, 1: None}
    vrr = [0]
    qk_v = None
    for g in range(NG):
        s, intile = divmod(g, 4)
        ldv = hb.group_load(g, xin)
        if intile == 0:
            p.wait("sync", [("ld", p.val("ld"))])
            qkg = [p.guards["qksb"]] if "qksb" in p.guards else []
            qk_lv = p.op("sync", "dma_start", out=qsb[:],
                         in_=bass.AP(qdram.tensor, s * 2048,
                                     [[T, 128], [1, 2048]]),
                         wait=phaseA_done + qkg, sem="ld", n=16)
            p.wait("sync", [("ld", p.val("ld"))])
            qk_v = p.op("sync", "dma_start", out=ksb[:],
                        in_=bass.AP(kdram.tensor, s * 2048,
                                    [[T, 128], [1, 2048]]),
                        sem="ld", n=16)
        for jc in range(16):
            tag = "A" if jc % 2 == 0 else "B"
            ps = hb.bank(tag)
            w0 = [("ld", qk_v)] + hb.fwait(tag)
            tv = p.op("tensor", "matmul", ps,
                      ksb[:, jc * 128:(jc + 1) * 128],
                      qsb[:, intile * 512:(intile + 1) * 512],
                      start=True, stop=True, wait=w0, sem="t")
            p.guards["qksb"] = ("t", tv)
            smw = [p.guards["simtmp"]] if "simtmp" in p.guards else []
            rv = p.op("vector", "tensor_scalar", simtmp[:], ps,
                      1.0 / 2048, 0.0, op0=ALU.mult, op1=ALU.max,
                      wait=[("t", tv)] + smw, sem="v")
            hb.free[tag] = ("v", rv)
            av = p.op("scalar", "activation",
                      attnT[:, jc * 512:(jc + 1) * 512], simtmp[:],
                      AF.Square, wait=[("v", rv)], sem="a")
            p.guards["simtmp"] = ("a", av)
            hb.free[tag] = ("a", av) if False else hb.free[tag]
        at_a = p.val("a")

        quarts = [0, 1, 2, 3] if intile % 2 == 0 else [3, 2, 1, 0]
        for qf in quarts:
            bi = None
            for b in (0, 1):
                if vstate[b] == (s, qf):
                    bi = b
                    break
            if bi is None:
                bi = vrr[0] % 2
                vrr[0] += 1
                w0 = list(phaseA_done)
                if vcons[bi] is not None:
                    w0.append(("t", vcons[bi]))
                for jc in range(16):
                    p.op("sync", "dma_start",
                         out=vq2[bi][:, jc * 512:(jc + 1) * 512],
                         in_=vdram[s * 2048 + jc * 128:
                                   s * 2048 + (jc + 1) * 128,
                                   qf * 512:(qf + 1) * 512],
                         wait=w0 if jc == 0 else (), sem=f"vq{bi}", n=16)
                vstate[bi] = (s, qf)
                vld[bi] = p.val(f"vq{bi}")
            vl = vld[bi]
            ggw = [p.guards["gatesb"]] if "gatesb" in p.guards else []
            gt_v = p.op("gpsimd", "dma_start",
                        out=gatesb[:].rearrange("p (c m) -> p c m", c=4),
                        in_=bass.AP(gdram.tensor, qf * 4 * 128 * T + g * 512,
                                    [[T, 128], [128 * T, 4], [1, 512]]),
                        wait=ggw + list(phaseA_done), sem="gl", n=16)
            for dv4 in range(4):
                dvc = qf * 4 + dv4
                tag = "A" if dv4 % 2 == 0 else "B"
                ps = hb.bank(tag)
                w0 = [(f"vq{bi}", vl), ("a", at_a)] + hb.fwait(tag)
                tv = mm_chain(p, ps,
                              [vq2[bi][:, jc * 512 + dv4 * 128:
                                       jc * 512 + (dv4 + 1) * 128]
                               for jc in range(16)],
                              [attnT[:, jc * 512:(jc + 1) * 512]
                               for jc in range(16)], w0)
                vcons[bi] = tv
                ev = p.op("vector", "tensor_tensor",
                          ogsb[:, dvc * 512:(dvc + 1) * 512], ps,
                          gatesb[:, dv4 * 512:(dv4 + 1) * 512],
                          op=ALU.mult, wait=[("t", tv), ("gl", gt_v)],
                          sem="v")
                hb.free[tag] = ("v", ev)
                p.guards["gatesb"] = ("v", ev)
        og_v = p.val("v")

        ya = hb.stage(hb.yfm, ogsb, wout, 16, 8, AF.Identity, bo,
                      [("v", og_v)], mtot=1024)
        hb.epilogue(g, [("a", ya)], xout, eps5)
    if fused is None:
        return finish(p)
    return None


def run_l2attn(x, ln_scale, ln_bias, W_hidden, b_hidden, W_qk, b_qk,
               os_weight, os_bias, W_out, b_out):
    return _launch("l2attn", build_l2attn, x,
                   {"ln_scale": ln_scale, "ln_bias": ln_bias,
                    "W_hidden": W_hidden, "b_hidden": b_hidden,
                    "W_qk": W_qk, "b_qk": b_qk, "os_weight": os_weight,
                    "os_bias": os_bias, "W_out": W_out, "b_out": b_out})




def build_hsel(p, consts):
    hsel = p.sbuf("hsel", [128, 2], BF16)
    mv = p.op("gpsimd", "memset", hsel[:], 0.0, sem="g")
    p.op("gpsimd", "affine_select", hsel[:, 0:1], consts["ones"][:, 0:1],
         [[1, 1]], ALU.is_ge, 0.0, base=63, channel_multiplier=-1,
         wait=[("g", mv)], sem="g")
    p.op("gpsimd", "affine_select", hsel[:, 1:2], consts["ones"][:, 0:1],
         [[1, 1]], ALU.is_ge, 0.0, base=-64, channel_multiplier=1,
         sem="g")
    return hsel, ("g", p.val("g"))


def qk_norm_chunk(p, hsel, ps_qn, q2tmp, qnstg, psrc, wait0, dram_ap, tag):
    sg = [p.guards[tag + "q2"]] if (tag + "q2") in p.guards else []
    av = p.op("scalar", "activation", q2tmp[:], psrc, AF.Square,
              wait=wait0 + sg, sem="a")
    p.guards[tag + "q2"] = ("a", av)
    mmw = [("a", av)] + ([p.guards["psqn"]] if "psqn" in p.guards else [])
    tv = p.op("tensor", "matmul", ps_qn[0:2, 0:512], hsel[:], q2tmp[:],
              start=True, stop=True, wait=mmw, sem="t")
    gv = [p.guards["qnstg"]] if "qnstg" in p.guards else []
    ev = p.op("vector", "tensor_copy", qnstg[0:2, :], ps_qn[0:2, 0:512],
              wait=[("t", tv)] + gv, sem="v")
    p.guards["psqn"] = ("v", ev)
    sv = p.op("sync", "dma_start", out=dram_ap, in_=qnstg[0:2, :],
              wait=[("v", ev)], sem="qn", n=16)
    p.guards["qnstg"] = ("qn", sv)
    return sv


def mhat_rows(p, hb, qn2d, kn2d, mdram, s, qnt, knt, kmax2, mnt, extra=None,
              ex_scalar=None, kmax2_ready=None):
    gq = [p.guards["qnt"]] if "qnt" in p.guards else []
    gq += [p.guards["xng"]] if "xng" in p.guards else []
    lv1 = p.op("sync", "dma_start", out=qnt,
               in_=bass.AP(qn2d.tensor, s * 2048, [[T, 16], [1, 2048]]),
               wait=gq, sem="ld", n=16)
    if kmax2_ready is None:
        lv2 = p.op("sync", "dma_start", out=knt,
                   in_=bass.AP(kn2d.tensor, s * 2048, [[T, 16], [1, 2048]]),
                   sem="ld", n=16)
        rv = p.op("vector", "reduce_max", kmax2[:], knt, axis=AX.X,
                  wait=[("ld", lv2)], sem="v")
        kwait = [("v", rv), ("ld", lv2)]
    else:
        kwait = list(kmax2_ready) + [("ld", lv1)]
    t1 = p.op("vector", "tensor_scalar", mnt, qnt, kmax2[:], None,
              op0=ALU.mult, wait=kwait, sem="v")
    t2 = p.op("scalar", "activation", mnt, mnt, AF.Sqrt,
              wait=[("v", t1)], sem="a")
    if extra is not None:
        t3 = p.op("vector", "tensor_scalar", mnt, mnt, ex_scalar,
                  -1.0, op0=ALU.add, op1=ALU.mult, wait=[("a", t2)], sem="v")
    else:
        t3 = p.op("vector", "tensor_scalar_mul", mnt, mnt, -1.0,
                  wait=[("a", t2)], sem="v")
    sv = p.op("sync", "dma_start",
              out=bass.AP(mdram.tensor, s * 16 * 2048, [[2048, 16], [1, 2048]]),
              in_=mnt, wait=[("v", t3)], sem="sm", n=16)
    p.guards["qnt"] = ("sm", sv)
    return sv



def load_w_scaled(p, name, dram_ap, kin, mout, scale, stg, gate=()):
    nkc = kin // 128
    sb = p.sbuf(name, [128, nkc * mout], BF16)
    last = None
    for kc in range(nkc):
        w0 = list(gate)
        if last is not None:
            w0.append(last)
        p.wait("sync", [("ld", p.val("ld"))])
        lv = p.op("sync", "dma_start", sem="ld", n=16, out=stg[:, 0:mout],
                  in_=dram_ap[kc * 128:(kc + 1) * 128, :], wait=w0)
        vv = p.op("vector", "tensor_scalar_mul",
                  sb[:, kc * mout:(kc + 1) * mout], stg[:, 0:mout],
                  float(scale), wait=[("ld", lv)], sem="v")
        last = ("v", vv)
    return sb, last


def build_l0attn(fused=None):
    if fused is None:
        p = P("l0attn")
        nc = p.nc
        xin = nc.dram_tensor("xin", [T, D], F32, kind="ExternalInput").ap()
        xout = nc.dram_tensor("xout", [T, D], F32, kind="ExternalOutput").ap()
        W = {k: nc.dram_tensor(k2, sh, F32, kind="ExternalInput").ap()
             for k, k2, sh in (
                 ("L0_attn_ln_scale", "ln_scale", [1024]),
                 ("L0_attn_ln_bias", "ln_bias", [1024]),
                 ("L0_attn_wq", "wq", [1024, 1024]),
                 ("L0_attn_wk", "wk", [1024, 1024]),
                 ("L0_attn_wv", "wv", [1024, 1024]))}
        consts = build_consts(p, f32=True, bf=True, eps=(1e-5, 1e-6))
    else:
        p, consts, xin, xout, W = fused
        nc = p.nc
    gmd = W["L0_attn_ln_scale"]; btd = W["L0_attn_ln_bias"]
    wqd = W["L0_attn_wq"]; wkd = W["L0_attn_wk"]; wvd = W["L0_attn_wv"]

    qdram = p.dram("qfm", [1024, T], BF16).ap()
    kdram = p.dram("kfm", [1024, T], BF16).ap()
    vdram = p.dram("vtm", [T, 1024], BF16).ap()
    xndram = p.dram("xn32", [T, 1024], F32).ap()
    mdram = p.dram("mdr", [BPC, 16, 2048], F32).ap()
    qn2d = p.dram("qn2d", [2, 8, T], F32).ap()
    kn2d = p.dram("kn2d", [2, 8, T], F32).ap()
    eps5 = consts["eps1em05"]
    eps6 = consts["eps1em06"]

    ps_s1 = p.psum("ps_s1", [128, 2048], F32)
    banks = {"A": ps_s1[:, 0:512], "B": ps_s1[:, 512:1024],
             "ep0": ps_s1[:, 1024:1536], "ep1": ps_s1[:, 1536:2048]}
    hb = HBCommon(p, consts, h_cols=0, yfm_cols=0, banks=banks,
                  ep_ap=ps_s1[:, 1024:2048])

    kmax2 = p.sbuf("kmax2", [16, 1], F32)

    if p.stack is None:
        p.stack = contextlib.ExitStack()
    wstack = contextlib.ExitStack()
    saved_stack = p.stack
    p.stack = wstack

    gbc, gv = bcast_row(p, "gbc", gmd, consts, hb.ps_scr[:])
    bbc, bv_ = bcast_row(p, "bbc", btd, consts, hb.ps_scr[:], wait_ps=[gv])
    hsel, hselv = build_hsel(p, consts)
    ps_qn = p.psum("ps_qn", [128, 512], F32)
    q2tmp = p.sbuf("q2tmp", [128, 512], BF16)
    qnstg = p.sbuf("qnstg", [2, 512], F32)

    xnf = p.sbuf("xnf", [128, 1024], F32)
    xnf2 = p.sbuf("xnf2", [128, 1024], F32)
    xnf3 = p.sbuf("xnf3", [128, 1024], F32)
    st_in2 = LNState(p, "hbi2")
    wstg = xnf
    wq, wqv = load_w_scaled(p, "wq_sb", wqd, 1024, 1024, 1.0 / 32.0, wstg,
                            gate=[bv_])
    wk, _ = load_w_fm(p, "wk_sb", wkd, 1024, 1024)
    wv_, _ = load_w_fm(p, "wv_sb", wvd, 1024, 1024)
    wldv = p.val("gld")

    pstg2 = [p.sbuf("pstg0", [128, 512], BF16),
             p.sbuf("pstg1", [128, 512], BF16)]
    vtmp = p.sbuf("vtmp", [128, 512], BF16)
    vacc = p.sbuf("vacc_a", [128, 4 * 1024], BF16)

    p.stack = saved_stack

    hb.g_gate = [bv_, wqv, ("gld", wldv)]

    for g in range(NG):
        ldv = hb.group_load(g, xin)
        sv = ln_stats(p, hb.st_in, hb.ps_scr[:], hb.xt, eps5,
                      [[("ld", ldv), ("g", consts["g_done"])]] * 4)
        xn_vals = []
        for j in range(4):
            n1 = p.op("vector", "tensor_scalar", xnf[:], hb.xt[j],
                      hb.st_in.mean4[:, j:j + 1], hb.st_in.rstd4[:, j:j + 1],
                      op0=ALU.subtract, op1=ALU.mult,
                      wait=[("v", sv)] if j == 0 else (), sem="v")
            sv2 = ln_stats(p, st_in2, hb.ps_scr[:], [xnf[:]], eps6,
                           [[("v", n1)]])
            n2 = p.op("vector", "tensor_scalar", xnf2[:], xnf[:],
                      st_in2.mean4[:, 0:1], st_in2.rstd4[:, 0:1],
                      op0=ALU.subtract, op1=ALU.mult,
                      wait=[("v", sv2)], sem="v")
            n3 = p.op("vector", "tensor_mul", xnf2[:], xnf2[:], gbc[:],
                      wait=[("v", n2)], sem="v")
            xw = [p.guards["xnf3"]] if "xnf3" in p.guards else []
            n4 = p.op("vector", "tensor_add", xnf3[:], xnf2[:], bbc[:],
                      wait=[("v", n3)] + xw, sem="v")
            sx = p.op("sync", "dma_start",
                      out=xndram[g * 512 + j * 128: g * 512 + (j + 1) * 128, :],
                      in_=xnf3[:], wait=[("v", n4)], sem="sx", n=16)
            p.guards["xnf3"] = ("sx", sx)
            n5 = p.op("vector", "tensor_copy",
                      hb.xnb[:, j * 1024:(j + 1) * 1024], xnf3[:],
                      wait=[("v", n4)], sem="v")
            xn_vals.append(n5)
        src_tiles = [hb.xnb[:, j * 1024:(j + 1) * 1024] for j in range(4)]

        def dview(j):
            return bass.AP(hb.xnT, j * 128,
                           [[8 * 512, 128], [512, 8], [1, 128]])
        xv = transpose_to_fm(p, hb.ps_tbf, consts["idbf"], src_tiles, dview,
                             4, wait_src=[[("v", xn_vals[j])] + hb.fwait("tbf")
                                          for j in range(4)])
        hb.free["tbf"] = ("v", xv)

        for which, wsb, sem_s, dr, nd in (("q", wq, "qs", qdram, qn2d),
                                          ("k", wk, "ks", kdram, kn2d)):
            for mc in range(8):
                tag = "A" if mc % 2 == 0 else "B"
                ps = hb.bank(tag)
                tv = mm_chain(p, ps,
                              [wsb[:, kc * 1024 + mc * 128:
                                   kc * 1024 + (mc + 1) * 128]
                               for kc in range(8)],
                              [hb.xnT[:, kc * 512:(kc + 1) * 512]
                               for kc in range(8)],
                              [("v", xv)] + hb.fwait(tag))
                pi = mc % 2
                pstg = pstg2[pi]
                pw = ([p.guards[f"pstg{pi}"]]
                      if f"pstg{pi}" in p.guards else [])
                av = p.op("scalar", "activation", pstg[:], ps, AF.Copy,
                          wait=[("t", tv)] + pw, sem="a")
                psv = p.op("sync", "dma_start",
                           out=bass.AP(dr.tensor, (mc * 128) * T + g * 512,
                                       [[T, 128], [1, 512]]),
                           in_=pstg[:], wait=[("a", av)], sem=sem_s, n=16)
                p.guards[f"pstg{pi}"] = (sem_s, psv)
                qk_norm_chunk(p, hsel, ps_qn, q2tmp, qnstg, pstg[:],
                              [("a", av), hselv],
                              bass.AP(nd.tensor, mc * T + g * 512,
                                      [[8 * T, 2], [1, 512]]), which)
                hb.free[tag] = ("a", av)
        vacc_ev = None
        for mc in range(8):
            tag = "A" if mc % 2 == 0 else "B"
            ps = hb.bank(tag)
            tv = mm_chain(p, ps,
                          [wv_[:, kc * 1024 + mc * 128:
                               kc * 1024 + (mc + 1) * 128]
                           for kc in range(8)],
                          [hb.xnT[:, kc * 512:(kc + 1) * 512]
                           for kc in range(8)],
                          [("v", xv)] + hb.fwait(tag))
            vw = [p.guards["vtmp"]] if "vtmp" in p.guards else []
            av = p.op("scalar", "activation", vtmp[:], ps, AF.Copy,
                      wait=[("t", tv)] + vw, sem="a")
            hb.free[tag] = ("a", av)
            tw = [("a", av)] + hb.fwait("tbf")
            for jj in range(4):
                p.op("tensor", "transpose",
                     hb.ps_tbf[:, jj * 128:(jj + 1) * 128],
                     vtmp[:, jj * 128:(jj + 1) * 128],
                     consts["idbf"][:],
                     wait=tw if jj == 0 else (),
                     sem="t" if jj == 3 else None)
            ttv = p.val("t")
            p.guards["vtmp"] = ("t", ttv)
            dv = bass.AP(vacc, mc * 128, [[4 * 1024, 128], [1024, 4], [1, 128]])
            pin = bass.AP(hb.ps_tbf, 0, [[1024, 128], [128, 4], [1, 128]])
            vw2 = [p.guards["vacc"]] if "vacc" in p.guards else []
            vacc_ev = p.op("vector", "tensor_copy", dv, pin,
                           wait=[("t", ttv)] + vw2, sem="v")
            hb.free["tbf"] = ("v", vacc_ev)
        for jj in range(4):
            vsv = p.op("sync", "dma_start",
                       out=vdram[g * 512 + jj * 128: g * 512 + (jj + 1) * 128, :],
                       in_=vacc[:, jj * 1024:(jj + 1) * 1024],
                       wait=[("v", vacc_ev)], sem="sv", n=16)
        p.guards["vacc"] = ("sv", vsv)
        hb.g_gate = [("v", xn_vals[-1])]

    p.barrier()
    wstack.close()

    khat2 = [p.sbuf("khat0", [65, 2048], BF16),
             p.sbuf("khat1", [65, 2048], BF16)]
    qhat2 = [p.sbuf("qhat0", [65, 2048], BF16),
             p.sbuf("qhat1", [65, 2048], BF16)]
    v12 = [p.sbuf("v1_0", [128, 16 * 65], BF16),
           p.sbuf("v1_1", [128, 16 * 65], BF16)]
    PT = p.sbuf("PT", [128, 16 * 512], BF16)
    lr2 = [p.sbuf("lr0", [65, 512], F32), p.sbuf("lr1", [65, 512], F32)]
    yst2 = [p.sbuf("yst0", [64, 512], BF16), p.sbuf("yst1", [64, 512], BF16)]
    invl4 = p.sbuf("invl4", [128, 4], F32)
    ybf = p.sbuf("ybf", [128, 4 * 4096], BF16)
    ps_av = p.psum("ps_av", [128, 512], F32)
    avbanks = [ps_av[0:65, 0:512], ps_s1[0:65, 1536:2048]]
    avkeys = ("avA", "ep1")
    hb.free["avA"] = None
    itc = [0]
    pend_tail = [None]
    xng = p.sbuf("xng", [128, 4 * 1024], F32)
    qnt = bass.AP(xng, 0, [[4096, 16], [1, 2048]])
    knt = bass.AP(xng, 2048, [[4096, 16], [1, 2048]])
    mnt = bass.AP(xng, 32 * 4096, [[4096, 16], [1, 2048]])

    p.op("gpsimd", "memset", khat2[0][64:65, :], 1.0, sem="g")
    p.op("gpsimd", "memset", khat2[1][64:65, :], 1.0, sem="g")
    p.op("gpsimd", "memset", v12[0][:], 1.0, sem="g")
    p.op("gpsimd", "memset", v12[1][:], 1.0, sem="g")
    gB_done = p.val("g")
    p.guards["pb0"] = ("g", gB_done)
    p.guards["pb1"] = ("g", gB_done)

    ldsem = ("gld", "gl2")

    def emit_loads(par, s_, h, m_done):
        sm = ldsem[par]
        key = "pb%d" % par
        kg = [p.guards[key]] if key in p.guards else []
        p.op("gpsimd", "dma_start", out=khat2[par][0:64, :],
             in_=bass.AP(kdram.tensor, (h * 64) * T + s_ * 2048,
                         [[T, 64], [1, 2048]]),
             wait=kg, sem=sm, n=16)
        p.op("gpsimd", "dma_start", out=qhat2[par][0:64, :],
             in_=bass.AP(qdram.tensor, (h * 64) * T + s_ * 2048,
                         [[T, 64], [1, 2048]]),
             sem=sm, n=16)
        r_ = (h % 2) * 8 + h // 2
        p.op("gpsimd", "dma_start", out=qhat2[par][64:65, :],
             in_=bass.AP(mdram.tensor, (s_ * 16 + r_) * 2048,
                         [[2048, 1], [1, 2048]]),
             wait=[("sm", m_done)], sem=sm, n=16)
        vlv = None
        for jc in range(16):
            vlv = p.op("gpsimd", "dma_start",
                       out=v12[par][:, jc * 65: jc * 65 + 64],
                       in_=vdram[s_ * 2048 + jc * 128:
                                 s_ * 2048 + (jc + 1) * 128,
                                 h * 64:(h + 1) * 64],
                       sem=sm, n=16)
        return (sm, vlv)

    for s in range(BPC):
        mhat_rows(p, hb, qn2d, kn2d, mdram, s, qnt, knt, kmax2, mnt)
        m_done = p.val("sm")

        pend = {0: emit_loads(0, s, 0, m_done)}
        for h in range(16):
            par = h % 2
            if h + 1 < 16:
                pend[1 - par] = emit_loads(1 - par, s, h + 1, m_done)
            khat, qhat, v1 = khat2[par], qhat2[par], v12[par]
            smp, vlv = pend[par]
            first = [("g", gB_done)] if (s == 0 and h == 0) else []
            for ib in range(4):
                bk = itc[0] % 2
                avps = avbanks[bk]
                avkey = avkeys[bk]
                ptg = [p.guards["PT"]] if "PT" in p.guards else []
                avs = []
                for jc in range(16):
                    tag = "A" if jc % 2 == 0 else "B"
                    ps = hb.bank(tag)
                    tv = p.op("tensor", "matmul", ps,
                              khat[:, jc * 128:(jc + 1) * 128],
                              qhat[:, ib * 512:(ib + 1) * 512],
                              start=True, stop=True,
                              wait=[(smp, vlv)] + first + hb.fwait(tag) +
                              (ptg if jc == 0 else []),
                              sem="t")
                    av = p.op("scalar", "activation",
                              PT[:, jc * 512:(jc + 1) * 512], ps, AF.Exp,
                              wait=[("t", tv)], sem="a")
                    avs.append(av)
                    hb.free[tag] = ("a", av)
                    if jc >= 2:
                        j2 = jc - 2
                        p.op("tensor", "matmul", avps,
                             v1[:, j2 * 65:(j2 + 1) * 65],
                             PT[:, j2 * 512:(j2 + 1) * 512],
                             start=(j2 == 0), stop=False,
                             wait=[("a", avs[j2])] +
                             (hb.fwait(avkey) if j2 == 0 else []))
                for j2 in (14, 15):
                    p.op("tensor", "matmul", avps,
                         v1[:, j2 * 65:(j2 + 1) * 65],
                         PT[:, j2 * 512:(j2 + 1) * 512],
                         start=False, stop=(j2 == 15),
                         wait=[("a", avs[j2])],
                         sem="t" if j2 == 15 else None)
                tv = p.val("t")
                p.guards["PT"] = ("t", tv)
                tp_ = itc[0] % 2
                lw = ([p.guards["lrp%d" % tp_]]
                      if ("lrp%d" % tp_) in p.guards else [])
                lev = p.op("vector", "tensor_copy", lr2[tp_][64:65, :],
                           avps[64:65, :], wait=[("t", tv)] + lw, sem="v")
                yw = ([p.guards["ystp%d" % tp_]]
                      if ("ystp%d" % tp_) in p.guards else [])
                ysv = p.op("vector", "tensor_copy", yst2[tp_][:],
                           avps[0:64, :], wait=[("t", tv)] + yw, sem="v")
                hb.free[avkey] = ("v", ysv)
                if pend_tail[0] is not None:
                    pend_tail[0]()

                def mk_tail(tp_=tp_, lev=lev, ysv=ysv, ib=ib, h=h):
                    def tail():
                        sg = [p.guards["scr"]] if "scr" in p.guards else []
                        for j in range(4):
                            p.op("tensor", "transpose",
                                 hb.ps_scr[:, 16 + j:16 + j + 1],
                                 lr2[tp_][64:65, j * 128:(j + 1) * 128],
                                 consts["idf32"][64:65, 64:65],
                                 wait=[("v", lev)] + sg if j == 0 else (),
                                 sem="t" if j == 3 else None)
                        ltv = p.val("t")
                        p.guards["lrp%d" % tp_] = ("t", ltv)
                        ig = ([p.guards["invg"]]
                              if "invg" in p.guards else [])
                        inv_v = p.op("vector", "reciprocal", invl4[:],
                                     bass.AP(hb.ps_scr, 16,
                                             [[1024, 128], [1, 4]]),
                                     wait=[("t", ltv)] + ig, sem="v")
                        p.guards["scr"] = ("v", inv_v)
                        tw = [("v", ysv)] + hb.fwait("tbf")
                        for j in range(4):
                            p.op("tensor", "transpose",
                                 hb.ps_tbf[:, j * 64:(j + 1) * 64],
                                 yst2[tp_][:, j * 128:(j + 1) * 128],
                                 consts["idbf"][0:64, 0:64],
                                 wait=tw if j == 0 else (),
                                 sem="t" if j == 3 else None)
                        ytv = p.val("t")
                        p.guards["ystp%d" % tp_] = ("t", ytv)
                        for j in range(4):
                            p.op("vector", "tensor_scalar",
                                 ybf[:, ib * 4096 + j * 1024 + h * 64:
                                     ib * 4096 + j * 1024 + (h + 1) * 64],
                                 hb.ps_tbf[:, j * 64:(j + 1) * 64],
                                 invl4[:, j:j + 1], None, op0=ALU.mult,
                                 wait=([("t", ytv), ("v", inv_v)]
                                       if j == 0 else ()),
                                 sem="v" if j == 3 else None)
                        hb.free["tbf"] = ("v", p.val("v"))
                        p.guards["invg"] = ("v", p.val("v"))
                    return tail
                pend_tail[0] = mk_tail()
                itc[0] += 1
            p.guards["pb%d" % par] = ("t", p.val("t"))
        if pend_tail[0] is not None:
            pend_tail[0]()
            pend_tail[0] = None
        heads_v = p.val("v")

        for ib in range(4):
            g = s * 4 + ib
            ldv = hb.group_load(g, xin)
            xw = [p.guards["xng"]] if "xng" in p.guards else []
            xgl = p.op("sync", "dma_start",
                       out=xng[:].rearrange("p (j d) -> p j d", j=4),
                       in_=xndram[g * 512:(g + 1) * 512, :].rearrange(
                           "(j p) d -> p j d", p=128),
                       wait=xw + [("sm", m_done)], sem="ld", n=16)
            ep_evs = []
            for j in range(4):
                cg = [("v", heads_v)] if j == 0 else []
                cg += [p.guards["ytm"]] if "ytm" in p.guards and j == 0 else []
                cv = p.op("vector", "tensor_copy",
                          hb.ytm[:, j * 1024:(j + 1) * 1024],
                          ybf[:, ib * 4096 + j * 1024:
                              ib * 4096 + (j + 1) * 1024],
                          wait=cg, sem="v")
                av2 = p.op("vector", "tensor_add",
                           hb.ytm[:, j * 1024:(j + 1) * 1024],
                           hb.ytm[:, j * 1024:(j + 1) * 1024],
                           xng[:, j * 1024:(j + 1) * 1024],
                           wait=[("v", cv), ("ld", xgl)], sem="v")
                ep_evs.append(av2)
            p.guards["xng"] = ("v", ep_evs[-1])
            yt = [hb.ytm[:, j * 1024:(j + 1) * 1024] for j in range(4)]
            sv2 = ln_stats(p, hb.st_ep, hb.ps_scr[:], yt, eps5,
                           [[("v", ep_evs[j])] for j in range(4)])
            st_vals = []
            for j in range(4):
                p.op("vector", "tensor_scalar", yt[j], yt[j],
                     hb.st_ep.mean4[:, j:j + 1], hb.st_ep.rstd4[:, j:j + 1],
                     op0=ALU.subtract, op1=ALU.mult,
                     wait=[("v", sv2)] if j == 0 else (), sem="v")
                uv = p.val("v")
                addv = p.op("vector", "tensor_add", hb.xt[j], hb.xt[j], yt[j],
                            wait=[("v", uv), ("ld", xgl)], sem="v")
                stv = p.op("sync", "dma_start",
                           out=xout[g * 512 + j * 128:
                                    g * 512 + (j + 1) * 128, :],
                           in_=hb.xt[j], wait=[("v", addv)], sem="st", n=16)
                st_vals.append(stv)
            hb.g_gate = [("st", st_vals[-1])]
            p.guards["ytm"] = ("a", p.val("a"))
    if fused is None:
        return finish(p)
    return None


def run_l0attn(x, ln_scale, ln_bias, wq, wk, wv):
    return _launch("l0attn", build_l0attn, x,
                   {"ln_scale": ln_scale, "ln_bias": ln_bias,
                    "wq": wq, "wk": wk, "wv": wv})



def prep_transpose_w(p, tag, wdram, nrows, ncols, consts, stage_f32, sgn_bf,
                     ps_tbf, dst_sb, dst_sb_width, scale=1.0, init_gate=()):
    nrc = nrows // 128
    ndc = ncols // 128
    prev_done = None
    last = None
    for rc in range(nrc):
        w0 = [("g", consts["g_done"])] + list(init_gate)
        if prev_done is not None:
            w0.append(prev_done)
        p.wait("sync", [("ld", p.val("ld"))])
        ldv = p.op("sync", "dma_start", out=stage_f32[:, 0:ncols],
                   in_=wdram[rc * 128:(rc + 1) * 128, :], sem="ld", n=16,
                   wait=w0)
        bv = p.op("vector", "tensor_scalar_mul", sgn_bf[:, 0:ncols],
                  stage_f32[:, 0:ncols], float(scale),
                  wait=[("ld", ldv)], sem="v")
        ev = None
        for base in range(0, ndc, 8):
            nb = min(8, ndc - base)
            tw = [("v", bv)]
            if ev is not None:
                tw.append(ev)
            for i in range(nb):
                p.op("tensor", "transpose",
                     ps_tbf[:, i * 128:(i + 1) * 128],
                     sgn_bf[:, (base + i) * 128:(base + i + 1) * 128],
                     consts["idbf"][:],
                     wait=tw if i == 0 else (),
                     sem="t" if i == nb - 1 else None)
            tv = p.val("t")
            pin = bass.AP(ps_tbf, 0, [[1024, 128], [128, nb], [1, 128]])
            dv = bass.AP(dst_sb, base * nrows + rc * 128,
                         [[dst_sb_width, 128], [nrows, nb], [1, 128]])
            vv = p.op("vector", "tensor_copy", dv, pin,
                      wait=[("t", tv)], sem="v")
            ev = ("v", vv)
            last = ev
        prev_done = last
    return last


def build_l3attn(fused=None):
    if fused is None:
        p = P("l3attn")
        nc = p.nc
        xin = nc.dram_tensor("xin", [T, D], F32, kind="ExternalInput").ap()
        xout = nc.dram_tensor("xout", [T, D], F32, kind="ExternalOutput").ap()
        W = {k: nc.dram_tensor(k2, sh, F32, kind="ExternalInput").ap()
             for k, k2, sh in (
                 ("L3_attn_wq", "wq", [1024, 1024]),
                 ("L3_attn_wk", "wk", [256, 1024]),
                 ("L3_attn_wv", "wv", [256, 1024]),
                 ("L3_attn_wo", "wo", [1024, 1024]))}
        consts = build_consts(p, f32=True, bf=True, eps=(1e-5, 1e-6))
    else:
        p, consts, xin, xout, W = fused
        nc = p.nc
    wqd = W["L3_attn_wq"]; wkd = W["L3_attn_wk"]
    wvd = W["L3_attn_wv"]; wod = W["L3_attn_wo"]

    qdram = p.dram("qfm", [1024, T], BF16).ap()
    kdram = p.dram("kfm", [256, T], BF16).ap()
    vdram = p.dram("vtm", [T, 256], BF16).ap()
    mdram = p.dram("mdr", [BPC, 16, 2048], F32).ap()
    qn2d = p.dram("qn2d", [2, 8, T], F32).ap()
    kn2d = p.dram("kn2d", [2, 2, T], F32).ap()
    kx4d = p.dram("kx4d", [4], F32).ap()
    eps5 = consts["eps1em05"]

    ps_s1 = p.psum("ps_s1", [128, 2048], F32)
    banks = {"A": ps_s1[:, 0:512], "B": ps_s1[:, 512:1024],
             "ep0": ps_s1[:, 1024:1536], "ep1": ps_s1[:, 1536:2048]}
    hb = HBCommon(p, consts, h_cols=0, banks=banks,
                  ep_ap=ps_s1[:, 1024:2048])

    cmaskT = p.sbuf("cmaskT", [128, 4 * 512], F32)
    kmax2 = p.sbuf("kmax2", [16, 1], F32)
    kmax4 = p.sbuf("kmax4", [4, 1], F32)
    knt4 = p.sbuf("knt4", [4, 2048], F32)
    PT = p.sbuf("PT", [128, 16 * 512], BF16)
    woT = p.sbuf("woT", [128, 8 * 1024], BF16)
    yfm = p.sbuf("yfm_l3", [128, 8 * 512], F32)
    vacc = PT[:, 0:2048]
    qnt = bass.AP(hb.ytm, 0, [[4096, 16], [1, 2048]])
    knt = None
    mnt = bass.AP(hb.ytm, 32 * 4096, [[4096, 16], [1, 2048]])

    if p.stack is None:
        p.stack = contextlib.ExitStack()
    wstack = contextlib.ExitStack()
    saved_stack = p.stack
    p.stack = wstack

    zt = p.sbuf("zt", [128, 512], F32)
    zv = p.op("gpsimd", "memset", zt[:], 0.0, sem="g")
    for off in range(4):
        p.op("gpsimd", "affine_select", cmaskT[:, off * 512:(off + 1) * 512],
             zt[:], [[1, 512]], ALU.is_ge, -1e30, base=-off * 128,
             channel_multiplier=-1, wait=[("g", zv)], sem="g")
    g_done2 = p.val("g")

    wstg = hb.xg
    wbfs = hb.xnb
    wqT = p.sbuf("wqT", [128, 8 * 1024], BF16)
    wkT = p.sbuf("wkT", [128, 8 * 256], BF16)
    wvT = p.sbuf("wvT", [128, 8 * 256], BF16)
    r1 = prep_transpose_w(p, "wq", wqd, 1024, 1024, consts, wstg, wbfs,
                          hb.ps_tbf, wqT, 8 * 1024, scale=1.0 / 8.0)
    r2 = prep_transpose_w(p, "wk", wkd, 256, 1024, consts, wstg, wbfs,
                          hb.ps_tbf, wkT, 8 * 256, init_gate=[r1])
    r3 = prep_transpose_w(p, "wv", wvd, 256, 1024, consts, wstg, wbfs,
                          hb.ps_tbf, wvT, 8 * 256, init_gate=[r2])
    r4 = prep_transpose_w(p, "wo", wod, 1024, 1024, consts, wstg, wbfs,
                          hb.ps_tbf, woT, 8 * 1024, init_gate=[r3])
    hb.free["tbf"] = r4

    pstg2 = [p.sbuf("pstg0", [128, 512], BF16),
             p.sbuf("pstg1", [128, 512], BF16)]
    vtmp = p.sbuf("vtmp", [128, 512], BF16)
    hsel, hselv = build_hsel(p, consts)
    ps_qn = p.psum("ps_qn", [128, 512], F32)
    q2tmp = p.sbuf("q2tmp", [128, 512], BF16)
    qnstg = p.sbuf("qnstg", [2, 512], F32)
    p.stack = saved_stack

    hb.g_gate = [r4, ("g", g_done2)]

    for g in range(NG):
        ldv = hb.group_load(g, xin)
        xv = hb.sweep(g, ldv, eps5)
        for mc in range(8):
            tag = "A" if mc % 2 == 0 else "B"
            ps = hb.bank(tag)
            tv = mm_chain(p, ps,
                          [wqT[:, kc * 1024 + mc * 128:
                               kc * 1024 + (mc + 1) * 128]
                           for kc in range(8)],
                          [hb.xnT[:, kc * 512:(kc + 1) * 512]
                           for kc in range(8)],
                          [("v", xv)] + hb.fwait(tag))
            pi = mc % 2
            pstg = pstg2[pi]
            pw = ([p.guards[f"pstg{pi}"]]
                  if f"pstg{pi}" in p.guards else [])
            av = p.op("scalar", "activation", pstg[:], ps, AF.Copy,
                      wait=[("t", tv)] + pw, sem="a")
            hb.free[tag] = ("a", av)
            psv = p.op("sync", "dma_start",
                       out=bass.AP(qdram.tensor, (mc * 128) * T + g * 512,
                                   [[T, 128], [1, 512]]),
                       in_=pstg[:], wait=[("a", av)], sem="qs", n=16)
            p.guards[f"pstg{pi}"] = ("qs", psv)
            qk_norm_chunk(p, hsel, ps_qn, q2tmp, qnstg, pstg[:],
                          [("a", av), hselv],
                          bass.AP(qn2d.tensor, mc * T + g * 512,
                                  [[8 * T, 2], [1, 512]]), "q")
        for mc in range(2):
            tag = "A" if mc % 2 == 0 else "B"
            ps = hb.bank(tag)
            tv = mm_chain(p, ps,
                          [wkT[:, kc * 256 + mc * 128:
                               kc * 256 + (mc + 1) * 128]
                           for kc in range(8)],
                          [hb.xnT[:, kc * 512:(kc + 1) * 512]
                           for kc in range(8)],
                          [("v", xv)] + hb.fwait(tag))
            pi = mc % 2
            pstg = pstg2[pi]
            pw = ([p.guards[f"pstg{pi}"]]
                  if f"pstg{pi}" in p.guards else [])
            av = p.op("scalar", "activation", pstg[:], ps, AF.Copy,
                      wait=[("t", tv)] + pw, sem="a")
            hb.free[tag] = ("a", av)
            psv = p.op("sync", "dma_start",
                       out=bass.AP(kdram.tensor, (mc * 128) * T + g * 512,
                                   [[T, 128], [1, 512]]),
                       in_=pstg[:], wait=[("a", av)], sem="ks", n=16)
            p.guards[f"pstg{pi}"] = ("ks", psv)
            qk_norm_chunk(p, hsel, ps_qn, q2tmp, qnstg, pstg[:],
                          [("a", av)],
                          bass.AP(kn2d.tensor, mc * T + g * 512,
                                  [[2 * T, 2], [1, 512]]), "k")
        vacc_ev = None
        for mc in range(2):
            tag = "A" if mc % 2 == 0 else "B"
            ps = hb.bank(tag)
            tv = mm_chain(p, ps,
                          [wvT[:, kc * 256 + mc * 128:
                               kc * 256 + (mc + 1) * 128]
                           for kc in range(8)],
                          [hb.xnT[:, kc * 512:(kc + 1) * 512]
                           for kc in range(8)],
                          [("v", xv)] + hb.fwait(tag))
            vw = [p.guards["vtmp"]] if "vtmp" in p.guards else []
            av = p.op("scalar", "activation", vtmp[:], ps, AF.Copy,
                      wait=[("t", tv)] + vw, sem="a")
            hb.free[tag] = ("a", av)
            tw = [("a", av)] + hb.fwait("tbf")
            for jj in range(4):
                p.op("tensor", "transpose",
                     hb.ps_tbf[:, jj * 128:(jj + 1) * 128],
                     vtmp[:, jj * 128:(jj + 1) * 128],
                     consts["idbf"][:],
                     wait=tw if jj == 0 else (),
                     sem="t" if jj == 3 else None)
            ttv = p.val("t")
            p.guards["vtmp"] = ("t", ttv)
            dv = bass.AP(PT, mc * 128, [[16 * 512, 128], [256, 4], [1, 128]])
            pin = bass.AP(hb.ps_tbf, 0, [[1024, 128], [128, 4], [1, 128]])
            vw2 = [p.guards["vacc"]] if "vacc" in p.guards else []
            vacc_ev = p.op("vector", "tensor_copy", dv, pin,
                           wait=[("t", ttv)] + vw2, sem="v")
            hb.free["tbf"] = ("v", vacc_ev)
        for jj in range(4):
            vsv = p.op("sync", "dma_start",
                       out=vdram[g * 512 + jj * 128: g * 512 + (jj + 1) * 128, :],
                       in_=vacc[:, jj * 256:(jj + 1) * 256],
                       wait=[("v", vacc_ev)], sem="sv", n=16)
        p.guards["vacc"] = ("sv", vsv)
        hb.g_gate = [("v", xv)]

    p.barrier()
    wstack.close()

    khat2 = [p.sbuf("khat0", [65, 2048], BF16),
             p.sbuf("khat1", [65, 2048], BF16)]
    qhat2 = [p.sbuf("qhat0", [65, 2048], BF16),
             p.sbuf("qhat1", [65, 2048], BF16)]
    v12 = [p.sbuf("v1_0", [128, 16 * 65], BF16),
           p.sbuf("v1_1", [128, 16 * 65], BF16)]
    lr2 = [p.sbuf("lr0", [65, 512], F32), p.sbuf("lr1", [65, 512], F32)]
    yst2 = [p.sbuf("yst0", [64, 512], BF16), p.sbuf("yst1", [64, 512], BF16)]
    invl4 = p.sbuf("invl4", [128, 4], F32)
    cbf = p.sbuf("cbf", [128, 4 * 4096], BF16)
    ps_av = p.psum("ps_av", [128, 512], F32)
    avbanks = [ps_av[0:65, 0:512], ps_s1[0:65, 1536:2048]]
    avkeys = ("avA", "ep1")
    hb.free["avA"] = None
    itc = [0]
    pend_tail = [None]

    p.op("gpsimd", "memset", khat2[0][64:65, :], 1.0, sem="g")
    p.op("gpsimd", "memset", khat2[1][64:65, :], 1.0, sem="g")
    p.op("gpsimd", "memset", v12[0][:], 1.0, sem="g")
    p.op("gpsimd", "memset", v12[1][:], 1.0, sem="g")
    gB_done = p.val("g")
    p.guards["pb0"] = ("g", gB_done)
    p.guards["pb1"] = ("g", gB_done)

    ldsem = ("gld", "gl2")

    def emit_loads(par, s_, h, m_done):
        sm = ldsem[par]
        kvh = h // 4
        key = "pb%d" % par
        kg = [p.guards[key]] if key in p.guards else []
        p.op("gpsimd", "dma_start", out=khat2[par][0:64, :],
             in_=bass.AP(kdram.tensor, (kvh * 64) * T + s_ * 2048,
                         [[T, 64], [1, 2048]]),
             wait=kg, sem=sm, n=16)
        p.op("gpsimd", "dma_start", out=qhat2[par][0:64, :],
             in_=bass.AP(qdram.tensor, (h * 64) * T + s_ * 2048,
                         [[T, 64], [1, 2048]]),
             sem=sm, n=16)
        r_ = (h % 2) * 8 + h // 2
        p.op("gpsimd", "dma_start", out=qhat2[par][64:65, :],
             in_=bass.AP(mdram.tensor, (s_ * 16 + r_) * 2048,
                         [[2048, 1], [1, 2048]]),
             wait=[("sm", m_done)], sem=sm, n=16)
        vlv = None
        for jc in range(16):
            vlv = p.op("gpsimd", "dma_start",
                       out=v12[par][:, jc * 65: jc * 65 + 64],
                       in_=vdram[s_ * 2048 + jc * 128:
                                 s_ * 2048 + (jc + 1) * 128,
                                 kvh * 64:(kvh + 1) * 64],
                       sem=sm, n=16)
        return (sm, vlv)

    for s in range(BPC):
        gq2 = [p.guards["qnt"]] if "qnt" in p.guards else []
        gq2 += [p.guards["xng"]] if "xng" in p.guards else []
        klv4 = p.op("sync", "dma_start", out=knt4[:],
                    in_=bass.AP(kn2d.tensor, s * 2048, [[T, 4], [1, 2048]]),
                    wait=gq2, sem="ld", n=16)
        kr4 = p.op("vector", "reduce_max", kmax4[:], knt4[:], axis=AX.X,
                   wait=[("ld", klv4)], sem="v")
        ks4 = p.op("sync", "dma_start",
                   out=kx4d.rearrange("(a b) -> a b", b=1),
                   in_=kmax4[:],
                   wait=[("v", kr4)], sem="sm", n=16)
        p.wait("sync", [("sm", p.val("sm"))])
        for r_ in range(16):
            h_ = 2 * (r_ % 8) + r_ // 8
            kv_ = h_ // 4
            r2_ = (kv_ % 2) * 2 + kv_ // 2
            p.wait("sync", [("ld", p.val("ld"))])
            p.op("sync", "dma_start",
                 out=kmax2[r_:r_ + 1, 0:1],
                 in_=bass.AP(kx4d.tensor, r2_, [[1, 1], [1, 1]]),
                 sem="ld", n=16)
        k16v = p.val("ld")
        p.wait("sync", [("ld", k16v)])
        mhat_rows(p, hb, qn2d, kn2d, mdram, s, qnt, knt, kmax2, mnt,
                  kmax2_ready=[("ld", k16v)])
        m_done = p.val("sm")

        pend = {0: emit_loads(0, s, 0, m_done)}
        for h in range(16):
            par = h % 2
            if h + 1 < 16:
                pend[1 - par] = emit_loads(1 - par, s, h + 1, m_done)
            khat, qhat, v1 = khat2[par], qhat2[par], v12[par]
            smp, vlv = pend[par]
            first = [("g", gB_done)] if (s == 0 and h == 0) else []
            for ib in range(4):
                njc = 4 * ib + 4
                bk = itc[0] % 2
                avps = avbanks[bk]
                avkey = avkeys[bk]
                ptg = [p.guards["PT"]] if "PT" in p.guards else []
                avs = []
                for jc in range(njc):
                    tag = "A" if jc % 2 == 0 else "B"
                    ps = hb.bank(tag)
                    tv = p.op("tensor", "matmul", ps,
                              khat[:, jc * 128:(jc + 1) * 128],
                              qhat[:, ib * 512:(ib + 1) * 512],
                              start=True, stop=True,
                              wait=[(smp, vlv)] + first + hb.fwait(tag) +
                              (ptg if jc == 0 else []),
                              sem="t")
                    if jc >= 4 * ib:
                        off = jc - 4 * ib
                        mv2 = p.op("vector", "tensor_tensor", ps, ps,
                                   cmaskT[:, off * 512:(off + 1) * 512],
                                   op=ALU.add,
                                   wait=[("t", tv), ("g", g_done2)], sem="v")
                        av = p.op("scalar", "activation",
                                  PT[:, jc * 512:(jc + 1) * 512], ps, AF.Exp,
                                  wait=[("v", mv2)], sem="a")
                    else:
                        av = p.op("scalar", "activation",
                                  PT[:, jc * 512:(jc + 1) * 512], ps, AF.Exp,
                                  wait=[("t", tv)], sem="a")
                    avs.append(av)
                    hb.free[tag] = ("a", av)
                    if jc >= 2:
                        j2 = jc - 2
                        p.op("tensor", "matmul", avps,
                             v1[:, j2 * 65:(j2 + 1) * 65],
                             PT[:, j2 * 512:(j2 + 1) * 512],
                             start=(j2 == 0), stop=False,
                             wait=[("a", avs[j2])] +
                             (hb.fwait(avkey) if j2 == 0 else []))
                for j2 in (njc - 2, njc - 1):
                    p.op("tensor", "matmul", avps,
                         v1[:, j2 * 65:(j2 + 1) * 65],
                         PT[:, j2 * 512:(j2 + 1) * 512],
                         start=False, stop=(j2 == njc - 1),
                         wait=[("a", avs[j2])],
                         sem="t" if j2 == njc - 1 else None)
                tv = p.val("t")
                p.guards["PT"] = ("t", tv)
                tp_ = itc[0] % 2
                lw = ([p.guards["lrp%d" % tp_]]
                      if ("lrp%d" % tp_) in p.guards else [])
                lev = p.op("vector", "tensor_copy", lr2[tp_][64:65, :],
                           avps[64:65, :], wait=[("t", tv)] + lw, sem="v")
                yw = ([p.guards["ystp%d" % tp_]]
                      if ("ystp%d" % tp_) in p.guards else [])
                ysv = p.op("vector", "tensor_copy", yst2[tp_][:],
                           avps[0:64, :], wait=[("t", tv)] + yw, sem="v")
                hb.free[avkey] = ("v", ysv)
                if pend_tail[0] is not None:
                    pend_tail[0]()

                def mk_tail(tp_=tp_, lev=lev, ysv=ysv, ib=ib, h=h):
                    def tail():
                        sg = [p.guards["scr"]] if "scr" in p.guards else []
                        for j in range(4):
                            p.op("tensor", "transpose",
                                 hb.ps_scr[:, 16 + j:16 + j + 1],
                                 lr2[tp_][64:65, j * 128:(j + 1) * 128],
                                 consts["idf32"][64:65, 64:65],
                                 wait=[("v", lev)] + sg if j == 0 else (),
                                 sem="t" if j == 3 else None)
                        ltv = p.val("t")
                        p.guards["lrp%d" % tp_] = ("t", ltv)
                        ig = ([p.guards["invg"]]
                              if "invg" in p.guards else [])
                        inv_v = p.op("vector", "reciprocal", invl4[:],
                                     bass.AP(hb.ps_scr, 16,
                                             [[1024, 128], [1, 4]]),
                                     wait=[("t", ltv)] + ig, sem="v")
                        p.guards["scr"] = ("v", inv_v)
                        tw = [("v", ysv)] + hb.fwait("tbf")
                        for j in range(4):
                            p.op("tensor", "transpose",
                                 hb.ps_tbf[:, j * 64:(j + 1) * 64],
                                 yst2[tp_][:, j * 128:(j + 1) * 128],
                                 consts["idbf"][0:64, 0:64],
                                 wait=tw if j == 0 else (),
                                 sem="t" if j == 3 else None)
                        ytv = p.val("t")
                        p.guards["ystp%d" % tp_] = ("t", ytv)
                        for j in range(4):
                            p.op("vector", "tensor_scalar",
                                 cbf[:, ib * 4096 + j * 1024 + h * 64:
                                     ib * 4096 + j * 1024 + (h + 1) * 64],
                                 hb.ps_tbf[:, j * 64:(j + 1) * 64],
                                 invl4[:, j:j + 1], None, op0=ALU.mult,
                                 wait=([("t", ytv), ("v", inv_v)]
                                       if j == 0 else ()),
                                 sem="v" if j == 3 else None)
                        hb.free["tbf"] = ("v", p.val("v"))
                        p.guards["invg"] = ("v", p.val("v"))
                    return tail
                pend_tail[0] = mk_tail()
                itc[0] += 1
            p.guards["pb%d" % par] = ("t", p.val("t"))
        if pend_tail[0] is not None:
            pend_tail[0]()
            pend_tail[0] = None
        heads_v = p.val("v")

        for ib in range(4):
            g = s * 4 + ib
            ldv = hb.group_load(g, xin)
            src_tiles = [cbf[:, ib * 4096 + j * 1024:
                             ib * 4096 + (j + 1) * 1024] for j in range(4)]

            def dview(j):
                return bass.AP(hb.xnT, j * 128,
                               [[8 * 512, 128], [512, 8], [1, 128]])
            cxv = transpose_to_fm(p, hb.ps_tbf, consts["idbf"], src_tiles,
                                  dview, 4,
                                  wait_src=[[("v", heads_v)] +
                                            hb.fwait("tbf")
                                            for j in range(4)])
            hb.free["tbf"] = ("v", cxv)
            for mc in range(8):
                tag = "A" if mc % 2 == 0 else "B"
                ps = hb.bank(tag)
                tv = mm_chain(p, ps,
                              [woT[:, kc * 1024 + mc * 128:
                                   kc * 1024 + (mc + 1) * 128]
                               for kc in range(8)],
                              [hb.xnT[:, kc * 512:(kc + 1) * 512]
                               for kc in range(8)],
                              [("v", cxv)] + hb.fwait(tag))
                av = p.op("scalar", "activation",
                          yfm[:, mc * 512:(mc + 1) * 512], ps, AF.Copy,
                          wait=[("t", tv)], sem="a")
                hb.free[tag] = ("a", av)
            ya = p.val("a")
            hb.yfm = yfm
            stv = hb.epilogue(g, [("a", ya)], xout, eps5,
                              x_wait=[("ld", ldv)])
            p.guards["xng"] = ("st", stv[-1])
    if fused is None:
        return finish(p)
    return None


def run_l3attn(x, wq, wk, wv, wo):
    return _launch("l3attn", build_l3attn, x,
                   {"wq": wq, "wk": wk, "wv": wv, "wo": wo})



def build_l3ffn(fused=None):
    if fused is None:
        p = P("l3ffn")
        nc = p.nc
        xin = nc.dram_tensor("xin", [T, D], F32, kind="ExternalInput").ap()
        xout = nc.dram_tensor("xout", [T, D], F32, kind="ExternalOutput").ap()
        W = {k: nc.dram_tensor(k2, sh, F32, kind="ExternalInput").ap()
             for k, k2, sh in (
                 ("L3_ffn_Wg", "Wg", [1024, 8]),
                 ("L3_ffn_W1", "W1", [8, 1024, 1408]),
                 ("L3_ffn_W2", "W2", [8, 1024, 1408]),
                 ("L3_ffn_W3", "W3", [8, 1408, 1024]),
                 ("L3_ffn_Ws1", "Ws1", [1024, 2816]),
                 ("L3_ffn_Ws2", "Ws2", [1024, 2816]),
                 ("L3_ffn_Ws3", "Ws3", [2816, 1024]))}
        consts = build_consts(p, f32=True, bf=True, eps=(1e-5, 1e-6))
    else:
        p, consts, xin, xout, W = fused
        nc = p.nc
    wgd = W["L3_ffn_Wg"]; w1d = W["L3_ffn_W1"]; w2d = W["L3_ffn_W2"]
    w3d = W["L3_ffn_W3"]; ws1d = W["L3_ffn_Ws1"]; ws2d = W["L3_ffn_Ws2"]
    ws3d = W["L3_ffn_Ws3"]

    xnTd = p.dram("xnTd", [1024, T], BF16).ap()
    wfmd = p.dram("wfmd", [8, T], F32).ap()
    yaccd = p.dram("yaccd", [1024, T], F32).ap()
    eps5 = consts["eps1em05"]
    hb = HBCommon(p, consts, h_cols=0)

    xnf3 = p.sbuf("xnf3", [128, 1024], F32)
    xnT32 = p.sbuf("xnT32", [128, 8 * 512], F32)
    wg = p.sbuf("wg_sb", [128, 8 * 8], F32)
    rsb = p.sbuf("rsb", [128, 8], F32)
    m1 = p.sbuf("m1", [128, 1], F32)
    m2 = p.sbuf("m2", [128, 1], F32)
    eq1 = p.sbuf("eq1", [128, 8], F32)
    eq2 = p.sbuf("eq2", [128, 8], F32)
    rmk = p.sbuf("rmk", [128, 8], F32)
    dd = p.sbuf("dd", [128, 2], F32)
    pp = p.sbuf("pp", [128, 2], F32)
    w8 = p.sbuf("w8", [128, 8], F32)
    wfstg = p.sbuf("wfstg", [8, 512], F32)
    w1e = p.sbuf("w1e", [128, 8 * 1408], BF16)
    w2e = p.sbuf("w2e", [128, 8 * 1408], BF16)
    w3e = p.sbuf("w3e", [128, 11 * 1024], BF16)
    hsb = p.sbuf("hsb", [128, 11 * 512], BF16)
    stmp = p.sbuf("stmp", [128, 512], F32)
    tmp_sb = p.sbuf("tmp_sb", [128, 512], F32)
    webc = p.sbuf("webc", [128, 512], F32)
    wrow = p.sbuf("wrow", [1, 512], F32)

    for kc in range(8):
        p.op("sync", "dma_start", out=wg[:, kc * 8:(kc + 1) * 8],
             in_=wgd[kc * 128:(kc + 1) * 128, :], sem="ld", n=16)
    wgv = p.val("ld")
    hb.g_gate = [("ld", wgv), ("g", consts["g_done"])]

    for g in range(NG):
        ldv = hb.group_load(g, xin)
        sv = ln_stats(p, hb.st_in, hb.ps_scr[:], hb.xt, eps5,
                      [[("ld", ldv), ("g", consts["g_done"])]] * 4)
        xnb_vals = []
        xn32_vals = []
        for j in range(4):
            xw = [p.guards["xnf3"]] if "xnf3" in p.guards else []
            n1 = p.op("vector", "tensor_scalar", xnf3[:], hb.xt[j],
                      hb.st_in.mean4[:, j:j + 1], hb.st_in.rstd4[:, j:j + 1],
                      op0=ALU.subtract, op1=ALU.mult,
                      wait=([("v", sv)] if j == 0 else []) + xw,
                      sem="v")
            n2 = p.op("vector", "tensor_copy",
                      hb.xnb[:, j * 1024:(j + 1) * 1024], xnf3[:],
                      wait=[("v", n1)], sem="v")
            xnb_vals.append(n2)
            tw = [("v", n1)] + hb.fwait("ep0", "ep1")
            for cc in range(8):
                p.op("tensor", "transpose",
                     hb.ep_ap[:, cc * 128:(cc + 1) * 128],
                     xnf3[:, cc * 128:(cc + 1) * 128],
                     consts["idf32"][:],
                     wait=tw if cc == 0 else (),
                     sem="t" if cc == 7 else None)
            ttv = p.val("t")
            dv32 = bass.AP(xnT32, j * 128, [[8 * 512, 128], [512, 8], [1, 128]])
            pin = hb.ep_ap.rearrange("p (c m) -> p c m", c=8)
            ev = p.op("vector", "tensor_copy", dv32, pin,
                      wait=[("t", ttv)], sem="v")
            hb.free["ep0"] = hb.free["ep1"] = ("v", ev)
            p.guards["xnf3"] = ("v", ev)
            xn32_vals.append(ev)
        src_tiles = [hb.xnb[:, j * 1024:(j + 1) * 1024] for j in range(4)]

        def dview(j):
            return bass.AP(hb.xnT, j * 128,
                           [[8 * 512, 128], [512, 8], [1, 128]])
        xtg = [p.guards["xnTs"]] if "xnTs" in p.guards else []
        xv = transpose_to_fm(p, hb.ps_tbf, consts["idbf"], src_tiles, dview,
                             4, wait_src=[[("v", xnb_vals[j])] +
                                          hb.fwait("tbf") + xtg
                                          for j in range(4)])
        hb.free["tbf"] = ("v", xv)
        for kc in range(8):
            sxv = p.op("sync", "dma_start",
                       out=bass.AP(xnTd.tensor, (kc * 128) * T + g * 512,
                                   [[T, 128], [1, 512]]),
                       in_=hb.xnT[:, kc * 512:(kc + 1) * 512],
                       wait=[("v", xv)] if kc == 0 else (), sem="sx", n=16)
        p.guards["xnTs"] = ("sx", sxv)
        for j in range(4):
            for kc in range(8):
                p.op("tensor", "matmul", hb.bank("A")[:, 0:8],
                     xnT32[:, kc * 512 + j * 128: kc * 512 + (j + 1) * 128],
                     wg[:, kc * 8:(kc + 1) * 8],
                     start=(kc == 0), stop=(kc == 7),
                     wait=([("v", xn32_vals[3]), ("ld", wgv)] +
                           hb.fwait("A")) if kc == 0 else (),
                     sem="t" if kc == 7 else None)
            rtv = p.val("t")
            rg = [p.guards["rsb"]] if "rsb" in p.guards else []
            r0 = p.op("vector", "tensor_copy", rsb[:], hb.bank("A")[:, 0:8],
                      wait=[("t", rtv)] + rg, sem="v")
            hb.free["A"] = ("v", r0)
            v1_ = p.op("vector", "reduce_max", m1[:], rsb[:], axis=AX.X,
                       wait=[("v", r0)], sem="v")
            v2_ = p.op("vector", "tensor_single_scalar", eq1[:], rsb[:],
                       m1[:], op=ALU.is_ge, wait=[("v", v1_)], sem="v")
            v3_ = p.op("vector", "scalar_tensor_tensor", rmk[:], eq1[:],
                       -1e30, rsb[:], op0=ALU.mult, op1=ALU.add,
                       wait=[("v", v2_)], sem="v")
            v4_ = p.op("vector", "reduce_max", m2[:], rmk[:], axis=AX.X,
                       wait=[("v", v3_)], sem="v")
            v5_ = p.op("vector", "tensor_single_scalar", eq2[:], rmk[:],
                       m2[:], op=ALU.is_ge, wait=[("v", v4_)], sem="v")
            v6_ = p.op("vector", "tensor_sub", dd[:, 0:1], m1[:], m2[:],
                       wait=[("v", v5_)], sem="v")
            v7_ = p.op("vector", "tensor_scalar_mul", dd[:, 1:2], dd[:, 0:1],
                       -1.0, wait=[("v", v6_)], sem="v")
            a1_ = p.op("scalar", "activation", pp[:], dd[:], AF.Sigmoid,
                       wait=[("v", v7_)], sem="a")
            v8_ = p.op("vector", "tensor_scalar", w8[:], eq1[:], pp[:, 0:1],
                       None, op0=ALU.mult, wait=[("a", a1_)], sem="v")
            v9_ = p.op("vector", "scalar_tensor_tensor", w8[:], eq2[:],
                       pp[:, 1:2], w8[:], op0=ALU.mult, op1=ALU.add,
                       wait=[("v", v8_)], sem="v")
            wtv = p.op("tensor", "transpose", hb.ps_scr[0:8, 0:128],
                       w8[:, 0:8], consts["idf32"][:],
                       wait=[("v", v9_)] + ([p.guards["scr8"]]
                                            if "scr8" in p.guards else []),
                       sem="t")
            wev = p.op("vector", "tensor_copy", wfstg[:, j * 128:(j + 1) * 128],
                       hb.ps_scr[0:8, 0:128],
                       wait=[("t", wtv)] + ([p.guards["wfstg"]]
                                            if ("wfstg" in p.guards and j == 0)
                                            else []),
                       sem="v")
            p.guards["scr8"] = ("v", wev)
        swv = p.op("sync", "dma_start",
                   out=bass.AP(wfmd.tensor, g * 512, [[T, 8], [1, 512]]),
                   in_=wfstg[:], wait=[("v", p.val("v"))], sem="sw", n=16)
        p.guards["wfstg"] = ("sw", swv)
        hb.g_gate = [("v", xv), ("v", p.val("v"))]

    phaseA = [("sx", p.val("sx")), ("sw", p.val("sw"))]

    ya_gate = {}
    wcons = None
    xg_cons = None
    for e in range(10):
        wld = [("g", consts["g_done"])] + list(phaseA)
        if wcons is not None:
            wld.append(("t", wcons))
        if e < 8:
            for kc in range(8):
                p.op("gpsimd", "dma_start", out=w1e[:, kc * 1408:(kc + 1) * 1408],
                     in_=w1d[e, kc * 128:(kc + 1) * 128, :],
                     wait=wld if kc == 0 else (), sem="gld", n=16)
                p.op("gpsimd", "dma_start", out=w2e[:, kc * 1408:(kc + 1) * 1408],
                     in_=w2d[e, kc * 128:(kc + 1) * 128, :], sem="gld", n=16)
            for kc in range(11):
                p.op("gpsimd", "dma_start", out=w3e[:, kc * 1024:(kc + 1) * 1024],
                     in_=w3d[e, kc * 128:(kc + 1) * 128, :], sem="gld", n=16)
        else:
            e2 = e - 8
            for kc in range(8):
                p.op("gpsimd", "dma_start", out=w1e[:, kc * 1408:(kc + 1) * 1408],
                     in_=ws1d[kc * 128:(kc + 1) * 128,
                              e2 * 1408:(e2 + 1) * 1408],
                     wait=wld if kc == 0 else (), sem="gld", n=16)
                p.op("gpsimd", "dma_start", out=w2e[:, kc * 1408:(kc + 1) * 1408],
                     in_=ws2d[kc * 128:(kc + 1) * 128,
                              e2 * 1408:(e2 + 1) * 1408], sem="gld", n=16)
            for kc in range(11):
                p.op("gpsimd", "dma_start", out=w3e[:, kc * 1024:(kc + 1) * 1024],
                     in_=ws3d[e2 * 1408 + kc * 128: e2 * 1408 + (kc + 1) * 128,
                              :], sem="gld", n=16)
        wv_ = p.val("gld")

        for g in range(NG):
            xld = [("gld", wv_)]
            if xg_cons is not None:
                xld.append(("t", xg_cons))
            for kc in range(8):
                p.op("gpsimd", "dma_start",
                     out=hb.xnT[:, kc * 512:(kc + 1) * 512],
                     in_=bass.AP(xnTd.tensor, (kc * 128) * T + g * 512,
                                 [[T, 128], [1, 512]]),
                     wait=xld if kc == 0 else (), sem="gld", n=16)
            xlv = p.val("gld")
            if e < 8:
                p.wait("sync", [("ld", p.val("ld"))])
                wg_ = [p.guards["wrow"]] if "wrow" in p.guards else []
                wlv = p.op("sync", "dma_start", out=wrow[:],
                           in_=bass.AP(wfmd.tensor, e * T + g * 512,
                                       [[512, 1], [1, 512]]),
                           wait=wg_, sem="ld", n=16)
                btv = p.op("tensor", "matmul", hb.bank("ep0"),
                           consts["ones"][0:1, 0:128], wrow[0:1, :],
                           start=True, stop=True,
                           wait=[("ld", wlv)] + hb.fwait("ep0"), sem="t")
                wbv = p.op("vector", "tensor_copy", webc[:], hb.bank("ep0"),
                           wait=[("t", btv)] + ([p.guards["webc"]]
                                                if "webc" in p.guards else []),
                           sem="v")
                hb.free["ep0"] = ("v", wbv)
                p.guards["wrow"] = ("t", btv)
            for hc in range(11):
                tvA = mm_chain(p, hb.bank("A"),
                               [w1e[:, kc * 1408 + hc * 128:
                                    kc * 1408 + (hc + 1) * 128]
                                for kc in range(8)],
                               [hb.xnT[:, kc * 512:(kc + 1) * 512]
                                for kc in range(8)],
                               [("gld", xlv)] + hb.fwait("A"))
                tvB = mm_chain(p, hb.bank("B"),
                               [w2e[:, kc * 1408 + hc * 128:
                                    kc * 1408 + (hc + 1) * 128]
                                for kc in range(8)],
                               [hb.xnT[:, kc * 512:(kc + 1) * 512]
                                for kc in range(8)],
                               hb.fwait("B"))
                sg = [p.guards["stmp"]] if "stmp" in p.guards else []
                av = p.op("scalar", "activation", stmp[:], hb.bank("A"),
                          AF.Silu, wait=[("t", tvA)] + sg, sem="a")
                hb.free["A"] = ("a", av)
                hv = p.op("vector", "tensor_tensor",
                          hsb[:, hc * 512:(hc + 1) * 512], stmp[:],
                          hb.bank("B"), op=ALU.mult,
                          wait=[("a", av), ("t", tvB)] +
                          ([p.guards["hsb"]] if ("hsb" in p.guards and
                                                 hc == 0) else []),
                          sem="v")
                hb.free["B"] = ("v", hv)
                p.guards["stmp"] = ("v", hv)
            h_v = p.val("v")
            xg_cons = p.val("t")
            for mc in range(8):
                tag = "A" if mc % 2 == 0 else "B"
                tv = mm_chain(p, hb.bank(tag),
                              [w3e[:, kc * 1024 + mc * 128:
                                   kc * 1024 + (mc + 1) * 128]
                               for kc in range(11)],
                              [hsb[:, kc * 512:(kc + 1) * 512]
                               for kc in range(11)],
                              [("v", h_v)] + hb.fwait(tag))
                tg = [p.guards["tmp_sb"]] if "tmp_sb" in p.guards else []
                if e < 8:
                    tv2 = p.op("vector", "scalar_tensor_tensor", tmp_sb[:],
                               hb.bank(tag), 1.0, webc[:],
                               op0=ALU.mult, op1=ALU.mult,
                               wait=[("t", tv), ("v", wbv)] + tg, sem="v")
                else:
                    tv2 = p.op("vector", "tensor_copy", tmp_sb[:],
                               hb.bank(tag), wait=[("t", tv)] + tg, sem="v")
                hb.free[tag] = ("v", tv2)
                yw = [ya_gate[g]] if g in ya_gate else []
                yav = p.op("gpsimd", "dma_start",
                           out=bass.AP(yaccd.tensor, (mc * 128) * T + g * 512,
                                       [[T, 128], [1, 512]]),
                           in_=tmp_sb[:],
                           accum_op=(ALU.bypass if e == 0 else ALU.add),
                           wait=[("v", tv2)] + yw, sem="ya", n=16)
                p.guards["tmp_sb"] = ("ya", yav)
                if e < 8:
                    p.guards["webc"] = ("v", tv2)
            ya_gate[g] = ("ya", p.val("ya"))
        wcons = p.val("t")

    ya_all = p.val("ya")

    hb.g_gate = []
    p.wait("sync", [("ld", p.val("ld"))])
    for g in range(NG):
        ldv = hb.group_load(g, xin)
        p.wait("sync", [("ld", p.val("ld"))])
        ylv = p.op("sync", "dma_start",
                   out=hb.yfm[:].rearrange("p (c m) -> p c m", c=8),
                   in_=bass.AP(yaccd.tensor, g * 512,
                               [[T, 128], [128 * T, 8], [1, 512]]),
                   wait=[("ya", ya_all)] + ([p.guards["yfm"]]
                                            if "yfm" in p.guards else []),
                   sem="ld", n=16)
        hb.epilogue(g, [("ld", ylv)], xout, eps5, x_wait=[("ld", ylv)])
        p.guards["yfm"] = ("t", p.val("t"))
    if fused is None:
        return finish(p)
    return None


def run_l3ffn(x, Wg, W1, W2, W3, Ws1, Ws2, Ws3):
    return _launch("l3ffn", build_l3ffn, x,
                   {"Wg": Wg, "W1": W1, "W2": W2, "W3": W3,
                    "Ws1": Ws1, "Ws2": Ws2, "Ws3": Ws3})



L1A_STOP = [None]


def build_l1attn(fused=None):
    if fused is None:
        p = P("l1attn")
        nc = p.nc
        xin = nc.dram_tensor("xin", [T, D], F32, kind="ExternalInput").ap()
        xout = nc.dram_tensor("xout", [T, D], F32, kind="ExternalOutput").ap()
        W = {k: nc.dram_tensor(k2, sh, F32, kind="ExternalInput").ap()
             for k, k2, sh in (
                 ("L1_attn_w_qkv", "w_qkv", [1024, 3072]),
                 ("L1_attn_b_qkv", "b_qkv", [3072]),
                 ("L1_attn_w_out", "w_out", [1024, 1024]),
                 ("L1_attn_b_out", "b_out", [1024]),
                 ("L1_attn_w1_rpe", "w1_rpe", [1, 32]),
                 ("L1_attn_b1_rpe", "b1_rpe", [32]),
                 ("L1_attn_w2_rpe", "w2_rpe", [32, 16]),
                 ("L1_attn_b2_rpe", "b2_rpe", [16]))}
        consts = build_consts(p, f32=True, bf=True, eps=(1e-5, 1e-6))
    else:
        p, consts, xin, xout, W = fused
        nc = p.nc
    wqkvd = W["L1_attn_w_qkv"]; bqkvd = W["L1_attn_b_qkv"]
    wod = W["L1_attn_w_out"]; bod = W["L1_attn_b_out"]
    w1rd = W["L1_attn_w1_rpe"]; b1rd = W["L1_attn_b1_rpe"]
    w2rd = W["L1_attn_w2_rpe"]; b2rd = W["L1_attn_b2_rpe"]

    qdram = p.dram("qfm", [1024, T], BF16).ap()
    kdram = p.dram("kfm", [1024, T], BF16).ap()
    vdram = p.dram("vtm", [T, 1024], BF16).ap()
    mdram = p.dram("mdr", [BPC, 16, 2048], F32).ap()
    qn2d = p.dram("qn2d", [2, 8, T], F32).ap()
    kn2d = p.dram("kn2d", [2, 8, T], F32).ap()
    bm16d = p.dram("bm16d", [16], F32).ap()
    ewided = p.dram("ewided", [16, 128 * 4351 + 8], BF16).ap()
    eps5 = consts["eps1em05"]

    ps_s1 = p.psum("ps_s1", [128, 2048], F32)
    banks = {"A": ps_s1[:, 0:512], "B": ps_s1[:, 512:1024],
             "ep0": ps_s1[:, 1024:1536], "ep1": ps_s1[:, 1536:2048]}
    hb = HBCommon(p, consts, h_cols=0, banks=banks,
                  ep_ap=ps_s1[:, 1024:2048])

    wstg = hb.xg
    wo, _ = load_w_fm(p, "wo_sb", wod, 1024, 1024)
    bo, bov = load_bias_cols(p, "bo", bod, 1024, consts, hb.ps_scr[:])
    kmax2 = p.sbuf("kmax2", [16, 1], F32)
    bmax_r = p.sbuf("bmax_r", [16, 1], F32)
    yfm = p.sbuf("yfm_l1", [128, 8 * 512], F32)
    PT = p.sbuf("PT", [128, 16 * 512], BF16)

    if p.stack is None:
        p.stack = contextlib.ExitStack()
    wstack = contextlib.ExitStack()
    saved_stack = p.stack
    p.stack = wstack

    wq, wqv = load_w_scaled(p, "wq_sb", wqkvd[:, 0:1024], 1024, 1024,
                            1.0 / 8.0, wstg)
    wk, _ = load_w_fm(p, "wk_sb", wqkvd[:, 1024:2048], 1024, 1024)
    wv_, _ = load_w_fm(p, "wv_sb", wqkvd[:, 2048:3072], 1024, 1024)
    wldv = p.val("gld")
    bq, bqv = load_bias_cols(p, "bq", bqkvd[0:1024], 1024, consts,
                             hb.ps_scr[:], wait_extra=[wqv, ("v", bov)])
    bsc = p.op("vector", "tensor_scalar_mul", bq[:], bq[:], 1.0 / 8.0,
               wait=[("v", bqv)], sem="v")
    bk, bkv = load_bias_cols(p, "bk", bqkvd[1024:2048], 1024, consts,
                             hb.ps_scr[:], wait_extra=[("v", bsc)])
    bv, bvv = load_bias_cols(p, "bv", bqkvd[2048:3072], 1024, consts,
                             hb.ps_scr[:], wait_extra=[("v", bkv)])

    h_rpe = bass.AP(yfm, 0, [[4096, 32], [1, 4095]])
    tab = bass.AP(hb.ytm, 0, [[4096, 16], [1, 4095]])
    etab = bass.AP(PT, 0, [[8192, 16], [1, 4095]])
    relrow_h = hb.ytm
    relrow = bass.AP(relrow_h, 0, [[4096, 1], [1, 4095]])
    sgnrow = bass.AP(yfm, 0, [[4096, 1], [1, 4095]])
    w1r = p.sbuf("w1r", [1, 32], F32)
    w2r = p.sbuf("w2r", [32, 16], F32)
    b1c = p.sbuf("b1c", [32, 1], F32)
    b2c = p.sbuf("b2c", [16, 1], F32)
    brow = p.sbuf("brow", [1, 32], F32)
    bmax = p.sbuf("bmax", [16, 1], F32)
    bmr = p.sbuf("bmr", [1, 16], F32)
    hsel, hselv = build_hsel(p, consts)
    ps_qn = p.psum("ps_qn", [128, 512], F32)
    q2tmp = p.sbuf("q2tmp", [128, 512], BF16)
    qnstg = p.sbuf("qnstg", [2, 512], F32)

    iv = p.op("gpsimd", "iota", relrow, [[1, 4095]], base=-2047,
              channel_multiplier=0, allow_small_or_imprecise_dtypes=True,
              sem="g")
    sv_ = p.op("scalar", "sign", sgnrow, relrow,
               wait=[("g", iv)], sem="a")
    av_ = p.op("scalar", "activation", relrow, relrow, AF.Abs,
               wait=[("a", sv_)], sem="a")
    lv_ = p.op("scalar", "activation", relrow, relrow, AF.Ln,
               bias=consts["ones"][0:1, 0:1], wait=[("a", av_)], sem="a")
    rv_ = p.op("vector", "tensor_mul", relrow, relrow, sgnrow,
               wait=[("a", lv_)], sem="v")
    p.wait("sync", [("ld", p.val("ld"))])
    l1 = p.op("sync", "dma_start", out=w1r[:],
              in_=w1rd, sem="ld", n=16)
    l1b = p.op("sync", "dma_start", out=brow[:],
               in_=b1rd.rearrange("(a n) -> a n", a=1), sem="ld", n=16)
    l2 = p.op("sync", "dma_start", out=w2r[:], in_=w2rd, sem="ld", n=16)
    l2b = p.op("sync", "dma_start", out=bmr[:, 0:16],
               in_=b2rd.rearrange("(a n) -> a n", a=1), sem="ld", n=16)
    t1 = p.op("tensor", "transpose", hb.ps_scr[0:32, 0:1], brow[:],
              consts["idf32"][0:1, 0:1],
              wait=[("ld", l2b), ("g", consts["g_done"]), ("v", bvv)],
              sem="t")
    e1 = p.op("vector", "tensor_copy", b1c[:], hb.ps_scr[0:32, 0:1],
              wait=[("t", t1)], sem="v")
    t2 = p.op("tensor", "transpose", hb.ps_scr[0:16, 1:2], bmr[:, 0:16],
              consts["idf32"][0:1, 0:1], wait=[("v", e1)], sem="t")
    e2 = p.op("vector", "tensor_copy", b2c[:], hb.ps_scr[0:16, 1:2],
              wait=[("t", t2)], sem="v")
    for i in range(8):
        w = 512 if i < 7 else 4095 - 7 * 512
        tv = p.op("tensor", "matmul", hb.ps_scr[0:32, 512:512 + w],
                  w1r[:], bass.AP(relrow_h, i * 512, [[4096, 1], [1, w]]),
                  start=True, stop=True,
                  wait=[("v", rv_), ("v", e2), ("ld", l2b)], sem="t")
        p.op("scalar", "activation",
             bass.AP(yfm, i * 512, [[4096, 32], [1, w]]),
             hb.ps_scr[0:32, 512:512 + w], AF.Relu, bias=b1c[:],
             wait=[("t", tv)], sem="a")
        hwv = p.val("a")
        p.wait("tensor", [("a", hwv)])
    for i in range(8):
        w = 512 if i < 7 else 4095 - 7 * 512
        tv = p.op("tensor", "matmul", hb.ps_scr[0:16, 512:512 + w],
                  w2r[:], bass.AP(yfm, i * 512, [[4096, 32], [1, w]]),
                  start=True, stop=True, sem="t")
        p.op("scalar", "activation",
             bass.AP(hb.ytm, i * 512, [[4096, 16], [1, w]]),
             hb.ps_scr[0:16, 512:512 + w], AF.Identity, bias=b2c[:],
             wait=[("t", tv)], sem="a")
        tbv = p.val("a")
        p.wait("tensor", [("a", tbv)])
    bmv = p.op("vector", "reduce_max", bmax[:], tab, axis=AX.X,
               wait=[("a", tbv)], sem="v")
    ev_ = p.op("scalar", "activation", etab, tab, AF.Exp,
               wait=[("a", tbv)], sem="a")
    t3 = p.op("tensor", "transpose", hb.ps_scr[0:1, 0:16], bmax[:],
              consts["idf32"][0:16, 0:16], wait=[("v", bmv)], sem="t")
    e3 = p.op("vector", "tensor_copy", bmr[:, 0:16], hb.ps_scr[0:1, 0:16],
              wait=[("t", t3)], sem="v")
    bs1 = p.op("sync", "dma_start",
               out=bm16d.rearrange("(a b) -> a b", a=1),
               in_=bmr[0:1, 0:16],
               wait=[("v", e3)], sem="se", n=16)
    p.wait("sync", [("se", p.val("se"))])
    for r_ in range(16):
        h_ = 2 * (r_ % 8) + r_ // 8
        e4 = p.op("sync", "dma_start", out=bmax_r[r_:r_ + 1, 0:1],
                  in_=bass.AP(bm16d.tensor, h_, [[1, 1], [1, 1]]),
                  sem="ld", n=16)
    etabd = p.dram("etabd", [16, 4095], BF16).ap()
    et_s = p.op("sync", "dma_start", out=etabd, in_=etab,
                wait=[("a", ev_)], sem="se", n=16)
    p.wait("sync", [("se", et_s)])
    ew_v = None
    for h in range(16):
        dst = bass.AP(ewided.tensor, h * (128 * 4351 + 8),
                      [[4351, 128], [1, 4095]])
        srcb = bass.AP(etabd.tensor, h * 4095, [[0, 128], [1, 4095]])
        ew_v = p.op("sync", "dma_start", out=dst, in_=srcb,
                    sem="se", n=16)

    pstg = p.sbuf("pstg", [128, 512], BF16)
    vtmp = p.sbuf("vtmp", [128, 512], BF16)
    vacc = PT[:, 0:4096]
    qnt = bass.AP(hb.ytm, 0, [[4096, 16], [1, 2048]])
    knt = bass.AP(hb.ytm, 2048, [[4096, 16], [1, 2048]])
    mnt = bass.AP(hb.ytm, 32 * 4096, [[4096, 16], [1, 2048]])
    p.stack = saved_stack

    hb.g_gate = [("v", bvv), ("v", bov), ("gld", wldv), ("se", ew_v),
                 ("ld", e4), ("g", hselv[1])]
    if L1A_STOP[0] == "prep":
        p.wait("sync", [("se", p.val("se")), ("v", p.val("v")),
                        ("gld", p.val("gld"))])
        p.verify()
        return p.finalize()

    for g in range(NG):
        ldv = hb.group_load(g, xin)
        xv = hb.sweep(g, ldv, eps5)
        for which, wsb, bsb_, sem_s, dr, nd in (
                ("q", wq, bq, "qs", qdram, qn2d),
                ("k", wk, bk, "ks", kdram, kn2d)):
            for mc in range(8):
                tag = "A" if mc % 2 == 0 else "B"
                ps = hb.bank(tag)
                tv = mm_chain(p, ps,
                              [wsb[:, kc * 1024 + mc * 128:
                                   kc * 1024 + (mc + 1) * 128]
                               for kc in range(8)],
                              [hb.xnT[:, kc * 512:(kc + 1) * 512]
                               for kc in range(8)],
                              [("v", xv)] + hb.fwait(tag))
                pw = [p.guards["pstg"]] if "pstg" in p.guards else []
                av = p.op("scalar", "activation", pstg[:], ps, AF.Identity,
                          bias=bsb_[:, mc:mc + 1],
                          wait=[("t", tv)] + pw, sem="a")
                hb.free[tag] = ("a", av)
                psv = p.op("sync", "dma_start",
                           out=bass.AP(dr.tensor, (mc * 128) * T + g * 512,
                                       [[T, 128], [1, 512]]),
                           in_=pstg[:], wait=[("a", av)], sem=sem_s, n=16)
                p.guards["pstg"] = (sem_s, psv)
                qk_norm_chunk(p, hsel, ps_qn, q2tmp, qnstg, pstg[:],
                              [("a", av)],
                              bass.AP(nd.tensor, mc * T + g * 512,
                                      [[8 * T, 2], [1, 512]]), which)
        vacc_ev = None
        for mc in range(8):
            tag = "A" if mc % 2 == 0 else "B"
            ps = hb.bank(tag)
            tv = mm_chain(p, ps,
                          [wv_[:, kc * 1024 + mc * 128:
                               kc * 1024 + (mc + 1) * 128]
                           for kc in range(8)],
                          [hb.xnT[:, kc * 512:(kc + 1) * 512]
                           for kc in range(8)],
                          [("v", xv)] + hb.fwait(tag))
            vw = [p.guards["vtmp"]] if "vtmp" in p.guards else []
            av = p.op("scalar", "activation", vtmp[:], ps, AF.Identity,
                      bias=bv[:, mc:mc + 1], wait=[("t", tv)] + vw, sem="a")
            hb.free[tag] = ("a", av)
            tw = [("a", av)] + hb.fwait("tbf")
            for jj in range(4):
                p.op("tensor", "transpose",
                     hb.ps_tbf[:, jj * 128:(jj + 1) * 128],
                     vtmp[:, jj * 128:(jj + 1) * 128],
                     consts["idbf"][:],
                     wait=tw if jj == 0 else (),
                     sem="t" if jj == 3 else None)
            ttv = p.val("t")
            p.guards["vtmp"] = ("t", ttv)
            dv = bass.AP(PT, mc * 128, [[8 * 1024, 128], [1024, 4], [1, 128]])
            pin = bass.AP(hb.ps_tbf, 0, [[1024, 128], [128, 4], [1, 128]])
            vw2 = [p.guards["vacc"]] if "vacc" in p.guards else []
            vacc_ev = p.op("vector", "tensor_copy", dv, pin,
                           wait=[("t", ttv)] + vw2, sem="v")
            hb.free["tbf"] = ("v", vacc_ev)
        for jj in range(4):
            vsv = p.op("sync", "dma_start",
                       out=vdram[g * 512 + jj * 128: g * 512 + (jj + 1) * 128, :],
                       in_=vacc[:, jj * 1024:(jj + 1) * 1024],
                       wait=[("v", vacc_ev)], sem="sv", n=16)
        p.guards["vacc"] = ("sv", vsv)
        hb.g_gate = [("v", xv)]

    p.barrier()
    wstack.close()

    khat2 = [p.sbuf("khat0", [65, 2048], BF16),
             p.sbuf("khat1", [65, 2048], BF16)]
    qhat2 = [p.sbuf("qhat0", [65, 2048], BF16),
             p.sbuf("qhat1", [65, 2048], BF16)]
    v12 = [p.sbuf("v1_0", [128, 16 * 65], BF16),
           p.sbuf("v1_1", [128, 16 * 65], BF16)]
    wide2 = [p.sbuf("wide0", [128, 3968], BF16),
             p.sbuf("wide1", [128, 3968], BF16)]
    lr2 = [p.sbuf("lr0", [65, 512], F32), p.sbuf("lr1", [65, 512], F32)]
    yst2 = [p.sbuf("yst0", [64, 512], BF16), p.sbuf("yst1", [64, 512], BF16)]
    invl4 = p.sbuf("invl4", [128, 4], F32)
    cbf = p.sbuf("cbf", [128, 4 * 4096], BF16)
    ps_av = p.psum("ps_av", [128, 512], F32)
    avbanks = [ps_av[0:65, 0:512], ps_s1[0:65, 1536:2048]]
    avkeys = ("avA", "ep1")
    hb.free["avA"] = None
    itc = [0]
    pend_tail = [None]

    p.op("gpsimd", "memset", khat2[0][64:65, :], 1.0, sem="g")
    p.op("gpsimd", "memset", khat2[1][64:65, :], 1.0, sem="g")
    p.op("gpsimd", "memset", v12[0][:], 1.0, sem="g")
    p.op("gpsimd", "memset", v12[1][:], 1.0, sem="g")
    gB_done = p.val("g")
    p.guards["pbt0"] = ("g", gB_done)
    p.guards["pbt1"] = ("g", gB_done)

    ldsem = ("gld", "gl2")

    def emit_loads(par, s_, h, m_done):
        sm = ldsem[par]
        kg = []
        for key in ("pbt%d" % par, "pbv%d" % par):
            if key in p.guards:
                kg.append(p.guards[key])
        p.op("gpsimd", "dma_start", out=khat2[par][0:64, :],
             in_=bass.AP(kdram.tensor, (h * 64) * T + s_ * 2048,
                         [[T, 64], [1, 2048]]),
             wait=kg, sem=sm, n=16)
        p.op("gpsimd", "dma_start", out=qhat2[par][0:64, :],
             in_=bass.AP(qdram.tensor, (h * 64) * T + s_ * 2048,
                         [[T, 64], [1, 2048]]),
             sem=sm, n=16)
        r_ = (h % 2) * 8 + h // 2
        p.op("gpsimd", "dma_start", out=qhat2[par][64:65, :],
             in_=bass.AP(mdram.tensor, (s_ * 16 + r_) * 2048,
                         [[2048, 1], [1, 2048]]),
             wait=[("sm", m_done)], sem=sm, n=16)
        p.op("gpsimd", "dma_start", out=wide2[par][:],
             in_=bass.AP(ewided.tensor, h * (128 * 4351 + 8) + 127,
                         [[4350, 128], [1, 3968]]),
             sem=sm, n=16)
        vlv = None
        for jc in range(16):
            vlv = p.op("gpsimd", "dma_start",
                       out=v12[par][:, jc * 65: jc * 65 + 64],
                       in_=vdram[s_ * 2048 + jc * 128:
                                 s_ * 2048 + (jc + 1) * 128,
                                 h * 64:(h + 1) * 64],
                       sem=sm, n=16)
        return (sm, vlv)

    for s in range(BPC):
        mhat_rows(p, hb, qn2d, kn2d, mdram, s, qnt, knt, kmax2, mnt,
                  extra=True, ex_scalar=bmax_r[:])
        m_done = p.val("sm")
        if L1A_STOP[0] == "pass1":
            continue

        pend = {0: emit_loads(0, s, 0, m_done)}
        for h in range(16):
            par = h % 2
            if h + 1 < 16:
                pend[1 - par] = emit_loads(1 - par, s, h + 1, m_done)
            khat, qhat, v1 = khat2[par], qhat2[par], v12[par]
            wide = wide2[par]
            smp, vlv = pend[par]
            first = [("g", gB_done)] if (s == 0 and h == 0) else []
            for ib in range(4):
                bk = itc[0] % 2
                avps = avbanks[bk]
                avkey = avkeys[bk]
                ptg = [p.guards["PT"]] if "PT" in p.guards else []
                ebvs = []
                for jc in range(16):
                    tag = "A" if jc % 2 == 0 else "B"
                    ps = hb.bank(tag)
                    tv = p.op("tensor", "matmul", ps,
                              khat[:, jc * 128:(jc + 1) * 128],
                              qhat[:, ib * 512:(ib + 1) * 512],
                              start=True, stop=True,
                              wait=[(smp, vlv)] + first + hb.fwait(tag) +
                              (ptg if jc == 0 else []),
                              sem="t")
                    av = p.op("scalar", "activation",
                              PT[:, jc * 512:(jc + 1) * 512], ps, AF.Exp,
                              wait=[("t", tv)], sem="a")
                    hb.free[tag] = ("a", av)
                    off = 1920 - jc * 128 + ib * 512
                    ebv = p.op("vector", "tensor_mul",
                               PT[:, jc * 512:(jc + 1) * 512],
                               PT[:, jc * 512:(jc + 1) * 512],
                               wide[:, off:off + 512],
                               wait=[("a", av)], sem="v")
                    ebvs.append(ebv)
                    if jc >= 3:
                        j2 = jc - 3
                        p.op("tensor", "matmul", avps,
                             v1[:, j2 * 65:(j2 + 1) * 65],
                             PT[:, j2 * 512:(j2 + 1) * 512],
                             start=(j2 == 0), stop=False,
                             wait=[("v", ebvs[j2])] +
                             (hb.fwait(avkey) if j2 == 0 else []))
                for j2 in (13, 14, 15):
                    p.op("tensor", "matmul", avps,
                         v1[:, j2 * 65:(j2 + 1) * 65],
                         PT[:, j2 * 512:(j2 + 1) * 512],
                         start=False, stop=(j2 == 15),
                         wait=[("v", ebvs[j2])],
                         sem="t" if j2 == 15 else None)
                tv = p.val("t")
                p.guards["PT"] = ("t", tv)
                tp_ = itc[0] % 2
                lw = ([p.guards["lrp%d" % tp_]]
                      if ("lrp%d" % tp_) in p.guards else [])
                lev = p.op("vector", "tensor_copy", lr2[tp_][64:65, :],
                           avps[64:65, :], wait=[("t", tv)] + lw, sem="v")
                yw = ([p.guards["ystp%d" % tp_]]
                      if ("ystp%d" % tp_) in p.guards else [])
                ysv = p.op("vector", "tensor_copy", yst2[tp_][:],
                           avps[0:64, :], wait=[("t", tv)] + yw, sem="v")
                hb.free[avkey] = ("v", ysv)
                if pend_tail[0] is not None:
                    pend_tail[0]()

                def mk_tail(tp_=tp_, lev=lev, ysv=ysv, ib=ib, h=h):
                    def tail():
                        sg = [p.guards["scr"]] if "scr" in p.guards else []
                        for j in range(4):
                            p.op("tensor", "transpose",
                                 hb.ps_scr[:, 16 + j:16 + j + 1],
                                 lr2[tp_][64:65, j * 128:(j + 1) * 128],
                                 consts["idf32"][64:65, 64:65],
                                 wait=[("v", lev)] + sg if j == 0 else (),
                                 sem="t" if j == 3 else None)
                        ltv = p.val("t")
                        p.guards["lrp%d" % tp_] = ("t", ltv)
                        ig = ([p.guards["invg"]]
                              if "invg" in p.guards else [])
                        inv_v = p.op("vector", "reciprocal", invl4[:],
                                     bass.AP(hb.ps_scr, 16,
                                             [[1024, 128], [1, 4]]),
                                     wait=[("t", ltv)] + ig, sem="v")
                        p.guards["scr"] = ("v", inv_v)
                        tw = [("v", ysv)] + hb.fwait("tbf")
                        for j in range(4):
                            p.op("tensor", "transpose",
                                 hb.ps_tbf[:, j * 64:(j + 1) * 64],
                                 yst2[tp_][:, j * 128:(j + 1) * 128],
                                 consts["idbf"][0:64, 0:64],
                                 wait=tw if j == 0 else (),
                                 sem="t" if j == 3 else None)
                        ytv = p.val("t")
                        p.guards["ystp%d" % tp_] = ("t", ytv)
                        for j in range(4):
                            p.op("vector", "tensor_scalar",
                                 cbf[:, ib * 4096 + j * 1024 + h * 64:
                                     ib * 4096 + j * 1024 + (h + 1) * 64],
                                 hb.ps_tbf[:, j * 64:(j + 1) * 64],
                                 invl4[:, j:j + 1], None, op0=ALU.mult,
                                 wait=([("t", ytv), ("v", inv_v)]
                                       if j == 0 else ()),
                                 sem="v" if j == 3 else None)
                        hb.free["tbf"] = ("v", p.val("v"))
                        p.guards["invg"] = ("v", p.val("v"))
                    return tail
                pend_tail[0] = mk_tail()
                itc[0] += 1
            p.guards["pbt%d" % par] = ("t", p.val("t"))
            p.guards["pbv%d" % par] = ("v", p.val("v"))
        if pend_tail[0] is not None:
            pend_tail[0]()
            pend_tail[0] = None
        heads_v = p.val("v")

        for ib in range(4):
            g = s * 4 + ib
            ldv = hb.group_load(g, xin)
            src_tiles = [cbf[:, ib * 4096 + j * 1024:
                             ib * 4096 + (j + 1) * 1024] for j in range(4)]

            def dview(j):
                return bass.AP(hb.xnT, j * 128,
                               [[8 * 512, 128], [512, 8], [1, 128]])
            cxv = transpose_to_fm(p, hb.ps_tbf, consts["idbf"], src_tiles,
                                  dview, 4,
                                  wait_src=[[("v", heads_v)] +
                                            hb.fwait("tbf")
                                            for j in range(4)])
            hb.free["tbf"] = ("v", cxv)
            for mc in range(8):
                tag = "A" if mc % 2 == 0 else "B"
                ps = hb.bank(tag)
                tv = mm_chain(p, ps,
                              [wo[:, kc * 1024 + mc * 128:
                                   kc * 1024 + (mc + 1) * 128]
                               for kc in range(8)],
                              [hb.xnT[:, kc * 512:(kc + 1) * 512]
                               for kc in range(8)],
                              [("v", cxv)] + hb.fwait(tag))
                av = p.op("scalar", "activation",
                          yfm[:, mc * 512:(mc + 1) * 512], ps, AF.Identity,
                          bias=bo[:, mc:mc + 1], wait=[("t", tv)], sem="a")
                hb.free[tag] = ("a", av)
            ya = p.val("a")
            hb.yfm = yfm
            stv = hb.epilogue(g, [("a", ya)], xout, eps5,
                              x_wait=[("ld", ldv)])
            p.guards["xng"] = ("st", stv[-1])
    if L1A_STOP[0] == "pass1":
        p.wait("sync", [("sm", p.val("sm"))])
        p.verify()
        return p.finalize()
    if fused is None:
        return finish(p)
    return None


def run_l1attn(x, w_qkv, b_qkv, w_out, b_out, w1_rpe, b1_rpe, w2_rpe, b2_rpe):
    return _launch("l1attn", build_l1attn, x,
                   {"w_qkv": w_qkv, "b_qkv": b_qkv, "w_out": w_out,
                    "b_out": b_out, "w1_rpe": w1_rpe, "b1_rpe": b1_rpe,
                    "w2_rpe": w2_rpe, "b2_rpe": b2_rpe})



WSPECS = [
    ("L0_attn_ln_scale", [1024]), ("L0_attn_ln_bias", [1024]),
    ("L0_attn_wq", [1024, 1024]), ("L0_attn_wk", [1024, 1024]),
    ("L0_attn_wv", [1024, 1024]),
    ("L0_ffn_weight", [1024, 1024]), ("L0_ffn_bias", [1024]),
    ("L1_attn_w_qkv", [1024, 3072]), ("L1_attn_b_qkv", [3072]),
    ("L1_attn_w_out", [1024, 1024]), ("L1_attn_b_out", [1024]),
    ("L1_attn_w1_rpe", [1, 32]), ("L1_attn_b1_rpe", [32]),
    ("L1_attn_w2_rpe", [32, 16]), ("L1_attn_b2_rpe", [16]),
    ("L1_ffn_w1", [4096, 1024]), ("L1_ffn_b1", [4096]),
    ("L1_ffn_w2", [1024, 4096]), ("L1_ffn_b2", [1024]),
    ("L2_attn_ln_scale", [1024]), ("L2_attn_ln_bias", [1024]),
    ("L2_attn_W_hidden", [1024, 4096]), ("L2_attn_b_hidden", [4096]),
    ("L2_attn_W_qk", [1024, 128]), ("L2_attn_b_qk", [128]),
    ("L2_attn_os_weight", [2, 128]), ("L2_attn_os_bias", [2, 128]),
    ("L2_attn_W_out", [2048, 1024]), ("L2_attn_b_out", [1024]),
    ("L2_ffn_U1", [1024, 256]), ("L2_ffn_V1", [256, 4096]),
    ("L2_ffn_b1", [4096]), ("L2_ffn_U2", [4096, 256]),
    ("L2_ffn_V2", [256, 1024]), ("L2_ffn_b2", [1024]),
    ("L3_attn_wq", [1024, 1024]), ("L3_attn_wk", [256, 1024]),
    ("L3_attn_wv", [256, 1024]), ("L3_attn_wo", [1024, 1024]),
    ("L3_ffn_Wg", [1024, 8]), ("L3_ffn_W1", [8, 1024, 1408]),
    ("L3_ffn_W2", [8, 1024, 1408]), ("L3_ffn_W3", [8, 1408, 1024]),
    ("L3_ffn_Ws1", [1024, 2816]), ("L3_ffn_Ws2", [1024, 2816]),
    ("L3_ffn_Ws3", [2816, 1024]),
]

FUSED_BLOCKS = [build_l0attn, build_l0ffn, build_l1attn, build_l1ffn,
                build_l2attn, build_l2ffn, build_l3attn, build_l3ffn]


def build_fused():
    import contextlib
    p = P("fused")
    nc = p.nc
    xin = nc.dram_tensor("xin", [T, D], F32, kind="ExternalInput").ap()
    xout = nc.dram_tensor("xout", [T, D], F32, kind="ExternalOutput").ap()
    W = {name: nc.dram_tensor(name, shape, F32, kind="ExternalInput").ap()
         for name, shape in WSPECS}
    xb = [p.dram("xbuf0", [T, D], F32).ap(),
          p.dram("xbuf1", [T, D], F32).ap()]
    consts = build_consts(p, f32=True, bf=True, eps=(1e-5, 1e-6))
    cur = xin
    for i, bfn in enumerate(FUSED_BLOCKS):
        dst = xout if i == len(FUSED_BLOCKS) - 1 else xb[i % 2]
        p.barrier()
        p.prefix = "b%d_" % i
        with contextlib.ExitStack() as st:
            p.stack = st
            bfn(fused=(p, consts, cur, dst, W))
            p.stack = None
        cur = dst
    p.prefix = ""
    return finish(p)


def kernel_fused(**inputs):
    nc = _get_prog("fused", build_fused)
    x = np.ascontiguousarray(inputs["L0_attn_x"], dtype=np.float32)
    xs = x.reshape(NCORES, T, D)
    in_maps = []
    for c in range(NCORES):
        m = {"xin": xs[c]}
        for name, _ in WSPECS:
            m[name] = inputs[name]
        in_maps.append(m)
    res = run_bass_kernel_spmd(nc, in_maps, core_ids=list(range(NCORES)))
    out = np.stack([r["xout"] for r in res.results])
    return np.ascontiguousarray(out.reshape(B, S, D), dtype=np.float32)


def kernel(**inputs):
    return kernel_fused(**inputs)
